# version 22
# baseline (speedup 1.0000x reference)
import math
import os
from contextlib import ExitStack

import numpy as np

import concourse.bass as bass
import concourse.mybir as mybir
from concourse.bass_utils import run_bass_kernel_spmd

F32 = mybir.dt.float32
BF16 = mybir.dt.bfloat16
ALU = mybir.AluOpType
AF = mybir.ActivationFunctionType
AXX = mybir.AxisListType.X

D = 1024
EPS = 1e-5
KT_FULL = [4, 8, 12, 16, 20, 24, 28, 32]



def _tile_cols(vec, n_tiles):
    return np.ascontiguousarray(vec.reshape(n_tiles, 128).T)


def host_inputs(inp, core):
    role = core % 2
    samp = core // 2
    f32 = np.float32
    d = {}
    x = np.asarray(inp["L0_attn_x"], dtype=f32)
    d["x_abs"] = np.ascontiguousarray(x[samp])

    d["l0_wqkv"] = np.ascontiguousarray(np.asarray(inp["L0_attn_W_attn"], f32))
    b_attn = np.asarray(inp["L0_attn_b_attn"], f32)
    d["l0_bqkv_cols"] = _tile_cols(b_attn, 24)
    d["l0_bv_row"] = np.ascontiguousarray(b_attn[2048:3072].reshape(1, 1024))
    d["l0_wproj"] = np.ascontiguousarray(np.asarray(inp["L0_attn_W_proj"], f32))
    d["l0_bproj_row"] = np.ascontiguousarray(np.asarray(inp["L0_attn_b_proj"], f32).reshape(1, 1024))

    tri = np.zeros((4, 128, 512), f32)
    for j in range(4):
        tri[j] = ((j * 128 + np.arange(128)[:, None]) <= np.arange(512)[None, :])
    d["l0_tri"] = np.ascontiguousarray(tri.transpose(1, 0, 2).reshape(128, 2048))

    d["l0_router_w"] = np.ascontiguousarray(np.asarray(inp["L0_ffn_router_weight"], f32))
    gup = np.asarray(inp["L0_ffn_gate_up_proj"], f32)
    perm = np.concatenate([np.arange(0, 2048, 2), np.arange(1, 2048, 2)])
    d["l0_w13"] = np.ascontiguousarray(gup[:, :, perm])
    gub = np.asarray(inp["L0_ffn_gate_up_proj_bias"], f32)[:, perm]
    d["l0_b13_cols"] = np.ascontiguousarray(gub.reshape(8, 16, 128).transpose(2, 0, 1).reshape(128, 128))
    d["l0_w2"] = np.ascontiguousarray(np.asarray(inp["L0_ffn_down_proj"], f32))
    d["l0_b2_rows"] = np.ascontiguousarray(np.asarray(inp["L0_ffn_down_proj_bias"], f32).reshape(1, 8192))

    d["l1_wqkv"] = np.ascontiguousarray(np.asarray(inp["L1_attn_qkv_w"], f32))
    b1 = np.asarray(inp["L1_attn_qkv_b"], f32)
    d["l1_bqkv_cols"] = _tile_cols(b1, 24)
    d["l1_bv_row"] = np.ascontiguousarray(b1[2048:3072].reshape(1, 1024))
    d["l1_rq_w"] = np.ascontiguousarray(np.asarray(inp["L1_attn_rq_w"], f32))
    d["l1_rq_b_cols"] = _tile_cols(np.asarray(inp["L1_attn_rq_b"], f32), 2)
    d["l1_rkv_w"] = np.ascontiguousarray(np.asarray(inp["L1_attn_rkv_w"], f32))
    rkvb = np.asarray(inp["L1_attn_rkv_b"], f32)
    d["l1_rk_b_cols"] = _tile_cols(rkvb[:256], 2)
    d["l1_rv_b_row"] = np.ascontiguousarray(rkvb[256:].reshape(1, 256))
    d["l1_rout_w"] = np.ascontiguousarray(np.asarray(inp["L1_attn_rout_w"], f32))
    d["l1_rout_b_cols"] = _tile_cols(np.asarray(inp["L1_attn_rout_b"], f32), 8)
    d["l1_wout"] = np.ascontiguousarray(np.asarray(inp["L1_attn_out_w"], f32))
    d["l1_bout_row"] = np.ascontiguousarray(np.asarray(inp["L1_attn_out_b"], f32).reshape(1, 1024))
    for nm, key in (("l1_gl_blk", "L1_attn_gl_w"), ("l1_gr_blk", "L1_attn_gr_w")):
        blk = np.zeros((128, 2), f32)
        wv = np.asarray(inp[key], f32)[:, 0]
        blk[0:64, 0] = wv
        blk[64:128, 1] = wv
        d[nm] = blk

    kk = np.arange(128)[:, None]
    qq = np.arange(128)[None, :]
    mprev = (qq <= kk).astype(f32)
    mdiag = (qq > kk).astype(f32)
    d["l1_mloc"] = np.ascontiguousarray(np.concatenate([mprev, mdiag], axis=1))
    m0 = np.zeros((4, 128, 256), f32)
    for s_ in range(4):
        m0[s_, :, 0:128] = 0.0 if (role == 0 and s_ == 0) else mprev
        m0[s_, :, 128:256] = mdiag
    d["l1_mloc0"] = np.ascontiguousarray(m0.transpose(1, 0, 2).reshape(128, 1024))
    m4 = np.zeros((4, 128, 1024), f32)
    for s_ in range(4):
        for qt_ in range(4):
            m4[s_, :, qt_ * 256:(qt_ + 1) * 256] = m0[s_] if qt_ == 0 else d["l1_mloc"]
    d["l1_mloc4"] = np.ascontiguousarray(m4.transpose(1, 0, 2).reshape(128, 4096))

    col_abs = np.empty(64, np.int64)
    col_abs[0:32] = role * 32 + np.arange(32)
    col_abs[32:64] = (1 - role) * 32 + np.arange(32)
    mlat = np.zeros((4, 64, 512), f32)
    for s_ in range(4):
        qchunk = role * 32 + (s_ * 512 + np.arange(512)) // 64
        mlat[s_] = (col_abs[:, None] < qchunk[None, :]).astype(f32)
    d["l1_mlat"] = np.ascontiguousarray(mlat.transpose(1, 0, 2).reshape(64, 2048))

    sel = np.zeros((128, 2), f32)
    sel[:, role] = 1.0
    d["half_sel"] = sel

    d["l1_gate_wT"] = np.ascontiguousarray(np.asarray(inp["L1_ffn_gate_w"], f32).T)
    d["l1_gate_b_row"] = np.ascontiguousarray(np.asarray(inp["L1_ffn_gate_b"], f32).reshape(1, 16))
    d["l1_w13"] = np.ascontiguousarray(np.asarray(inp["L1_ffn_w13"], f32).transpose(0, 2, 1))
    d["l1_w2"] = np.ascontiguousarray(np.asarray(inp["L1_ffn_w2"], f32).transpose(0, 2, 1))
    d["l1_w13s"] = np.ascontiguousarray(np.asarray(inp["L1_ffn_w13_s"], f32).T)
    d["l1_w2s"] = np.ascontiguousarray(np.asarray(inp["L1_ffn_w2_s"], f32).T)

    d["identity"] = np.eye(128, dtype=f32)
    d["row_1702"] = np.full((1, 128), 1.702, f32)
    return d


def reassemble(outs):
    full = np.empty((4, 4096, 1024), np.float32)
    for core in range(8):
        role, samp = core % 2, core // 2
        full[samp, role * 2048:(role + 1) * 2048] = outs[core]
    return full


ALL_INPUTS = [
    ("x_abs", (4096, 1024)),
    ("l0_wqkv", (1024, 3072)), ("l0_bqkv_cols", (128, 24)), ("l0_bv_row", (1, 1024)),
    ("l0_wproj", (1024, 1024)), ("l0_bproj_row", (1, 1024)),
    ("l0_tri", (128, 2048)),
    ("l0_router_w", (1024, 8)),
    ("l0_w13", (8, 1024, 2048)), ("l0_b13_cols", (128, 128)),
    ("l0_w2", (8, 1024, 1024)), ("l0_b2_rows", (1, 8192)),
    ("l1_wqkv", (1024, 3072)), ("l1_bqkv_cols", (128, 24)), ("l1_bv_row", (1, 1024)),
    ("l1_rq_w", (1024, 256)), ("l1_rq_b_cols", (128, 2)),
    ("l1_rkv_w", (1024, 512)), ("l1_rk_b_cols", (128, 2)), ("l1_rv_b_row", (1, 256)),
    ("l1_rout_w", (256, 1024)), ("l1_rout_b_cols", (128, 8)),
    ("l1_wout", (1024, 1024)), ("l1_bout_row", (1, 1024)),
    ("l1_gl_blk", (128, 2)), ("l1_gr_blk", (128, 2)),
    ("l1_mloc", (128, 256)), ("l1_mloc0", (128, 1024)), ("l1_mlat", (64, 2048)),
    ("l1_mloc4", (128, 4096)),
    ("half_sel", (128, 2)),
    ("l1_gate_wT", (1024, 16)), ("l1_gate_b_row", (1, 16)),
    ("l1_w13", (16, 1024, 1024)), ("l1_w2", (16, 512, 1024)),
    ("l1_w13s", (1024, 1024)), ("l1_w2s", (512, 1024)),
    ("identity", (128, 128)), ("row_1702", (1, 128)),
]



class Bld:
    def __init__(self, nc):
        self.nc = nc
        self.t = nc.tensor
        self.v = nc.vector
        self.s = nc.scalar
        self.g = nc.gpsimd
        self.y = nc.sync
        self._sems = {}
        self._waited = {}

    def sem(self, name):
        if name not in self._sems:
            self._sems[name] = [self.nc.alloc_semaphore(name), 0]
        return self._sems[name]

    def inc(self, name, instr, k=1):
        sm = self.sem(name)
        instr.then_inc(sm[0], k)
        sm[1] += k
        return sm[1]

    def n(self, name):
        return self.sem(name)[1]

    def wait(self, eng, name, val=None):
        sm = self.sem(name)
        if val is None:
            val = sm[1]
        if val <= 0:
            return
        key = (id(eng), name)
        if self._waited.get(key, -1) >= val:
            return
        if val > sm[1]:
            raise RuntimeError(f"wait {name}>={val} can never fire (max {sm[1]})")
        eng.wait_ge(sm[0], val)
        self._waited[key] = val

    def vstep(self, instr):
        self.inc("vv", instr)
        return instr

    def vdep(self, val=None):
        self.wait(self.v, "vv", val)

    def sstep(self, instr):
        self.inc("ss", instr)
        return instr

    def sdep(self, val=None):
        self.wait(self.s, "ss", val)

    def barrier(self):
        engines = [self.t, self.v, self.s, self.g, self.y]
        for name, (h, cnt) in list(self._sems.items()):
            if cnt == 0:
                continue
            for e in engines:
                self.wait(e, name, cnt)


def dma3(b, eng, dst2d, src2d, inner, sem, p=128):
    in3 = src2d.rearrange("(c p) f -> p c f", p=p)
    out3 = dst2d.rearrange("p (c f) -> p c f", f=inner)
    return b.inc(sem, eng.dma_start(out=out3, in_=in3), 16)


class Prog:
    def __init__(self, nc, b, dr):
        self.nc = nc
        self.b = b
        self.dr = dr
        alloc = nc.alloc_sbuf_tensor
        self.ident = alloc("ident", [128, 128], F32).ap()
        self.ones_row = alloc("ones_row", [1, 128], BF16).ap()
        self.ones_row_f = alloc("ones_row_f", [65, 128], F32).ap()
        self.row1702 = alloc("row1702", [1, 128], BF16).ap()
        self.wsc = alloc("wsc", [128, 32 * 8], F32).ap()
        self.comb = alloc("comb", [128, 16 * 16], F32).ap()
        self.stat = alloc("stat", [128, 16], F32).ap()
        self.hsel = alloc("hsel", [128, 2], F32).ap()
        g, v = b.g, b.v
        b.inc("dmisc", g.dma_start(out=self.ident, in_=dr["identity"]), 16)
        b.inc("dmisc", g.dma_start(out=self.row1702, in_=dr["row_1702"]), 16)
        b.inc("dmisc", g.dma_start(out=self.hsel, in_=dr["half_sel"]), 16)
        v.memset(self.ones_row, 1.0)
        b.inc("cmisc", v.memset(self.ones_row_f, 1.0))

    def emit_ln_stats(self, xtile, scr, dep_sems):
        b, v, s = self.b, self.b.v, self.b.s
        st = self.stat
        for eng in (v, s):
            for nm, val in dep_sems:
                b.wait(eng, nm, val)
        b.wait(v, "ln_out", b.n("ln_out"))
        scr_rel = getattr(self, "_scr_rel", None)
        if scr_rel is None:
            scr_rel = self._scr_rel = [0]
        for tt in range(4):
            xt = xtile[:, tt * 1024:(tt + 1) * 1024]
            b.vstep(v.reduce_sum(st[:, tt:tt + 1], xt, AXX))
            b.wait(s, "vv", scr_rel[0])
            b.inc("ln_sq", s.activation(scr, xt, AF.Square))
            b.wait(v, "ln_sq")
            b.vstep(v.reduce_sum(st[:, 4 + tt:5 + tt], scr, AXX))
            scr_rel[0] = b.n("vv")
        b.vdep()
        b.vstep(v.tensor_scalar(st[:, 0:4], st[:, 0:4], 1.0 / D, None, ALU.mult))
        b.vstep(v.tensor_scalar(st[:, 4:8], st[:, 4:8], 1.0 / D, None, ALU.mult))
        b.vdep()
        b.vstep(v.tensor_tensor(st[:, 8:12], st[:, 0:4], st[:, 0:4], ALU.mult))
        b.vdep()
        b.vstep(v.tensor_sub(st[:, 4:8], st[:, 4:8], st[:, 8:12]))
        b.vdep()
        b.vstep(v.tensor_scalar(st[:, 4:8], st[:, 4:8], EPS, None, ALU.add))
        b.vdep()
        b.vstep(v.reciprocal(st[:, 8:12], st[:, 4:8]))
        b.wait(s, "vv", b.n("vv"))
        b.sstep(s.activation(st[:, 12:16], st[:, 8:12], AF.Sqrt))
        b.wait(v, "ss", b.n("ss"))
        b.vdep()
        b.vstep(v.tensor_tensor(st[:, 8:12], st[:, 0:4], st[:, 12:16], ALU.mult))
        b.vdep()
        b.vstep(v.tensor_scalar(st[:, 8:12], st[:, 8:12], -1.0, None, ALU.mult))
        b.wait(s, "vv", b.n("vv"))

    def emit_ln_tile(self, xtile, tt, xn1tile):
        b, s = self.b, self.b.s
        st = self.stat
        b.sdep()
        return b.inc("ln_out", s.activation(
            xn1tile, xtile[:, tt * 1024:(tt + 1) * 1024], AF.Identity,
            bias=st[:, 8 + tt:9 + tt], scale=st[:, 12 + tt:13 + tt]))



def emit_p1(P, K_sb):
    nc, b, dr = P.nc, P.b, P.dr
    t, v, s, g, y = b.t, b.v, b.s, b.g, b.y
    es = ExitStack()
    xstage = [es.enter_context(nc.sbuf_tensor(f"xstage{i}", [128, 4096], F32)).ap() for i in range(2)]
    xnf = [es.enter_context(nc.sbuf_tensor(f"xnf{i}", [128, 1024], F32)).ap() for i in range(2)]
    scr = es.enter_context(nc.sbuf_tensor("scr", [128, 1024], F32)).ap()
    xnTt = es.enter_context(nc.sbuf_tensor("xnTt", [128, 8 * 512], BF16)).ap()
    xnTf = es.enter_context(nc.sbuf_tensor("xnTf", [128, 1024], F32)).ap()
    wqkv = es.enter_context(nc.sbuf_tensor("wqkv", [128, 8 * 3072], BF16)).ap()
    bq_cols = es.enter_context(nc.sbuf_tensor("bq_cols", [128, 24], F32)).ap()
    bv_row = es.enter_context(nc.sbuf_tensor("bv_row", [1, 1024], BF16)).ap()
    rw_sb = es.enter_context(nc.sbuf_tensor("rw_sb", [128, 64], F32)).ap()
    top_sc = es.enter_context(nc.sbuf_tensor("top_sc", [128, 32], F32)).ap()
    vstage = [es.enter_context(nc.sbuf_tensor(f"vstage{i}", [128, 4 * 1040], BF16)).ap() for i in range(2)]
    qstage = [es.enter_context(nc.sbuf_tensor(f"qstage{i}", [128, 512], BF16)).ap() for i in range(2)]
    psT = [es.enter_context(nc.psum_tensor(f"psT{i}", [128, 128], F32)).ap() for i in range(2)]
    psK = [es.enter_context(nc.psum_tensor(f"psK{i}", [128, 512], F32)).ap() for i in range(2)]
    psV = [es.enter_context(nc.psum_tensor(f"psV{i}", [128, 512], F32)).ap() for i in range(2)]
    psR = es.enter_context(nc.psum_tensor("psR", [128, 8], F32)).ap()

    dma3(b, g, wqkv, dr["l0_wqkv"], 3072, "dw")
    b.inc("dw", g.dma_start(out=bq_cols, in_=dr["l0_bqkv_cols"]), 16)
    b.inc("dw", g.dma_start(out=bv_row, in_=dr["l0_bv_row"]), 16)
    dma3(b, g, rw_sb, dr["l0_router_w"], 8, "dw")
    for i in range(2):
        v.memset(vstage[i], 1.0)
    b.wait(t, "dw")
    b.wait(t, "dmisc")
    b.wait(t, "cmisc")
    b.wait(s, "dw")

    ln_done = 0
    ln_rel = {0: 0, 1: 0}
    dx_cnt = {0: 0, 1: 0}
    xnf_rel = {0: 0, 1: 0}
    psT_rel = [0, 0]
    psK_rel = [("kev", 0), ("kev", 0)]
    psV_rel = [0, 0]
    vst_rel = {0: 0, 1: 0}
    qst_rel = {0: 0, 1: 0}
    psR_rel = None

    def issue_x(j):
        par = j % 2
        sm = "dx" if par == 0 else "dxa"
        b.wait(y, "ln_out", ln_rel[par])
        dma3(b, y, xstage[par], dr["x_abs"][j * 512:(j + 1) * 512, :], 1024, sm)
        dx_cnt[par] = (sm, b.n(sm))

    issue_x(0)
    for j in range(8):
        par = j % 2
        if j + 1 < 8:
            issue_x(j + 1)
        b.wait(v, "dxn", b.n("dxn"))
        P.emit_ln_stats(xstage[par], scr, [dx_cnt[par]])
        for tt in range(4):
            T = j * 4 + tt
            fp_ = tt % 2
            b.wait(s, "pe", xnf_rel[fp_])
            ln_done = P.emit_ln_tile(xstage[par], tt, xnf[fp_])
            if tt == 3:
                ln_rel[par] = ln_done
            for c in range(8):
                tp = c % 2
                b.wait(t, "xe", psT_rel[tp])
                b.wait(t, "ln_out", ln_done)
                pe_cnt = b.inc("pe", t.transpose(psT[tp], xnf[fp_][:, c * 128:(c + 1) * 128], P.ident))
                b.wait(v, "pe", pe_cnt)
                b.inc("xe", v.tensor_copy(xnTt[:, c * 512 + tt * 128: c * 512 + (tt + 1) * 128], psT[tp]))
                psT_rel[tp] = b.inc("xe", v.tensor_copy(xnTf[:, c * 128:(c + 1) * 128], psT[tp]))
            xnf_rel[fp_] = b.n("pe")
            b.wait(t, "xe", b.n("xe"))
            if psR_rel is not None:
                b.wait(t, "vv", psR_rel[0])
                b.wait(t, "ss", psR_rel[1])
            for c in range(8):
                mm = t.matmul(psR, xnTf[:, c * 128:(c + 1) * 128], rw_sb[:, c * 8:(c + 1) * 8],
                              start=(c == 0), stop=(c == 7))
            rmm_cnt = b.inc("rmm", mm)
            st = top_sc
            b.wait(v, "rmm", rmm_cnt)
            b.vdep()
            b.vstep(v.reduce_max(st[:, 0:1], psR, AXX))
            b.vdep()
            b.vstep(v.tensor_scalar(st[:, 8:16], psR, st[:, 0:1], None, ALU.is_equal))
            b.vdep()
            b.vstep(v.scalar_tensor_tensor(st[:, 16:24], st[:, 8:16], -1e30, psR, ALU.mult, ALU.add))
            b.vdep()
            b.vstep(v.reduce_max(st[:, 1:2], st[:, 16:24], AXX))
            b.vstep(v.tensor_scalar(st[:, 2:3], st[:, 0:1], -1.0, None, ALU.mult))
            b.wait(s, "vv", b.n("vv"))
            b.sstep(s.activation(st[:, 8:16], psR, AF.Exp, bias=st[:, 2:3]))
            rex_ss = b.n("ss")
            b.wait(v, "ss", rex_ss)
            b.vdep()
            b.vstep(v.tensor_scalar(st[:, 16:24], psR, st[:, 1:2], None, ALU.is_ge))
            rtop_vv = b.n("vv")
            b.vdep()
            b.vstep(v.tensor_tensor(st[:, 8:16], st[:, 8:16], st[:, 16:24], ALU.mult))
            b.vdep()
            b.vstep(v.reduce_sum(st[:, 3:4], st[:, 8:16], AXX))
            b.vdep()
            b.vstep(v.reciprocal(st[:, 4:5], st[:, 3:4]))
            b.vdep()
            b.vstep(v.tensor_scalar(P.wsc[:, T * 8:(T + 1) * 8], st[:, 8:16],
                                    st[:, 4:5], 1.0 / 1.702, ALU.mult, ALU.mult))
            psR_rel = (rtop_vv, rex_ss)
        b.wait(y, "xe", b.n("xe"))
        b.inc("dxn", y.dma_start(
            out=P.xnt_dram[j],
            in_=xnTt.rearrange("p (c f) -> p c f", f=512)), 16)
        for hp in range(8):
            kp = hp % 2
            b.wait(t, psK_rel[kp][0], psK_rel[kp][1])
            for c in range(8):
                mm = t.matmul(psK[kp], wqkv[:, c * 3072 + 1024 + hp * 128: c * 3072 + 1024 + (hp + 1) * 128],
                              xnTt[:, c * 512:(c + 1) * 512], start=(c == 0), stop=(c == 7))
            pe_cnt = b.inc("pe", mm)
            b.wait(v, "pe", pe_cnt)
            psK_rel[kp] = ("kev", b.inc("kev", v.tensor_scalar(
                K_sb[:, hp * 4096 + j * 512: hp * 4096 + (j + 1) * 512],
                psK[kp], bq_cols[:, 8 + hp:9 + hp], None, ALU.add)))
        b.wait(v, f"dv{par}", vst_rel[par])
        for kk in range(4):
            for vh in range(2):
                vp = (kk * 2 + vh) % 2
                b.wait(t, "vev", psV_rel[vp])
                for c in range(8):
                    t.matmul(psV[vp], xnTt[:, c * 512 + kk * 128: c * 512 + (kk + 1) * 128],
                             wqkv[:, c * 3072 + 2048 + vh * 512: c * 3072 + 2048 + (vh + 1) * 512],
                             start=(c == 0), stop=False)
                mm = t.matmul(psV[vp], P.ones_row, bv_row[0:1, vh * 512:(vh + 1) * 512],
                              start=False, stop=True)
                pe_cnt = b.inc("pe", mm)
                b.wait(v, "pe", pe_cnt)
                src = psV[vp].rearrange("p (h d) -> p h d", d=64)
                for gsub in range(4):
                    hg = vh * 4 + gsub
                    dst = vstage[par][:, kk * 1040 + hg * 130: kk * 1040 + (hg + 1) * 130]
                    dst = dst.rearrange("p (h e) -> p h e", e=65)[:, :, 0:64]
                    cnt = b.inc("vev", v.tensor_copy(dst, src[:, gsub * 2:gsub * 2 + 2, :]))
                psV_rel[vp] = cnt
        b.wait(y, "vev", b.n("vev"))
        for hg in range(8):
            src = vstage[par].rearrange("p (k g e) -> p k g e", g=8, e=130)[:, :, hg, :]
            b.inc(f"dv{par}", y.dma_start(out=P.v_dram[hg, :, j * 4:(j + 1) * 4, :], in_=src), 16)
        vst_rel[par] = b.n(f"dv{par}")
        for hp in range(8):
            qp = hp % 2
            b.wait(t, psK_rel[qp][0], psK_rel[qp][1])
            for c in range(8):
                mm = t.matmul(psK[qp], wqkv[:, c * 3072 + hp * 128: c * 3072 + (hp + 1) * 128],
                              xnTt[:, c * 512:(c + 1) * 512], start=(c == 0), stop=(c == 7))
            pe_cnt = b.inc("pe", mm)
            b.wait(v, "pe", pe_cnt)
            b.wait(v, f"dq{qp}", qst_rel[qp])
            cnt = b.inc("qev", v.tensor_scalar(qstage[qp], psK[qp],
                                               bq_cols[:, hp:hp + 1], None, ALU.add))
            psK_rel[qp] = ("qev", cnt)
            b.wait(y, "qev", b.n("qev"))
            b.inc(f"dq{qp}", y.dma_start(out=P.q_dram[hp, :, j, :], in_=qstage[qp]), 16)
            qst_rel[qp] = b.n(f"dq{qp}")
    es.close()
    b.barrier()



def emit_p2(P, K_sb):
    nc, b, dr = P.nc, P.b, P.dr
    t, v, s, g, y = b.t, b.v, b.s, b.g, b.y
    es = ExitStack()
    tri = es.enter_context(nc.sbuf_tensor("tri", [128, 2048], BF16)).ap()
    wproj = es.enter_context(nc.sbuf_tensor("wproj", [128, 8 * 1024], BF16)).ap()
    bp_row = es.enter_context(nc.sbuf_tensor("bp_row", [1, 1024], BF16)).ap()
    y_sb = es.enter_context(nc.sbuf_tensor("y_sb", [128, 8 * 512], BF16)).ap()
    exp2 = [es.enter_context(nc.sbuf_tensor(f"exp2_{i}", [128, 1024], BF16)).ap() for i in range(2)]
    vbuf = [es.enter_context(nc.sbuf_tensor(f"vbuf{i}", [128, 32 * 130], BF16)).ap() for i in range(2)]
    qbuf = [es.enter_context(nc.sbuf_tensor(f"qbuf{i}", [128, 512], BF16)).ap() for i in range(2)]
    bc_sb = es.enter_context(nc.sbuf_tensor("bc_sb", [64, 512], BF16)).ap()
    srow = [es.enter_context(nc.sbuf_tensor(f"srowp{i}", [1, 1024], F32)).ap() for i in range(2)]
    xsm = [es.enter_context(nc.sbuf_tensor(f"xsm{i}", [128, 1024], F32)).ap() for i in range(2)]
    psSd = [es.enter_context(nc.psum_tensor(f"psSd{i}", [128, 1024], F32)).ap() for i in range(2)]
    psAV = [es.enter_context(nc.psum_tensor(f"psAVd{i}", [65, 512], F32)).ap() for i in range(2)]
    psOB = es.enter_context(nc.psum_tensor("psOB", [128, 512], F32)).ap()
    psB = psOB[0:64, :]
    b.inc("dw3", g.dma_start(out=tri, in_=dr["l0_tri"]), 16)
    dma3(b, g, wproj, dr["l0_wproj"], 1024, "dw3")
    b.inc("dw3", g.dma_start(out=bp_row, in_=dr["l0_bproj_row"]), 16)
    b.wait(t, "dw3")
    b.wait(v, "dw3")

    exp_rel = {0: ("act", 0), 1: ("act", 0)}
    expbuf_rel = {0: 0, 1: 0}
    vbuf_rel = {0: 0, 1: 0}
    qbuf_rel = {0: 0, 1: 0}
    dvl_cnt = {0: 0, 1: 0}
    dql_cnt = {0: 0, 1: 0}
    psO_rel4 = {0: 0, 1: 0, 2: 0, 3: 0}
    oad_rel2 = {0: 0, 1: 0}
    dx1_rel = {0: 0, 1: 0}
    P._p2_psO_rel = 0
    P._p2_pending = []
    P._p2_avwar = {0: 0, 1: 0}
    P._p2_state = {"srow_war": 0, "psb_war": 0, "bce_war": 0, "vv_rec": 0}

    def make_epi1(av_all, par):
        def epi():
            st = P._p2_state
            pav = psAV[par]
            b.wait(v, "av", av_all)
            b.wait(v, "pe", st["srow_war"])
            b.vstep(v.tensor_scalar(srow[par][0:1, 0:512], pav[64:65, :], 1e-30, None, ALU.max))
            b.vdep()
            b.vstep(v.reciprocal(srow[par][0:1, 512:1024], srow[par][0:1, 0:512]))
            st["vv_rec"] = b.n("vv")
        return epi

    def make_epi2(hh, hg, par):
        def epi():
            st = P._p2_state
            pav = psAV[par]
            b.wait(t, "vv", st["vv_rec"])
            b.wait(t, "bce", st["psb_war"])
            b.wait(t, "oadd", P._p2_psO_rel)
            mmb = t.matmul(psB, P.ones_row_f[0:1, 0:64], srow[par][0:1, 512:1024], start=True, stop=True)
            st["srow_war"] = peb = b.inc("pe", mmb)
            b.wait(s, "pe", peb)
            b.wait(s, "ynorm", st["bce_war"])
            st["psb_war"] = b.inc("bce", s.copy(bc_sb, psB))
            b.wait(v, "bce", st["psb_war"])
            yn = v.tensor_tensor(y_sb[64 * hh:64 * hh + 64, hg * 512:(hg + 1) * 512],
                                 pav[0:64, :], bc_sb, ALU.mult)
            st["bce_war"] = b.inc("ynorm", yn)
            P._p2_avwar[par] = st["bce_war"]
        return epi

    for sl in range(8):
        KT = KT_FULL[sl]
        plist = [(hg, hh, kt2) for hg in range(8) for hh in range(2)
                 for kt2 in range(0, KT, 2)]

        def issue_hg(isl, hg):
            gp = hg % 2
            ktl = KT_FULL[isl]
            b.wait(y, "av", vbuf_rel[gp])
            b.inc(f"dvl{gp}", y.dma_start(
                out=vbuf[gp][:, 0:ktl * 130].rearrange("p (k e) -> p k e", e=130),
                in_=P.v_dram[hg, :, 0:ktl, :]), 16)
            dvl_cnt[gp] = b.n(f"dvl{gp}")
            b.wait(y, "pe", qbuf_rel[gp])
            b.inc(f"dql{gp}", y.dma_start(out=qbuf[gp], in_=P.q_dram[hg, :, isl, :]), 16)
            dql_cnt[gp] = b.n(f"dql{gp}")

        if sl == 0:
            issue_hg(0, 0)
            issue_hg(0, 1)

        def emit_S(j):
            hg, hh, kt2 = plist[j]
            gp = hg % 2
            sp = j % 2
            b.wait(t, exp_rel[sp][0], exp_rel[sp][1])
            if hh == 0 and kt2 == 0:
                b.wait(t, f"dql{gp}", dql_cnt[gp])
            mm = None
            for u in (0, 1):
                kt = kt2 + u
                mm = t.matmul(psSd[sp][:, u * 512:(u + 1) * 512],
                              K_sb[64 * hh:64 * hh + 64, hg * 4096 + kt * 128: hg * 4096 + (kt + 1) * 128],
                              qbuf[gp][64 * hh:64 * hh + 64, :],
                              start=True, stop=True)
            qbuf_rel[gp] = b.inc("pe", mm)
            return b.n("pe")

        pe_S = {0: emit_S(0)}
        for j, (hg, hh, kt2) in enumerate(plist):
            gp = hg % 2
            sp = j % 2
            b.wait(s, "pe", pe_S[j])
            b.wait(s, "av", expbuf_rel[sp])
            e_in = s.activation(exp2[sp], psSd[sp], AF.Exp, scale=0.125)
            act_cnt = b.inc("act", e_in)
            exp_rel[sp] = ("act", act_cnt)
            wait_for_av = ("act", act_cnt)
            for u in (0, 1):
                jj = kt2 + u - (KT - 4)
                if jj >= 0:
                    b.wait(v, "act", act_cnt)
                    b.wait(v, "av", expbuf_rel[sp])
                    mk = v.tensor_tensor(exp2[sp][:, u * 512:(u + 1) * 512],
                                         exp2[sp][:, u * 512:(u + 1) * 512],
                                         tri[:, jj * 512:(jj + 1) * 512], ALU.mult)
                    dve_cnt = b.inc("dve", mk)
                    exp_rel[sp] = ("dve", dve_cnt)
                    wait_for_av = ("dve", dve_cnt)
            if j + 1 < len(plist):
                pe_S[j + 1] = emit_S(j + 1)
            while P._p2_pending and P._p2_pending[0][0] <= j:
                P._p2_pending.pop(0)[1]()
            b.wait(t, wait_for_av[0], wait_for_av[1])
            if kt2 == 0:
                b.wait(t, f"dvl{gp}", dvl_cnt[gp])
                b.wait(t, "ynorm", P._p2_avwar[hh])
            mm = None
            for u in (0, 1):
                kt = kt2 + u
                mm = t.matmul(psAV[hh], vbuf[gp][:, kt * 130 + hh * 65: kt * 130 + hh * 65 + 65],
                              exp2[sp][:, u * 512:(u + 1) * 512],
                              start=(kt == 0), stop=(kt == KT - 1))
            expbuf_rel[sp] = b.inc("av", mm)
            vbuf_rel[gp] = b.n("av")
            if kt2 == KT - 2:
                P._p2_pending.append((j + 1, make_epi1(b.n("av"), hh)))
                P._p2_pending.append((j + 3, make_epi2(hh, hg, hh)))
                if hh == 1:
                    if hg + 2 < 8:
                        issue_hg(sl, hg + 2)
                    elif sl + 1 < 8:
                        issue_hg(sl + 1, hg - 6)
        while P._p2_pending:
            P._p2_pending.pop(0)[1]()
        b.wait(t, "ynorm", b.n("ynorm"))
        for sp_ in (0, 1):
            b.wait(t, exp_rel[sp_][0], exp_rel[sp_][1])
        for tt in range(4):
            tp_ = tt % 2
            dsm = "dx" if tp_ == 0 else "dxa"
            ssm = "dx1" if tp_ == 0 else "dxb"
            b.wait(y, "oadd", oad_rel2[tp_])
            b.wait(y, ssm, dx1_rel[tp_])
            b.inc(dsm, y.dma_start(
                out=xsm[tp_], in_=dr["x_abs"][(sl * 4 + tt) * 128:(sl * 4 + tt + 1) * 128, :]), 16)
            dx_t = b.n(dsm)
            for dh in range(2):
                psO_ = psSd[tp_][:, dh * 512:(dh + 1) * 512]
                b.wait(t, "oadd", psO_rel4[tp_ * 2 + dh])
                for c in range(8):
                    t.matmul(psO_, y_sb[:, c * 512 + tt * 128: c * 512 + (tt + 1) * 128],
                             wproj[:, c * 1024 + dh * 512: c * 1024 + (dh + 1) * 512],
                             start=(c == 0), stop=False)
                mm = t.matmul(psO_, P.ones_row, bp_row[0:1, dh * 512:(dh + 1) * 512],
                              start=False, stop=True)
                pe_cnt = b.inc("pe", mm)
                b.wait(v, "pe", pe_cnt)
                b.wait(v, dsm, dx_t)
                ad = v.tensor_tensor(xsm[tp_][:, dh * 512:(dh + 1) * 512],
                                     xsm[tp_][:, dh * 512:(dh + 1) * 512], psO_, ALU.add)
                psO_rel4[tp_ * 2 + dh] = b.inc("oadd", ad)
            oad_rel2[tp_] = b.n("oadd")
            b.wait(y, "oadd", oad_rel2[tp_])
            b.inc(ssm, y.dma_start(
                out=P.x1_dram[(sl * 4 + tt) * 128:(sl * 4 + tt + 1) * 128, :], in_=xsm[tp_]), 16)
            dx1_rel[tp_] = b.n(ssm)
        exp_rel[0] = ("oadd", psO_rel4[1])
        exp_rel[1] = ("oadd", psO_rel4[3])
    es.close()
    b.barrier()



def emit_p3(P):
    nc, b, dr = P.nc, P.b, P.dr
    t, v, s, g, y = b.t, b.v, b.s, b.g, b.y
    es = ExitStack()
    xnb = [es.enter_context(nc.sbuf_tensor(f"xnb{i}", [128, 8 * 512], BF16)).ap() for i in range(2)]
    w13 = [es.enter_context(nc.sbuf_tensor(f"w13_{i}", [128, 8 * 2048], BF16)).ap() for i in range(2)]
    w2 = es.enter_context(nc.sbuf_tensor("w2_sb", [128, 8 * 1024], BF16)).ap()
    act_sb = es.enter_context(nc.sbuf_tensor("act_sb", [128, 8 * 512], BF16)).ap()
    b13 = es.enter_context(nc.sbuf_tensor("b13", [128, 128], F32)).ap()
    b13u = es.enter_context(nc.sbuf_tensor("b13u", [128, 64], F32)).ap()
    b2row = es.enter_context(nc.sbuf_tensor("b2row", [1, 1024], BF16)).ap()
    g_s = es.enter_context(nc.sbuf_tensor("g_s", [128, 512], F32)).ap()
    s_s = es.enter_context(nc.sbuf_tensor("s_s", [128, 512], F32)).ap()
    u_s = es.enter_context(nc.sbuf_tensor("u_s", [128, 512], F32)).ap()
    contrib = [es.enter_context(nc.sbuf_tensor(f"contrib{i}", [128, 1024], F32)).ap() for i in range(4)]
    psA = [es.enter_context(nc.psum_tensor(f"psA_{i}", [128, 512], F32)).ap() for i in range(2)]
    psB2 = [es.enter_context(nc.psum_tensor(f"psB2_{i}", [128, 512], F32)).ap() for i in range(2)]
    psM = [es.enter_context(nc.psum_tensor(f"psM{i}", [128, 512], F32)).ap() for i in range(2)]

    b.inc("dmo", g.dma_start(out=b13, in_=dr["l0_b13_cols"]), 16)
    b.wait(v, "dmo", 16)
    b.vstep(v.tensor_scalar(
        b13u.rearrange("p (e f) -> p e f", f=8),
        b13.rearrange("p (e f) -> p e f", f=16)[:, :, 8:16],
        1.0, None, ALU.add))
    b13u_vv = b.n("vv")

    w13_rel = [0, 0]
    dma3(b, g, w13[0], dr["l0_w13"][0], 2048, "dm13_0")
    dma3(b, g, w2, dr["l0_w2"][0], 1024, "dmo2")
    psA_rel = [0, 0]
    psB2_rel = [0, 0]
    b2_rel = 0
    psM_rel = [0, 0]
    act_war = 0
    gs_war = 0
    ss_war = 0
    acc_reg = {}
    xnb_rel = [0, 0]
    cb_rel = [0, 0, 0, 0]
    dxnl_cnt = {0: 0, 1: 0}

    def issue_xnb(xp, tc):
        b.wait(y, "pe", xnb_rel[xp])
        b.inc(f"dxnl{xp}", y.dma_start(
            out=xnb[xp].rearrange("p (c f) -> p c f", f=512),
            in_=P.xnt_dram[tc]), 16)
        dxnl_cnt[xp] = b.n(f"dxnl{xp}")

    for e in range(8):
        ep = e % 2
        if e + 1 < 8:
            b.wait(g, "pe", w13_rel[1 - ep])
            dma3(b, g, w13[1 - ep], dr["l0_w13"][e + 1], 2048, f"dm13_{1 - ep}")
        b.wait(g, "pe", b2_rel)
        b.inc("dmo2e", g.dma_start(out=b2row, in_=dr["l0_b2_rows"][0:1, e * 1024:(e + 1) * 1024]), 16)
        b.wait(t, f"dm13_{ep}", b.n(f"dm13_{ep}"))
        b.wait(t, "dmo2e", b.n("dmo2e"))
        for tc in range(8):
            xp = tc % 2
            if e == 0 and tc == 0:
                issue_xnb(0, 0)
            b.wait(t, f"dxnl{xp}", dxnl_cnt[xp])
            for ip in range(8):
                if ip == 3:
                    if tc + 1 < 8:
                        issue_xnb(1 - xp, tc + 1)
                    elif e + 1 < 8:
                        issue_xnb(1 - xp, 0)
                ipp = ip % 2
                b.wait(t, "mgl", psA_rel[ipp])
                for c in range(8):
                    mm = t.matmul(psA[ipp], w13[ep][:, c * 2048 + ip * 128: c * 2048 + (ip + 1) * 128],
                                  xnb[xp][:, c * 512:(c + 1) * 512], start=(c == 0), stop=(c == 7))
                peA = b.inc("pe", mm)
                b.wait(t, "mup", psB2_rel[ipp])
                for c in range(8):
                    mm = t.matmul(psB2[ipp], w13[ep][:, c * 2048 + 1024 + ip * 128: c * 2048 + 1024 + (ip + 1) * 128],
                                  xnb[xp][:, c * 512:(c + 1) * 512], start=(c == 0), stop=(c == 7))
                peB = b.inc("pe", mm)
                w13_rel[ep] = peB
                b.wait(v, "pe", peA)
                b.wait(v, "sil", gs_war)
                psA_rel[ipp] = b.inc("mgl", v.tensor_scalar(
                    g_s, psA[ipp], b13[:, e * 16 + ip:e * 16 + ip + 1], 7.0, ALU.add, ALU.min))
                b.wait(s, "mgl", psA_rel[ipp])
                b.wait(s, "mact", ss_war)
                gs_war = b.inc("sil", s.activation(s_s, g_s, AF.Silu, scale=1.702))
                b.vdep(b13u_vv)
                b.wait(v, "mact", ss_war)
                b.wait(v, "pe", peB)
                b.wait(v, "pe", act_war)
                psB2_rel[ipp] = b.inc("mup", v.tensor_scalar(
                    u_s, psB2[ipp], b13u[:, e * 8 + ip:e * 8 + ip + 1], 8.0, ALU.add, ALU.min))
                b.wait(v, "sil", gs_war)
                b.wait(v, "mup", psB2_rel[ipp])
                ss_war = b.inc("mact", v.scalar_tensor_tensor(
                    act_sb[:, ip * 512:(ip + 1) * 512], u_s, -6.0, s_s, ALU.max, ALU.mult))
            xnb_rel[xp] = b.n("pe")
            b.wait(t, "mact", b.n("mact"))
            if tc == 0:
                b.wait(t, "dmo2", 16 * (e + 1))
            for tt in range(4):
                T = tc * 4 + tt
                cp = tt
                for dh in range(2):
                    mp = dh
                    b.wait(t, "macc", psM_rel[mp])
                    for ic in range(8):
                        t.matmul(psM[mp], act_sb[:, ic * 512 + tt * 128: ic * 512 + (tt + 1) * 128],
                                 w2[:, ic * 1024 + dh * 512: ic * 1024 + (dh + 1) * 512],
                                 start=(ic == 0), stop=False)
                    mm = t.matmul(psM[mp], P.row1702,
                                  b2row[0:1, dh * 512:(dh + 1) * 512], start=False, stop=True)
                    peM = b2_rel = b.inc("pe", mm)
                    b.wait(s, "pe", peM)
                    if cb_rel[cp] != 0:
                        b.wait(s, cb_rel[cp][0], cb_rel[cp][1])
                    psM_rel[mp] = b.inc("macc", s.activation(
                        contrib[cp][:, dh * 512:(dh + 1) * 512], psM[mp], AF.Identity,
                        scale=P.wsc[:, T * 8 + e: T * 8 + e + 1]))
                b.wait(g, "macc", b.n("macc"))
                dsem = f"dacc{T % 16}"
                b.wait(g, dsem, b.n(dsem))
                b.inc(dsem, g.dma_start(
                    out=P.x1_dram[T * 128:(T + 1) * 128, :], in_=contrib[cp],
                    accum_op=ALU.add), 16)
                cb_rel[cp] = (dsem, b.n(dsem))
            act_war = b.n("pe")
        if e + 1 < 8:
            b.wait(g, "pe", b.n("pe"))
            dma3(b, g, w2, dr["l0_w2"][e + 1], 1024, "dmo2")
    es.close()
    b.barrier()



def emit_p4(P, xn1T):
    nc, b, dr = P.nc, P.b, P.dr
    t, v, s, g, y = b.t, b.v, b.s, b.g, b.y
    es = ExitStack()
    xA = [es.enter_context(nc.sbuf_tensor(f"xA{i}", [128, 4096], F32)).ap() for i in range(2)]
    xB = [es.enter_context(nc.sbuf_tensor(f"xB{i}", [128, 4096], F32)).ap() for i in range(2)]
    xnf = [es.enter_context(nc.sbuf_tensor(f"xnf4{i}", [128, 1024], F32)).ap() for i in range(2)]
    scr = es.enter_context(nc.sbuf_tensor("scr4", [128, 1024], F32)).ap()
    xnTf = es.enter_context(nc.sbuf_tensor("xnTf4", [128, 1024], F32)).ap()
    gwT = es.enter_context(nc.sbuf_tensor("gwT", [128, 8 * 16], F32)).ap()
    gb_sb = es.enter_context(nc.sbuf_tensor("gb_sb", [128, 16], F32)).ap()
    gbrow = es.enter_context(nc.sbuf_tensor("gbrow", [1, 16], F32)).ap()
    tsc = es.enter_context(nc.sbuf_tensor("tsc", [128, 80], F32)).ap()
    psT = [es.enter_context(nc.psum_tensor(f"psT4_{i}", [128, 128], F32)).ap() for i in range(2)]
    psG = es.enter_context(nc.psum_tensor("psG", [128, 16], F32)).ap()
    dma3(b, g, gwT, dr["l1_gate_wT"], 16, "dl1")
    b.inc("dl1", g.dma_start(out=gbrow, in_=dr["l1_gate_b_row"]), 16)
    b.wait(t, "dl1")
    b.wait(t, "cmisc")
    b.wait(t, "dmisc")
    mmgb = t.matmul(psG, P.ones_row_f[0:1, :], gbrow[0:1, :], start=True, stop=True)
    pegb = b.inc("pe", mmgb)
    b.wait(v, "pe", pegb)
    b.vstep(v.tensor_copy(gb_sb, psG))
    gb_vv = b.n("vv")
    b.wait(v, "dmisc")

    psT_rel = [0, 0]
    psG_rel = ("vv", gb_vv)
    ln_done = 0
    ln_rel4 = {0: 0, 1: 0}
    sel_rel4 = {0: 0, 1: 0}
    xnf_rel4 = {0: 0, 1: 0}
    dxa_cnt = {0: 0, 1: 0}
    dxb_cnt = {0: 0, 1: 0}

    def issue_x14(tloc):
        pp = tloc % 2
        offA = 512 * tloc
        offB = (2048 + 512 * tloc) % 4096
        sma = "dxb" if pp == 0 else "dv0"
        smb = "dv1" if pp == 0 else "dq1"
        b.wait(y, "vv", sel_rel4[pp])
        b.wait(y, "ln_out", ln_rel4[pp])
        dma3(b, y, xA[pp], P.x1_dram[offA:offA + 512, :], 1024, sma)
        dma3(b, y, xB[pp], P.x1_dram[offB:offB + 512, :], 1024, smb)
        dxa_cnt[pp] = (sma, b.n(sma))
        dxb_cnt[pp] = (smb, b.n(smb))

    issue_x14(0)
    for tloc in range(8):
        pp = tloc % 2
        if tloc + 1 < 8:
            issue_x14(tloc + 1)
        b.wait(v, dxa_cnt[pp][0], dxa_cnt[pp][1])
        b.wait(v, dxb_cnt[pp][0], dxb_cnt[pp][1])
        b.wait(v, "ln_out", b.n("ln_out"))
        b.vdep()
        b.vstep(v.tensor_scalar(xA[pp], xA[pp], P.hsel[:, 0:1], None, ALU.mult))
        b.vdep()
        b.vstep(v.scalar_tensor_tensor(xA[pp], xB[pp], P.hsel[:, 1:2], xA[pp], ALU.mult, ALU.add))
        sel_rel4[pp] = b.n("vv")
        P.emit_ln_stats(xA[pp], scr, [("vv", sel_rel4[pp])])
        for tt in range(4):
            T = tloc * 4 + tt
            fp_ = tt % 2
            b.wait(s, "pe", xnf_rel4[fp_])
            ln_done = P.emit_ln_tile(xA[pp], tt, xnf[fp_])
            if tt == 3:
                ln_rel4[pp] = ln_done
            for c in range(8):
                tp = c % 2
                b.wait(t, "xe4", psT_rel[tp])
                b.wait(t, "ln_out", ln_done)
                pe_cnt = b.inc("pe", t.transpose(psT[tp], xnf[fp_][:, c * 128:(c + 1) * 128], P.ident))
                b.wait(v, "pe", pe_cnt)
                psT_rel[tp] = b.inc("xe4", v.tensor_copy(
                    xn1T[:, c * 4096 + T * 128: c * 4096 + (T + 1) * 128], psT[tp]))
                if tloc < 4:
                    psT_rel[tp] = b.inc("xe4", v.tensor_copy(xnTf[:, c * 128:(c + 1) * 128], psT[tp]))
            xnf_rel4[fp_] = b.n("pe")
            if tloc >= 4:
                continue
            b.wait(t, "xe4", b.n("xe4"))
            b.wait(t, psG_rel[0], psG_rel[1])
            for c in range(8):
                mm = t.matmul(psG, xnTf[:, c * 128:(c + 1) * 128], gwT[:, c * 16:(c + 1) * 16],
                              start=(c == 0), stop=(c == 7))
            gmm = b.inc("rmm", mm)
            b.wait(s, "rmm", gmm)
            b.wait(s, "vv", getattr(P, "_l1tsc", 0))
            b.sstep(s.activation(tsc[:, 0:16], psG, AF.Sigmoid))
            psG_rel = ("ss", b.n("ss"))
            b.wait(v, "ss", b.n("ss"))
            b.vdep(gb_vv)
            b.vstep(v.tensor_tensor(tsc[:, 16:32], tsc[:, 0:16], gb_sb, ALU.add))
            b.vdep()
            b.vstep(v.tensor_copy(tsc[:, 32:48], tsc[:, 16:32]))
            for it in range(4):
                b.vdep()
                b.vstep(v.reduce_max(tsc[:, 64 + it:65 + it], tsc[:, 32:48], AXX))
                if it < 3:
                    b.vdep()
                    b.vstep(v.tensor_scalar(tsc[:, 48:64], tsc[:, 32:48], tsc[:, 64 + it:65 + it], None, ALU.is_equal))
                    b.vdep()
                    b.vstep(v.scalar_tensor_tensor(tsc[:, 32:48], tsc[:, 48:64], -1e30, tsc[:, 32:48], ALU.mult, ALU.add))
            b.vdep()
            b.vstep(v.tensor_scalar(tsc[:, 48:64], tsc[:, 16:32], tsc[:, 67:68], None, ALU.is_ge))
            b.vdep()
            b.vstep(v.tensor_tensor(tsc[:, 32:48], tsc[:, 0:16], tsc[:, 48:64], ALU.mult))
            b.vdep()
            b.vstep(v.reduce_sum(tsc[:, 68:69], tsc[:, 32:48], AXX))
            b.vdep()
            b.vstep(v.reciprocal(tsc[:, 69:70], tsc[:, 68:69]))
            b.vdep()
            b.vstep(v.tensor_scalar(P.comb[:, T * 16:(T + 1) * 16], tsc[:, 32:48],
                                    tsc[:, 69:70], 2.5, ALU.mult, ALU.mult))
            P._l1tsc = b.n("vv")
    es.close()
    b.barrier()


def emit_p4b(P, xn1T, pooled, rk_sb, rv_sb):
    nc, b, dr = P.nc, P.b, P.dr
    t, v, s, g, y = b.t, b.v, b.s, b.g, b.y
    es = ExitStack()
    rkvw = es.enter_context(nc.sbuf_tensor("rkvw", [128, 8 * 512], BF16)).ap()
    rkb = es.enter_context(nc.sbuf_tensor("rkb", [128, 2], F32)).ap()
    rvb = es.enter_context(nc.sbuf_tensor("rvb", [1, 256], BF16)).ap()
    psP1 = es.enter_context(nc.psum_tensor("psP1", [128, 512], F32)).ap()
    dma3(b, g, rkvw, dr["l1_rkv_w"], 512, "drkv")
    b.inc("drkv", g.dma_start(out=rkb, in_=dr["l1_rk_b_cols"]), 16)
    b.inc("drkv", g.dma_start(out=rvb, in_=dr["l1_rv_b_row"]), 16)
    b.wait(t, "drkv")
    b.inc("rvones", v.memset(rv_sb, 1.0))
    b.wait(t, "rvones")
    poolf = es.enter_context(nc.sbuf_tensor("poolf", [128, 64], F32)).ap()
    for c in range(8):
        src = xn1T[:, c * 4096:(c + 1) * 4096].rearrange("p (l k) -> p l k", k=64)
        b.vdep()
        b.vstep(v.reduce_sum(poolf, src, AXX))
        b.vdep()
        b.vstep(v.tensor_scalar(pooled[:, c * 64:(c + 1) * 64],
                                poolf, 1.0 / 64.0, None, ALU.mult))
    pooled_bf = rv_sb
    b.wait(t, "vv", b.n("vv"))
    for rt in range(2):
        b.wait(t, "rkev", b.n("rkev"))
        for c in range(8):
            mm = t.matmul(psP1[:, 0:64], rkvw[:, c * 512 + rt * 128: c * 512 + (rt + 1) * 128],
                          pooled[:, c * 64:(c + 1) * 64], start=(c == 0), stop=(c == 7))
        pe_cnt = b.inc("pe", mm)
        b.wait(s, "pe", pe_cnt)
        b.inc("rkev", s.activation(rk_sb[:, rt * 64:(rt + 1) * 64], psP1[:, 0:64],
                                   AF.Identity, bias=rkb[:, rt:rt + 1]))
    b.wait(t, "rkev", b.n("rkev"))
    for c in range(8):
        t.matmul(psP1[0:64, 0:256], pooled[:, c * 64:(c + 1) * 64],
                 rkvw[:, c * 512 + 256: c * 512 + 512], start=(c == 0), stop=False)
    mm = t.matmul(psP1[0:64, 0:256], P.ones_row[0:1, 0:64], rvb[0:1, :], start=False, stop=True)
    pe_cnt = b.inc("pe", mm)
    b.wait(v, "pe", pe_cnt)
    dstv = rv_sb.rearrange("p (h e) -> p h e", e=65)[:, :, 0:64]
    b.vstep(v.tensor_copy(dstv, psP1[0:64, 0:256].rearrange("p (h d) -> p h d", d=64)))
    b.barrier()
    es.close()



def emit_p5(P, xn1T, pooled, rk_sb, rv_sb):
    nc, b, dr = P.nc, P.b, P.dr
    t, v, s, g, y = b.t, b.v, b.s, b.g, b.y
    es = ExitStack()
    wkv1 = es.enter_context(nc.sbuf_tensor("wkv1", [128, 8 * 2048], BF16)).ap()
    wq1 = es.enter_context(nc.sbuf_tensor("wq1", [128, 8 * 1024], BF16)).ap()
    rqw = es.enter_context(nc.sbuf_tensor("rqw", [128, 8 * 256], BF16)).ap()
    routw = es.enter_context(nc.sbuf_tensor("routw", [128, 2 * 1024], BF16)).ap()
    bq1 = es.enter_context(nc.sbuf_tensor("bq1", [128, 24], F32)).ap()
    bv1 = es.enter_context(nc.sbuf_tensor("bv1", [1, 1024], BF16)).ap()
    rqb = es.enter_context(nc.sbuf_tensor("rqb", [128, 2], F32)).ap()
    routb = es.enter_context(nc.sbuf_tensor("routb", [128, 8], F32)).ap()
    glb = es.enter_context(nc.sbuf_tensor("glb", [128, 2], BF16)).ap()
    grb = es.enter_context(nc.sbuf_tensor("grb", [128, 2], BF16)).ap()
    mloc4 = es.enter_context(nc.sbuf_tensor("mloc4", [128, 4096], BF16)).ap()
    mlat = es.enter_context(nc.sbuf_tensor("mlat", [64, 2048], BF16)).ap()
    K1 = es.enter_context(nc.sbuf_tensor("K1", [128, 8 * 640], BF16)).ap()
    V1 = es.enter_context(nc.sbuf_tensor("V1", [128, 5 * 1040], BF16)).ap()
    Q1 = es.enter_context(nc.sbuf_tensor("Q1", [128, 8 * 512], BF16)).ap()
    rq1 = es.enter_context(nc.sbuf_tensor("rq1", [128, 2 * 512], BF16)).ap()
    lat1 = es.enter_context(nc.sbuf_tensor("lat1", [128, 2 * 512], BF16)).ap()
    rexp = es.enter_context(nc.sbuf_tensor("rexp", [64, 512], BF16)).ap()
    exp4 = [es.enter_context(nc.sbuf_tensor(f"exp4_{i}", [128, 1024], BF16)).ap() for i in range(2)]
    loc1 = [es.enter_context(nc.sbuf_tensor(f"loc1_{i}", [128, 512], BF16)).ap() for i in range(2)]
    rem1 = es.enter_context(nc.sbuf_tensor("rem1", [128, 512], BF16)).ap()
    d1f = es.enter_context(nc.sbuf_tensor("d1f", [128, 512], F32)).ap()
    bc1 = es.enter_context(nc.sbuf_tensor("bc1", [64, 512], BF16)).ap()
    g1 = es.enter_context(nc.sbuf_tensor("g1", [65, 512], F32)).ap()
    srow1 = es.enter_context(nc.sbuf_tensor("srow1", [1, 1024], F32)).ap()
    mxo = [es.enter_context(nc.sbuf_tensor(f"mxo{i}", [128, 512], BF16)).ap() for i in range(2)]
    psSc = [es.enter_context(nc.psum_tensor(f"psSc{i}", [128, 1024], F32)).ap() for i in range(2)]
    psAV4 = [es.enter_context(nc.psum_tensor(f"psAV4_{i}", [65, 512], F32)).ap() for i in range(2)]
    psB1 = es.enter_context(nc.psum_tensor("psB1", [65, 512], F32)).ap()
    psX = es.enter_context(nc.psum_tensor("psX", [128, 512], F32)).ap()
    psPa = psX[:, 0:128]
    psPb = psSc[0][:, 0:512]
    psL = psSc[1][0:64, 0:512]
    psLAV = psSc[1][0:65, 512:1024]

    dma3(b, g, wkv1, dr["l1_wqkv"][:, 1024:3072], 2048, "da1")
    dma3(b, g, wq1, dr["l1_wqkv"][:, 0:1024], 1024, "da1")
    dma3(b, g, rqw, dr["l1_rq_w"], 256, "da1")
    dma3(b, g, routw, dr["l1_rout_w"], 1024, "da1")
    b.inc("da1", g.dma_start(out=bq1, in_=dr["l1_bqkv_cols"]), 16)
    b.inc("da1", g.dma_start(out=bv1, in_=dr["l1_bv_row"]), 16)
    b.inc("da1", g.dma_start(out=rqb, in_=dr["l1_rq_b_cols"]), 16)
    b.inc("da1", g.dma_start(out=routb, in_=dr["l1_rout_b_cols"]), 16)
    b.inc("da1", g.dma_start(out=glb, in_=dr["l1_gl_blk"]), 16)
    b.inc("da1", g.dma_start(out=grb, in_=dr["l1_gr_blk"]), 16)
    b.inc("da1", g.dma_start(out=mloc4, in_=dr["l1_mloc4"]), 16)
    b.inc("da1", g.dma_start(out=mlat, in_=dr["l1_mlat"]), 16)
    b.inc("v1ones", v.memset(V1, 1.0))
    b.wait(t, "da1")
    b.wait(t, "v1ones")
    b.wait(v, "da1")
    b.wait(s, "da1")

    def halo_cols(c, sl):
        if sl == 0:
            return xn1T[:, c * 4096 + 3968: c * 4096 + 4096]
        return xn1T[:, c * 4096 + sl * 512 - 128: c * 4096 + sl * 512]

    st = {
        "kev_rel": 0, "qev_rel": [0, 0],
        "psL_rel": ("ss", 0), "srow_war": 0, "psb_war": 0, "bce_war": ("lyn", 0),
        "vv_rec": 0, "psRt_rel": 0, "rem_rel": 0, "g1_war": 0, "psGB_rel": 0,
        "mx_rel": [0, 0], "locrel_vv": [0, 0], "locrel_pe": [0, 0],
    }
    exp_rel = {0: [("ss", 0)], 1: [("ss", 0)]}
    expb_rel = {0: 0, 1: 0}
    avrel = {0: ("locn", 0), 1: ("locn", 0)}
    pend = []

    def mk_epi1(lv_all, par):
        def f():
            b.wait(v, "lv", lv_all)
            b.wait(v, "pe", st["srow_war"])
            b.vstep(v.tensor_scalar(srow1[0:1, 0:512], psAV4[par][64:65, :], 1e-30, None, ALU.max))
            b.vdep()
            b.vstep(v.reciprocal(srow1[0:1, 512:1024], srow1[0:1, 0:512]))
            st["vv_rec"] = b.n("vv")
        return f

    def mk_epi2(hp, hh, par):
        def f():
            b.wait(t, "vv", st["vv_rec"])
            b.wait(t, "bce1", st["psb_war"])
            b.wait(t, "vv", st["psGB_rel"])
            mmb = t.matmul(psB1[0:64, :], P.ones_row_f[0:1, 0:64], srow1[0:1, 512:1024],
                           start=True, stop=True)
            st["srow_war"] = peb = b.inc("pe", mmb)
            b.wait(s, "pe", peb)
            b.wait(s, st["bce_war"][0], st["bce_war"][1])
            st["psb_war"] = b.inc("bce1", s.copy(bc1, psB1[0:64, :]))
            b.wait(v, "bce1", st["psb_war"])
            b.wait(v, "vv", st["locrel_vv"][hp % 2])
            b.wait(v, "pe", st["locrel_pe"][hp % 2])
            yn = v.tensor_tensor(loc1[hp % 2][64 * hh:64 * hh + 64, :],
                                 psAV4[par][0:64, :], bc1, ALU.mult)
            cnt = b.inc("locn", yn)
            st["bce_war"] = ("locn", cnt)
            avrel[par] = ("locn", cnt)
        return f

    def mk_stage3(hp, sl):
        def f():
            mp_ = hp % 2
            lcb = loc1[hp % 2]
            b.wait(t, "remev", st["psRt_rel"])
            mm = None
            for lp in range(2):
                mm = t.matmul(psX, routw[:, lp * 1024 + hp * 128: lp * 1024 + (hp + 1) * 128],
                              lat1[:, lp * 512:(lp + 1) * 512], start=(lp == 0), stop=(lp == 1))
            pe_cnt = b.inc("pe", mm)
            b.wait(s, "pe", pe_cnt)
            b.wait(s, "vv", st["rem_rel"])
            remev = b.inc("remev", s.activation(rem1, psX, AF.Identity, bias=routb[:, hp:hp + 1]))
            st["psRt_rel"] = remev
            locn_all = b.n("locn")
            b.wait(t, "locn", locn_all)
            b.wait(t, "remev", remev)
            b.wait(t, "bce1", st["psb_war"])
            t.matmul(psB1[0:1, :], glb[:, 0:1], lcb, start=True, stop=False)
            t.matmul(psB1[0:1, :], grb[:, 0:1], rem1, start=False, stop=True)
            t.matmul(psB1[64:65, :], glb[:, 1:2], lcb, start=True, stop=False)
            mm = t.matmul(psB1[64:65, :], grb[:, 1:2], rem1, start=False, stop=True)
            peg = b.inc("pe", mm)
            b.wait(s, "pe", peg)
            b.wait(s, "pe", st["g1_war"])
            s.activation(g1[0:1, :], psB1[0:1, :], AF.Sigmoid)
            sg = s.activation(g1[64:65, :], psB1[64:65, :], AF.Sigmoid)
            sg_cnt = b.inc("ss", sg)
            b.wait(t, "ss", sg_cnt)
            b.wait(t, avrel[1][0], avrel[1][1])
            t.matmul(psAV4[1][0:64, :], P.ones_row_f[0:1, 0:64], g1[0:1, :], start=True, stop=True)
            mm2 = t.matmul(psB1[0:64, :], P.ones_row_f[64:65, 0:64], g1[64:65, :], start=True, stop=True)
            peb3 = b.inc("pe", mm2)
            b.wait(v, "pe", peb3)
            b.wait(v, "remev", remev)
            b.wait(v, "locn", locn_all)
            b.vdep()
            b.vstep(v.tensor_sub(d1f, lcb, rem1))
            b.vdep()
            b.vstep(v.tensor_tensor(d1f[0:64, :], d1f[0:64, :], psAV4[1][0:64, :], ALU.mult))
            b.vstep(v.tensor_tensor(d1f[64:128, :], d1f[64:128, :], psB1[0:64, :], ALU.mult))
            b.wait(v, f"dmx{mp_}", st["mx_rel"][mp_])
            b.vdep()
            b.vstep(v.tensor_tensor(mxo[mp_], d1f, rem1, ALU.add))
            b.wait(y, "vv", b.n("vv"))
            b.inc(f"dmx{mp_}", y.dma_start(out=P.mx_dram[hp, :, sl * 512:(sl + 1) * 512], in_=mxo[mp_]), 16)
            st["mx_rel"][mp_] = b.n(f"dmx{mp_}")
            vvn = b.n("vv")
            st["rem_rel"] = vvn
            st["psGB_rel"] = vvn
            st["g1_war"] = peb3
            st["psb_war"] = max(st["psb_war"], 0)
            avrel[1] = ("vv", vvn)
            st["locrel_vv"][hp % 2] = vvn
            st["locrel_pe"][hp % 2] = peg
        return f

    for sl in range(4):
        for hp in range(8):
            b.wait(t, "kev", st["kev_rel"])
            if hp == 0:
                b.wait(t, "remev", st["psRt_rel"])
                for w_ in exp_rel[0]:
                    b.wait(t, w_[0], w_[1])
            mm = None
            for c in range(8):
                mm = t.matmul(psPa, wkv1[:, c * 2048 + hp * 128: c * 2048 + (hp + 1) * 128],
                              halo_cols(c, sl), start=(c == 0), stop=(c == 7))
            pea = b.inc("pe", mm)
            for c in range(8):
                mm = t.matmul(psPb, wkv1[:, c * 2048 + hp * 128: c * 2048 + (hp + 1) * 128],
                              xn1T[:, c * 4096 + sl * 512: c * 4096 + (sl + 1) * 512],
                              start=(c == 0), stop=(c == 7))
            peb = b.inc("pe", mm)
            b.wait(s, "pe", peb)
            s.activation(K1[:, hp * 640: hp * 640 + 128], psPa, AF.Identity, bias=bq1[:, 8 + hp:9 + hp])
            st["kev_rel"] = b.inc("kev", s.activation(
                K1[:, hp * 640 + 128: hp * 640 + 640], psPb, AF.Identity, bias=bq1[:, 8 + hp:9 + hp]))
        b.wait(t, "kev", b.n("kev"))
        for kt in range(5):
            for vh in range(2):
                b.wait(t, "vev", b.n("vev"))
                for c in range(8):
                    src = halo_cols(c, sl) if kt == 0 else \
                        xn1T[:, c * 4096 + sl * 512 + (kt - 1) * 128: c * 4096 + sl * 512 + kt * 128]
                    t.matmul(psPb, src,
                             wkv1[:, c * 2048 + 1024 + vh * 512: c * 2048 + 1024 + (vh + 1) * 512],
                             start=(c == 0), stop=False)
                mm = t.matmul(psPb, P.ones_row, bv1[0:1, vh * 512:(vh + 1) * 512], start=False, stop=True)
                pe_cnt = b.inc("pe", mm)
                b.wait(v, "pe", pe_cnt)
                src = psPb.rearrange("p (h d) -> p h d", d=64)
                for gsub in range(4):
                    hg = vh * 4 + gsub
                    dst = V1[:, kt * 1040 + hg * 130: kt * 1040 + (hg + 1) * 130]
                    dst = dst.rearrange("p (h e) -> p h e", e=65)[:, :, 0:64]
                    b.inc("vev", v.tensor_copy(dst, src[:, gsub * 2:gsub * 2 + 2, :]))
        b.wait(t, "vev", b.n("vev"))
        for hp in range(8):
            qp = hp % 2
            b.wait(t, "qev", st["qev_rel"][qp])
            ps = psPb if qp else psX
            for c in range(8):
                mm = t.matmul(ps, wq1[:, c * 1024 + hp * 128: c * 1024 + (hp + 1) * 128],
                              xn1T[:, c * 4096 + sl * 512: c * 4096 + (sl + 1) * 512],
                              start=(c == 0), stop=(c == 7))
            pe_cnt = b.inc("pe", mm)
            b.wait(s, "pe", pe_cnt)
            st["qev_rel"][qp] = b.inc("qev", s.activation(
                Q1[:, hp * 512:(hp + 1) * 512], ps, AF.Identity, bias=bq1[:, hp:hp + 1]))
        b.wait(t, "qev", b.n("qev"))
        for rt in range(2):
            qp = rt % 2
            b.wait(t, "rqev", max(0, b.n("rqev") - 1))
            ps = psPb if qp else psX
            for c in range(8):
                mm = t.matmul(ps, rqw[:, c * 256 + rt * 128: c * 256 + (rt + 1) * 128],
                              xn1T[:, c * 4096 + sl * 512: c * 4096 + (sl + 1) * 512],
                              start=(c == 0), stop=(c == 7))
            pe_cnt = b.inc("pe", mm)
            b.wait(s, "pe", pe_cnt)
            b.inc("rqev", s.activation(rq1[:, rt * 512:(rt + 1) * 512], ps,
                                       AF.Identity, bias=rqb[:, rt:rt + 1]))
        b.wait(t, "rqev", b.n("rqev"))
        for lh in range(4):
            lp, lr = lh // 2, 64 * (lh % 2)
            b.wait(t, st["psL_rel"][0], st["psL_rel"][1])
            if lh == 0:
                for w_ in exp_rel[1]:
                    b.wait(t, w_[0], w_[1])
            mm = t.matmul(psL, rk_sb[lr:lr + 64, lp * 64:(lp + 1) * 64],
                          rq1[lr:lr + 64, lp * 512:(lp + 1) * 512], start=True, stop=True)
            pe_cnt = b.inc("pe", mm)
            b.wait(s, "pe", pe_cnt)
            b.wait(s, "lmk", b.n("lmk"))
            b.sstep(s.activation(rexp, psL, AF.Exp, scale=0.125))
            st["psL_rel"] = ("ss", b.n("ss"))
            b.wait(v, "ss", b.n("ss"))
            b.wait(v, "lav", b.n("lav"))
            lmk = b.inc("lmk", v.tensor_tensor(rexp, rexp, mlat[:, sl * 512:(sl + 1) * 512], ALU.mult))
            b.wait(t, "lmk", lmk)
            b.wait(t, "lyn", b.n("lyn"))
            mm = t.matmul(psLAV, rv_sb[:, lh * 65: lh * 65 + 65], rexp, start=True, stop=True)
            b.inc("lav", mm)
            b.wait(v, "lav", b.n("lav"))
            b.wait(v, "pe", st["srow_war"])
            b.vstep(v.tensor_scalar(srow1[0:1, 0:512], psLAV[64:65, :], 1e-30, None, ALU.max))
            b.vdep()
            b.vstep(v.reciprocal(srow1[0:1, 512:1024], srow1[0:1, 0:512]))
            b.wait(t, "vv", b.n("vv"))
            b.wait(t, "bce1", st["psb_war"])
            b.wait(t, "vv", st["psGB_rel"])
            mmb = t.matmul(psB1[0:64, :], P.ones_row_f[0:1, 0:64], srow1[0:1, 512:1024],
                           start=True, stop=True)
            st["srow_war"] = peb = b.inc("pe", mmb)
            b.wait(s, "pe", peb)
            b.wait(s, st["bce_war"][0], st["bce_war"][1])
            st["psb_war"] = b.inc("bce1", s.copy(bc1, psB1[0:64, :]))
            b.wait(v, "bce1", b.n("bce1"))
            yn = v.tensor_tensor(lat1[lr:lr + 64, lp * 512:(lp + 1) * 512], psLAV[0:64, :], bc1, ALU.mult)
            st["bce_war"] = ("lyn", b.inc("lyn", yn))
        b.wait(t, "lyn", b.n("lyn"))
        exp_rel[0] = [("rqev", b.n("rqev"))]
        exp_rel[1] = [("lyn", b.n("lyn"))]
        for hp in range(8):
            for hh in range(2):
                ci = hp * 2 + hh
                sp = hh
                for w_ in exp_rel[sp]:
                    b.wait(t, w_[0], w_[1])
                psc = psSc[sp]
                mm = None
                for qt in range(4):
                    for u in (0, 1):
                        mm = t.matmul(
                            psc[:, qt * 256 + u * 128: qt * 256 + (u + 1) * 128],
                            K1[64 * hh:64 * hh + 64, hp * 640 + (qt + u) * 128: hp * 640 + (qt + u + 1) * 128],
                            Q1[64 * hh:64 * hh + 64, hp * 512 + qt * 128: hp * 512 + (qt + 1) * 128],
                            start=True, stop=True)
                pe_cnt = b.inc("pe", mm)
                b.wait(s, "pe", pe_cnt)
                b.wait(s, "lv", expb_rel[sp])
                e_in = s.activation(exp4[sp], psc, AF.Exp, scale=0.125)
                ss_cnt = b.inc("ss", e_in)
                exp_rel[sp] = [("ss", ss_cnt)]
                b.wait(v, "ss", ss_cnt)
                b.wait(v, "lv", expb_rel[sp])
                mk = b.inc("lmkk", v.tensor_tensor(exp4[sp], exp4[sp],
                                                   mloc4[:, sl * 1024:(sl + 1) * 1024], ALU.mult))
                exp_rel[sp] = [("lmkk", mk)]
                while pend and pend[0][0] <= ci:
                    pend.pop(0)[1]()
                b.wait(t, "lmkk", mk)
                b.wait(t, avrel[sp][0], avrel[sp][1])
                mm = None
                for qt in range(4):
                    t.matmul(psAV4[sp][:, qt * 128:(qt + 1) * 128],
                             V1[:, qt * 1040 + hp * 130 + hh * 65: qt * 1040 + hp * 130 + hh * 65 + 65],
                             exp4[sp][:, qt * 256: qt * 256 + 128], start=True, stop=False)
                    mm = t.matmul(psAV4[sp][:, qt * 128:(qt + 1) * 128],
                                  V1[:, (qt + 1) * 1040 + hp * 130 + hh * 65: (qt + 1) * 1040 + hp * 130 + hh * 65 + 65],
                                  exp4[sp][:, qt * 256 + 128: qt * 256 + 256], start=False, stop=True)
                expb_rel[sp] = b.inc("lv", mm)
                pend.append((ci + 1, mk_epi1(b.n("lv"), sp)))
                pend.append((ci + 2, mk_epi2(hp, hh, sp)))
            pend.append((hp * 2 + 3, mk_stage3(hp, sl)))
        while pend:
            pend.pop(0)[1]()
    b.barrier()
    es.close()



def emit_p5b(P):
    nc, b, dr = P.nc, P.b, P.dr
    t, v, s, g, y = b.t, b.v, b.s, b.g, b.y
    es_op = ExitStack()
    wout = es_op.enter_context(nc.sbuf_tensor("wout", [128, 8 * 1024], BF16)).ap()
    bo_row = es_op.enter_context(nc.sbuf_tensor("bo_row", [1, 1024], BF16)).ap()
    xsA = [es_op.enter_context(nc.sbuf_tensor(f"xsA{i}", [128, 1024], F32)).ap() for i in range(2)]
    xsB = [es_op.enter_context(nc.sbuf_tensor(f"xsB{i}", [128, 1024], F32)).ap() for i in range(2)]
    mixbuf = [es_op.enter_context(nc.sbuf_tensor(f"mixbuf{i}", [128, 8 * 128], BF16)).ap() for i in range(2)]
    psO5 = [es_op.enter_context(nc.psum_tensor(f"psO5_{i}", [128, 512], F32)).ap() for i in range(2)]
    dma3(b, g, wout, dr["l1_wout"], 1024, "dop")
    b.inc("dop", g.dma_start(out=bo_row, in_=dr["l1_bout_row"]), 16)
    b.wait(t, "dop")
    psO_rel = [0, 0]
    mixbuf_rel = [0, 0]
    oad_rel = [0, 0]
    st2_rel = {0: ("dx2", 0), 1: ("dout", 0)}
    dmxl_cnt = {0: 0, 1: 0}

    def issue_mx(T):
        mbp = T % 2
        sl, tt = T // 4, T % 4
        b.wait(y, "pe", mixbuf_rel[mbp])
        b.inc(f"dmxl{mbp}", y.dma_start(
            out=mixbuf[mbp].rearrange("p (c e) -> p c e", e=128),
            in_=P.mx_dram[:, :, sl * 512 + tt * 128: sl * 512 + (tt + 1) * 128].rearrange("c p e -> p c e")), 16)
        dmxl_cnt[mbp] = b.n(f"dmxl{mbp}")

    issue_mx(0)
    for T in range(16):
        sl, tt = T // 4, T % 4
        offA = 128 * T
        offB = (2048 + 128 * T) % 4096
        mbp = T % 2
        if T + 1 < 16:
            issue_mx(T + 1)
        b.wait(y, "oadd1", oad_rel[mbp])
        b.wait(y, st2_rel[mbp][0], st2_rel[mbp][1])
        xsem = "dxr" if mbp == 0 else "dxn"
        b.inc(xsem, y.dma_start(out=xsA[mbp], in_=P.x1_dram[offA:offA + 128, :]), 16)
        b.inc(xsem, y.dma_start(out=xsB[mbp], in_=P.x1_dram[offB:offB + 128, :]), 16)
        dx_t = b.n(xsem)
        b.wait(t, f"dmxl{mbp}", dmxl_cnt[mbp])
        b.wait(v, xsem, dx_t)
        b.vdep()
        b.vstep(v.tensor_scalar(xsA[mbp], xsA[mbp], P.hsel[:, 0:1], None, ALU.mult))
        b.vdep()
        b.vstep(v.scalar_tensor_tensor(xsA[mbp], xsB[mbp], P.hsel[:, 1:2], xsA[mbp], ALU.mult, ALU.add))
        sel_vv = b.n("vv")
        for dh in range(2):
            op = dh
            b.wait(t, "oadd1", psO_rel[op])
            ps = psO5[op]
            for c in range(8):
                t.matmul(ps, mixbuf[mbp][:, c * 128:(c + 1) * 128],
                         wout[:, c * 1024 + dh * 512: c * 1024 + (dh + 1) * 512],
                         start=(c == 0), stop=False)
            mm = t.matmul(ps, P.ones_row, bo_row[0:1, dh * 512:(dh + 1) * 512],
                          start=False, stop=True)
            pe_cnt = b.inc("pe", mm)
            b.wait(v, "pe", pe_cnt)
            b.vdep(sel_vv)
            ad = v.tensor_tensor(xsA[mbp][:, dh * 512:(dh + 1) * 512],
                                 xsA[mbp][:, dh * 512:(dh + 1) * 512], ps, ALU.add)
            psO_rel[op] = b.inc("oadd1", ad)
        mixbuf_rel[mbp] = b.n("pe")
        oad_rel[mbp] = b.n("oadd1")
        b.wait(y, "oadd1", b.n("oadd1"))
        ssem = "dx2" if mbp == 0 else "dout"
        b.inc(ssem, y.dma_start(out=P.x2p_dram[T * 128:(T + 1) * 128, :], in_=xsA[mbp]), 16)
        st2_rel[mbp] = (ssem, b.n(ssem))
    b.barrier()
    es_op.close()



def emit_p6(P, xn1T, out):
    nc, b, dr = P.nc, P.b, P.dr
    t, v, s, g, y = b.t, b.v, b.s, b.g, b.y
    es = ExitStack()
    x_res = es.enter_context(nc.sbuf_tensor("x_res", [128, 16 * 1024], F32)).ap()
    w13 = [es.enter_context(nc.sbuf_tensor(f"w13a_{i}", [128, 8 * 1024], BF16)).ap() for i in range(2)]
    w2 = [es.enter_context(nc.sbuf_tensor(f"w2a_{i}", [128, 4 * 1024], BF16)).ap() for i in range(2)]
    act_sb = es.enter_context(nc.sbuf_tensor("act1_sb", [128, 4 * 512], BF16)).ap()
    s_s = es.enter_context(nc.sbuf_tensor("s1_s", [128, 512], F32)).ap()
    psA = es.enter_context(nc.psum_tensor("psA1", [128, 512], F32)).ap()
    psB2 = es.enter_context(nc.psum_tensor("psB21", [128, 512], F32)).ap()
    psM = [es.enter_context(nc.psum_tensor(f"psM1{i}", [128, 512], F32)).ap() for i in range(2)]

    for T in range(16):
        b.inc("dx2l", y.dma_start(out=x_res[:, T * 1024:(T + 1) * 1024],
                                  in_=P.x2p_dram[T * 128:(T + 1) * 128, :]), 16)
    b.wait(v, "dx2l", b.n("dx2l"))

    NEXP = 17

    def wsrc(e):
        if e < 16:
            return dr["l1_w13"][e], dr["l1_w2"][e]
        return dr["l1_w13s"], dr["l1_w2s"]
    w_rel = [0, 0]
    dma3(b, g, w13[0], wsrc(0)[0], 1024, "dm1_0")
    dma3(b, g, w2[0], wsrc(0)[1], 1024, "dm2_0")
    psA_rel = 0
    psB2_rel = 0
    psM_rel = [0, 0]
    act_war = 0
    ss_war = 0
    macc_reg = {}
    for e in range(NEXP):
        ep = e % 2
        if e + 1 < NEXP:
            b.wait(g, "pe", w_rel[1 - ep])
            dma3(b, g, w13[1 - ep], wsrc(e + 1)[0], 1024, f"dm1_{1 - ep}")
            dma3(b, g, w2[1 - ep], wsrc(e + 1)[1], 1024, f"dm2_{1 - ep}")
        b.wait(t, f"dm1_{ep}", b.n(f"dm1_{ep}"))
        b.wait(t, f"dm2_{ep}", b.n(f"dm2_{ep}"))
        for tc in range(4):
            for fp in range(4):
                b.wait(t, "sil1", psA_rel)
                for c in range(8):
                    mm = t.matmul(psA, w13[ep][:, c * 1024 + fp * 128: c * 1024 + (fp + 1) * 128],
                                  xn1T[:, c * 4096 + tc * 512: c * 4096 + (tc + 1) * 512],
                                  start=(c == 0), stop=(c == 7))
                peA = b.inc("pe", mm)
                b.wait(t, "mact1", psB2_rel)
                for c in range(8):
                    mm = t.matmul(psB2, w13[ep][:, c * 1024 + 512 + fp * 128: c * 1024 + 512 + (fp + 1) * 128],
                                  xn1T[:, c * 4096 + tc * 512: c * 4096 + (tc + 1) * 512],
                                  start=(c == 0), stop=(c == 7))
                peB = b.inc("pe", mm)
                w_rel[ep] = peB
                b.wait(s, "pe", peA)
                b.wait(s, "mact1", ss_war)
                psA_rel = b.inc("sil1", s.activation(s_s, psA, AF.Silu))
                b.wait(v, "pe", peB)
                b.wait(v, "sil1", psA_rel)
                b.wait(v, "pe", act_war)
                psB2_rel = ss_war = b.inc("mact1", v.tensor_tensor(
                    act_sb[:, fp * 512:(fp + 1) * 512], psB2, s_s, ALU.mult))
            b.wait(t, "mact1", b.n("mact1"))
            for tt in range(4):
                T = tc * 4 + tt
                for dh in range(2):
                    mp = dh
                    b.wait(t, "macc1", psM_rel[mp])
                    for ic in range(4):
                        mm = t.matmul(psM[mp], act_sb[:, ic * 512 + tt * 128: ic * 512 + (tt + 1) * 128],
                                      w2[ep][:, ic * 1024 + dh * 512: ic * 1024 + (dh + 1) * 512],
                                      start=(ic == 0), stop=(ic == 3))
                    peM = b.inc("pe", mm)
                    b.wait(v, "pe", peM)
                    b.wait(v, "macc1", macc_reg.get((T, dh), 0))
                    scl = P.comb[:, T * 16 + e: T * 16 + e + 1] if e < 16 else 1.0
                    psM_rel[mp] = macc_reg[(T, dh)] = b.inc("macc1", v.scalar_tensor_tensor(
                        x_res[:, T * 1024 + dh * 512: T * 1024 + (dh + 1) * 512],
                        psM[mp], scl,
                        x_res[:, T * 1024 + dh * 512: T * 1024 + (dh + 1) * 512],
                        ALU.mult, ALU.add))
            act_war = b.n("pe")
            w_rel[ep] = b.n("pe")
    b.barrier()
    for T in range(16):
        b.inc("dout", y.dma_start(out=out[T * 128:(T + 1) * 128, :],
                                  in_=x_res[:, T * 1024:(T + 1) * 1024]), 16)
    b.wait(y, "dout", b.n("dout"))
    es.close()



def build_program(stage="full"):
    nc = bass.Bass("TRN2", target_bir_lowering=False)
    b = Bld(nc)
    dr = {}
    for name, shape in ALL_INPUTS:
        dr[name] = nc.dram_tensor(name, list(shape), F32, kind="ExternalInput").ap()
    P = Prog(nc, b, dr)
    P.out = nc.dram_tensor("out", [2048, 1024], F32, kind="ExternalOutput").ap()
    if stage in ("l0x", "l0m"):
        P.dbg = nc.dram_tensor("dbg", [4096, 1024], F32, kind="ExternalOutput").ap()
    P.x1_dram = nc.dram_tensor("x1_scratch", [4096, 1024], F32, kind="Internal").ap()
    P.v_dram = nc.dram_tensor("v_scratch", [8, 128, 32, 130], BF16, kind="Internal").ap()
    P.q_dram = nc.dram_tensor("q_scratch", [8, 128, 8, 512], BF16, kind="Internal").ap()
    P.xnt_dram = nc.dram_tensor("xnt_scratch", [8, 128, 8, 512], BF16, kind="Internal").ap()
    P.x2p_dram = nc.dram_tensor("x2p_scratch", [2048, 1024], F32, kind="Internal").ap()
    P.mx_dram = nc.dram_tensor("mx_scratch", [8, 128, 2048], BF16, kind="Internal").ap()

    es_k = ExitStack()
    K_sb = es_k.enter_context(nc.sbuf_tensor("K_sb", [128, 8 * 4096], BF16)).ap()
    emit_p1(P, K_sb)
    emit_p2(P, K_sb)
    es_k.close()

    def dbg_dump():
        for T in range(32):
            b.inc("dout", b.y.dma_start(out=P.dbg[T * 128:(T + 1) * 128, :],
                                        in_=P.x1_dram[T * 128:(T + 1) * 128, :]), 16)
        for T in range(16):
            b.inc("dout", b.y.dma_start(out=P.out[T * 128:(T + 1) * 128, :],
                                        in_=P.x1_dram[T * 128:(T + 1) * 128, :]), 16)
        b.wait(b.y, "dout", b.n("dout"))

    if stage == "l0x":
        dbg_dump()
        return nc
    emit_p3(P)
    if stage == "l0m":
        dbg_dump()
        return nc

    es1 = ExitStack()
    xn1T = es1.enter_context(nc.sbuf_tensor("xn1T", [128, 8 * 4096], BF16)).ap()
    pooled = es1.enter_context(nc.sbuf_tensor("pooled", [128, 8 * 64], BF16)).ap()
    rk_sb = es1.enter_context(nc.sbuf_tensor("rk_sb", [128, 2 * 64], BF16)).ap()
    rv_sb = es1.enter_context(nc.sbuf_tensor("rv_sb", [64, 4 * 65], BF16)).ap()
    emit_p4(P, xn1T)
    emit_p4b(P, xn1T, pooled, rk_sb, rv_sb)
    emit_p5(P, xn1T, pooled, rk_sb, rv_sb)
    emit_p5b(P)
    emit_p6(P, xn1T, P.out)
    es1.close()
    return nc


def kernel(**inputs):
    inputs.get("L1_ffn_input_ids")
    nc = build_program("full")
    im = [host_inputs(inputs, core) for core in range(8)]
    res = run_bass_kernel_spmd(nc, im, core_ids=list(range(8)))
    return reassemble([res.results[c]["out"] for c in range(8)])


# version 23
# speedup vs baseline: 1.0195x; 1.0195x over previous
import math
import os
from contextlib import ExitStack

import numpy as np

import concourse.bass as bass
import concourse.mybir as mybir
from concourse.bass_utils import run_bass_kernel_spmd

F32 = mybir.dt.float32
BF16 = mybir.dt.bfloat16
ALU = mybir.AluOpType
AF = mybir.ActivationFunctionType
AXX = mybir.AxisListType.X

D = 1024
EPS = 1e-5
KT_FULL = [4, 8, 12, 16, 20, 24, 28, 32]



def _tile_cols(vec, n_tiles):
    return np.ascontiguousarray(vec.reshape(n_tiles, 128).T)


def host_inputs(inp, core):
    role = core % 2
    samp = core // 2
    f32 = np.float32
    d = {}
    x = np.asarray(inp["L0_attn_x"], dtype=f32)
    d["x_abs"] = np.ascontiguousarray(x[samp])

    d["l0_wqkv"] = np.ascontiguousarray(np.asarray(inp["L0_attn_W_attn"], f32))
    b_attn = np.asarray(inp["L0_attn_b_attn"], f32)
    d["l0_bqkv_cols"] = _tile_cols(b_attn, 24)
    d["l0_bv_row"] = np.ascontiguousarray(b_attn[2048:3072].reshape(1, 1024))
    d["l0_wproj"] = np.ascontiguousarray(np.asarray(inp["L0_attn_W_proj"], f32))
    d["l0_bproj_row"] = np.ascontiguousarray(np.asarray(inp["L0_attn_b_proj"], f32).reshape(1, 1024))

    tri = np.zeros((4, 128, 512), f32)
    for j in range(4):
        tri[j] = ((j * 128 + np.arange(128)[:, None]) <= np.arange(512)[None, :])
    d["l0_tri"] = np.ascontiguousarray(tri.transpose(1, 0, 2).reshape(128, 2048))

    d["l0_router_w"] = np.ascontiguousarray(np.asarray(inp["L0_ffn_router_weight"], f32))
    gup = np.asarray(inp["L0_ffn_gate_up_proj"], f32)
    perm = np.concatenate([np.arange(0, 2048, 2), np.arange(1, 2048, 2)])
    d["l0_w13"] = np.ascontiguousarray(gup[:, :, perm])
    gub = np.asarray(inp["L0_ffn_gate_up_proj_bias"], f32)[:, perm]
    d["l0_b13_cols"] = np.ascontiguousarray(gub.reshape(8, 16, 128).transpose(2, 0, 1).reshape(128, 128))
    d["l0_w2"] = np.ascontiguousarray(np.asarray(inp["L0_ffn_down_proj"], f32))
    d["l0_b2_rows"] = np.ascontiguousarray(np.asarray(inp["L0_ffn_down_proj_bias"], f32).reshape(1, 8192))

    d["l1_wqkv"] = np.ascontiguousarray(np.asarray(inp["L1_attn_qkv_w"], f32))
    b1 = np.asarray(inp["L1_attn_qkv_b"], f32)
    d["l1_bqkv_cols"] = _tile_cols(b1, 24)
    d["l1_bv_row"] = np.ascontiguousarray(b1[2048:3072].reshape(1, 1024))
    d["l1_rq_w"] = np.ascontiguousarray(np.asarray(inp["L1_attn_rq_w"], f32))
    d["l1_rq_b_cols"] = _tile_cols(np.asarray(inp["L1_attn_rq_b"], f32), 2)
    d["l1_rkv_w"] = np.ascontiguousarray(np.asarray(inp["L1_attn_rkv_w"], f32))
    rkvb = np.asarray(inp["L1_attn_rkv_b"], f32)
    d["l1_rk_b_cols"] = _tile_cols(rkvb[:256], 2)
    d["l1_rv_b_row"] = np.ascontiguousarray(rkvb[256:].reshape(1, 256))
    d["l1_rout_w"] = np.ascontiguousarray(np.asarray(inp["L1_attn_rout_w"], f32))
    d["l1_rout_b_cols"] = _tile_cols(np.asarray(inp["L1_attn_rout_b"], f32), 8)
    d["l1_wout"] = np.ascontiguousarray(np.asarray(inp["L1_attn_out_w"], f32))
    d["l1_bout_row"] = np.ascontiguousarray(np.asarray(inp["L1_attn_out_b"], f32).reshape(1, 1024))
    for nm, key in (("l1_gl_blk", "L1_attn_gl_w"), ("l1_gr_blk", "L1_attn_gr_w")):
        blk = np.zeros((128, 2), f32)
        wv = np.asarray(inp[key], f32)[:, 0]
        blk[0:64, 0] = wv
        blk[64:128, 1] = wv
        d[nm] = blk

    kk = np.arange(128)[:, None]
    qq = np.arange(128)[None, :]
    mprev = (qq <= kk).astype(f32)
    mdiag = (qq > kk).astype(f32)
    d["l1_mloc"] = np.ascontiguousarray(np.concatenate([mprev, mdiag], axis=1))
    m0 = np.zeros((4, 128, 256), f32)
    for s_ in range(4):
        m0[s_, :, 0:128] = 0.0 if (role == 0 and s_ == 0) else mprev
        m0[s_, :, 128:256] = mdiag
    d["l1_mloc0"] = np.ascontiguousarray(m0.transpose(1, 0, 2).reshape(128, 1024))
    m4 = np.zeros((4, 128, 1024), f32)
    for s_ in range(4):
        for qt_ in range(4):
            m4[s_, :, qt_ * 256:(qt_ + 1) * 256] = m0[s_] if qt_ == 0 else d["l1_mloc"]
    d["l1_mloc4"] = np.ascontiguousarray(m4.transpose(1, 0, 2).reshape(128, 4096))

    col_abs = np.empty(64, np.int64)
    col_abs[0:32] = role * 32 + np.arange(32)
    col_abs[32:64] = (1 - role) * 32 + np.arange(32)
    mlat = np.zeros((4, 64, 512), f32)
    for s_ in range(4):
        qchunk = role * 32 + (s_ * 512 + np.arange(512)) // 64
        mlat[s_] = (col_abs[:, None] < qchunk[None, :]).astype(f32)
    d["l1_mlat"] = np.ascontiguousarray(mlat.transpose(1, 0, 2).reshape(64, 2048))

    sel = np.zeros((128, 2), f32)
    sel[:, role] = 1.0
    d["half_sel"] = sel

    d["l1_gate_wT"] = np.ascontiguousarray(np.asarray(inp["L1_ffn_gate_w"], f32).T)
    d["l1_gate_b_row"] = np.ascontiguousarray(np.asarray(inp["L1_ffn_gate_b"], f32).reshape(1, 16))
    d["l1_w13"] = np.ascontiguousarray(np.asarray(inp["L1_ffn_w13"], f32).transpose(0, 2, 1))
    d["l1_w2"] = np.ascontiguousarray(np.asarray(inp["L1_ffn_w2"], f32).transpose(0, 2, 1))
    d["l1_w13s"] = np.ascontiguousarray(np.asarray(inp["L1_ffn_w13_s"], f32).T)
    d["l1_w2s"] = np.ascontiguousarray(np.asarray(inp["L1_ffn_w2_s"], f32).T)

    d["identity"] = np.eye(128, dtype=f32)
    d["row_1702"] = np.full((1, 128), 1.702, f32)
    return d


def reassemble(outs):
    full = np.empty((4, 4096, 1024), np.float32)
    for core in range(8):
        role, samp = core % 2, core // 2
        full[samp, role * 2048:(role + 1) * 2048] = outs[core]
    return full


ALL_INPUTS = [
    ("x_abs", (4096, 1024)),
    ("l0_wqkv", (1024, 3072)), ("l0_bqkv_cols", (128, 24)), ("l0_bv_row", (1, 1024)),
    ("l0_wproj", (1024, 1024)), ("l0_bproj_row", (1, 1024)),
    ("l0_tri", (128, 2048)),
    ("l0_router_w", (1024, 8)),
    ("l0_w13", (8, 1024, 2048)), ("l0_b13_cols", (128, 128)),
    ("l0_w2", (8, 1024, 1024)), ("l0_b2_rows", (1, 8192)),
    ("l1_wqkv", (1024, 3072)), ("l1_bqkv_cols", (128, 24)), ("l1_bv_row", (1, 1024)),
    ("l1_rq_w", (1024, 256)), ("l1_rq_b_cols", (128, 2)),
    ("l1_rkv_w", (1024, 512)), ("l1_rk_b_cols", (128, 2)), ("l1_rv_b_row", (1, 256)),
    ("l1_rout_w", (256, 1024)), ("l1_rout_b_cols", (128, 8)),
    ("l1_wout", (1024, 1024)), ("l1_bout_row", (1, 1024)),
    ("l1_gl_blk", (128, 2)), ("l1_gr_blk", (128, 2)),
    ("l1_mloc", (128, 256)), ("l1_mloc0", (128, 1024)), ("l1_mlat", (64, 2048)),
    ("l1_mloc4", (128, 4096)),
    ("half_sel", (128, 2)),
    ("l1_gate_wT", (1024, 16)), ("l1_gate_b_row", (1, 16)),
    ("l1_w13", (16, 1024, 1024)), ("l1_w2", (16, 512, 1024)),
    ("l1_w13s", (1024, 1024)), ("l1_w2s", (512, 1024)),
    ("identity", (128, 128)), ("row_1702", (1, 128)),
]



class Bld:
    def __init__(self, nc):
        self.nc = nc
        self.t = nc.tensor
        self.v = nc.vector
        self.s = nc.scalar
        self.g = nc.gpsimd
        self.y = nc.sync
        self._sems = {}
        self._waited = {}

    def sem(self, name):
        if name not in self._sems:
            self._sems[name] = [self.nc.alloc_semaphore(name), 0]
        return self._sems[name]

    def inc(self, name, instr, k=1):
        sm = self.sem(name)
        instr.then_inc(sm[0], k)
        sm[1] += k
        return sm[1]

    def n(self, name):
        return self.sem(name)[1]

    def wait(self, eng, name, val=None):
        sm = self.sem(name)
        if val is None:
            val = sm[1]
        if val <= 0:
            return
        key = (id(eng), name)
        if self._waited.get(key, -1) >= val:
            return
        if val > sm[1]:
            raise RuntimeError(f"wait {name}>={val} can never fire (max {sm[1]})")
        eng.wait_ge(sm[0], val)
        self._waited[key] = val

    def vstep(self, instr):
        self.inc("vv", instr)
        return instr

    def vdep(self, val=None):
        self.wait(self.v, "vv", val)

    def sstep(self, instr):
        self.inc("ss", instr)
        return instr

    def sdep(self, val=None):
        self.wait(self.s, "ss", val)

    def barrier(self):
        engines = [self.t, self.v, self.s, self.g, self.y]
        for name, (h, cnt) in list(self._sems.items()):
            if cnt == 0:
                continue
            for e in engines:
                self.wait(e, name, cnt)


def dma3(b, eng, dst2d, src2d, inner, sem, p=128):
    in3 = src2d.rearrange("(c p) f -> p c f", p=p)
    out3 = dst2d.rearrange("p (c f) -> p c f", f=inner)
    return b.inc(sem, eng.dma_start(out=out3, in_=in3), 16)


class Prog:
    def __init__(self, nc, b, dr):
        self.nc = nc
        self.b = b
        self.dr = dr
        alloc = nc.alloc_sbuf_tensor
        self.ident = alloc("ident", [128, 128], F32).ap()
        self.ones_row = alloc("ones_row", [1, 128], BF16).ap()
        self.ones_row_f = alloc("ones_row_f", [65, 128], F32).ap()
        self.row1702 = alloc("row1702", [1, 128], BF16).ap()
        self.wsc = alloc("wsc", [128, 32 * 8], F32).ap()
        self.comb = alloc("comb", [128, 16 * 16], F32).ap()
        self.stat = alloc("stat", [128, 16], F32).ap()
        self.hsel = alloc("hsel", [128, 2], F32).ap()
        g, v = b.g, b.v
        b.inc("dmisc", g.dma_start(out=self.ident, in_=dr["identity"]), 16)
        b.inc("dmisc", g.dma_start(out=self.row1702, in_=dr["row_1702"]), 16)
        b.inc("dmisc", g.dma_start(out=self.hsel, in_=dr["half_sel"]), 16)
        v.memset(self.ones_row, 1.0)
        b.inc("cmisc", v.memset(self.ones_row_f, 1.0))

    def emit_ln_stats(self, xtile, scr, dep_sems):
        b, v, s = self.b, self.b.v, self.b.s
        st = self.stat
        for eng in (v, s):
            for nm, val in dep_sems:
                b.wait(eng, nm, val)
        b.wait(v, "ln_out", b.n("ln_out"))
        scr_rel = getattr(self, "_scr_rel", None)
        if scr_rel is None:
            scr_rel = self._scr_rel = [0]
        for tt in range(4):
            xt = xtile[:, tt * 1024:(tt + 1) * 1024]
            b.vstep(v.reduce_sum(st[:, tt:tt + 1], xt, AXX))
            b.wait(s, "vv", scr_rel[0])
            b.inc("ln_sq", s.activation(scr, xt, AF.Square))
            b.wait(v, "ln_sq")
            b.vstep(v.reduce_sum(st[:, 4 + tt:5 + tt], scr, AXX))
            scr_rel[0] = b.n("vv")
        b.vdep()
        b.vstep(v.tensor_scalar(st[:, 0:4], st[:, 0:4], 1.0 / D, None, ALU.mult))
        b.vstep(v.tensor_scalar(st[:, 4:8], st[:, 4:8], 1.0 / D, None, ALU.mult))
        b.vdep()
        b.vstep(v.tensor_tensor(st[:, 8:12], st[:, 0:4], st[:, 0:4], ALU.mult))
        b.vdep()
        b.vstep(v.tensor_sub(st[:, 4:8], st[:, 4:8], st[:, 8:12]))
        b.vdep()
        b.vstep(v.tensor_scalar(st[:, 4:8], st[:, 4:8], EPS, None, ALU.add))
        b.vdep()
        b.vstep(v.reciprocal(st[:, 8:12], st[:, 4:8]))
        b.wait(s, "vv", b.n("vv"))
        b.sstep(s.activation(st[:, 12:16], st[:, 8:12], AF.Sqrt))
        b.wait(v, "ss", b.n("ss"))
        b.vdep()
        b.vstep(v.tensor_tensor(st[:, 8:12], st[:, 0:4], st[:, 12:16], ALU.mult))
        b.vdep()
        b.vstep(v.tensor_scalar(st[:, 8:12], st[:, 8:12], -1.0, None, ALU.mult))
        b.wait(s, "vv", b.n("vv"))

    def emit_ln_tile(self, xtile, tt, xn1tile):
        b, s = self.b, self.b.s
        st = self.stat
        b.sdep()
        return b.inc("ln_out", s.activation(
            xn1tile, xtile[:, tt * 1024:(tt + 1) * 1024], AF.Identity,
            bias=st[:, 8 + tt:9 + tt], scale=st[:, 12 + tt:13 + tt]))



def emit_p1(P, K_sb):
    nc, b, dr = P.nc, P.b, P.dr
    t, v, s, g, y = b.t, b.v, b.s, b.g, b.y
    es = ExitStack()
    xstage = [es.enter_context(nc.sbuf_tensor(f"xstage{i}", [128, 4096], F32)).ap() for i in range(2)]
    xnf = [es.enter_context(nc.sbuf_tensor(f"xnf{i}", [128, 1024], F32)).ap() for i in range(2)]
    scr = es.enter_context(nc.sbuf_tensor("scr", [128, 1024], F32)).ap()
    xnTt = es.enter_context(nc.sbuf_tensor("xnTt", [128, 8 * 512], BF16)).ap()
    xnTf = es.enter_context(nc.sbuf_tensor("xnTf", [128, 1024], F32)).ap()
    wqkv = es.enter_context(nc.sbuf_tensor("wqkv", [128, 8 * 3072], BF16)).ap()
    bq_cols = es.enter_context(nc.sbuf_tensor("bq_cols", [128, 24], F32)).ap()
    bv_row = es.enter_context(nc.sbuf_tensor("bv_row", [1, 1024], BF16)).ap()
    rw_sb = es.enter_context(nc.sbuf_tensor("rw_sb", [128, 64], F32)).ap()
    top_sc = es.enter_context(nc.sbuf_tensor("top_sc", [128, 32], F32)).ap()
    vstage = [es.enter_context(nc.sbuf_tensor(f"vstage{i}", [128, 4 * 1040], BF16)).ap() for i in range(2)]
    qstage = [es.enter_context(nc.sbuf_tensor(f"qstage{i}", [128, 512], BF16)).ap() for i in range(2)]
    psT = [es.enter_context(nc.psum_tensor(f"psT{i}", [128, 128], F32)).ap() for i in range(2)]
    psK = [es.enter_context(nc.psum_tensor(f"psK{i}", [128, 512], F32)).ap() for i in range(2)]
    psV = [es.enter_context(nc.psum_tensor(f"psV{i}", [128, 512], F32)).ap() for i in range(2)]
    psR = es.enter_context(nc.psum_tensor("psR", [128, 8], F32)).ap()

    dma3(b, g, wqkv, dr["l0_wqkv"], 3072, "dw")
    b.inc("dw", g.dma_start(out=bq_cols, in_=dr["l0_bqkv_cols"]), 16)
    b.inc("dw", g.dma_start(out=bv_row, in_=dr["l0_bv_row"]), 16)
    dma3(b, g, rw_sb, dr["l0_router_w"], 8, "dw")
    for i in range(2):
        v.memset(vstage[i], 1.0)
    b.wait(t, "dw")
    b.wait(t, "dmisc")
    b.wait(t, "cmisc")
    b.wait(s, "dw")

    ln_done = 0
    ln_rel = {0: 0, 1: 0}
    dx_cnt = {0: 0, 1: 0}
    xnf_rel = {0: 0, 1: 0}
    psT_rel = [0, 0]
    psK_rel = [("kev", 0), ("kev", 0)]
    psV_rel = [0, 0]
    vst_rel = {0: 0, 1: 0}
    qst_rel = {0: 0, 1: 0}
    psR_rel = None

    def issue_x(j):
        par = j % 2
        sm = "dx" if par == 0 else "dxa"
        b.wait(y, "ln_out", ln_rel[par])
        dma3(b, y, xstage[par], dr["x_abs"][j * 512:(j + 1) * 512, :], 1024, sm)
        dx_cnt[par] = (sm, b.n(sm))

    issue_x(0)
    for j in range(8):
        par = j % 2
        if j + 1 < 8:
            issue_x(j + 1)
        b.wait(v, "dxn", b.n("dxn"))
        P.emit_ln_stats(xstage[par], scr, [dx_cnt[par]])
        for tt in range(4):
            T = j * 4 + tt
            fp_ = tt % 2
            b.wait(s, "pe", xnf_rel[fp_])
            ln_done = P.emit_ln_tile(xstage[par], tt, xnf[fp_])
            if tt == 3:
                ln_rel[par] = ln_done
            for c in range(8):
                tp = c % 2
                b.wait(t, "xe", psT_rel[tp])
                b.wait(t, "ln_out", ln_done)
                pe_cnt = b.inc("pe", t.transpose(psT[tp], xnf[fp_][:, c * 128:(c + 1) * 128], P.ident))
                b.wait(v, "pe", pe_cnt)
                b.inc("xe", v.tensor_copy(xnTt[:, c * 512 + tt * 128: c * 512 + (tt + 1) * 128], psT[tp]))
                psT_rel[tp] = b.inc("xe", v.tensor_copy(xnTf[:, c * 128:(c + 1) * 128], psT[tp]))
            xnf_rel[fp_] = b.n("pe")
            b.wait(t, "xe", b.n("xe"))
            if psR_rel is not None:
                b.wait(t, "vv", psR_rel[0])
                b.wait(t, "ss", psR_rel[1])
            for c in range(8):
                mm = t.matmul(psR, xnTf[:, c * 128:(c + 1) * 128], rw_sb[:, c * 8:(c + 1) * 8],
                              start=(c == 0), stop=(c == 7))
            rmm_cnt = b.inc("rmm", mm)
            st = top_sc
            b.wait(v, "rmm", rmm_cnt)
            b.vdep()
            b.vstep(v.reduce_max(st[:, 0:1], psR, AXX))
            b.vdep()
            b.vstep(v.tensor_scalar(st[:, 8:16], psR, st[:, 0:1], None, ALU.is_equal))
            b.vdep()
            b.vstep(v.scalar_tensor_tensor(st[:, 16:24], st[:, 8:16], -1e30, psR, ALU.mult, ALU.add))
            b.vdep()
            b.vstep(v.reduce_max(st[:, 1:2], st[:, 16:24], AXX))
            b.vstep(v.tensor_scalar(st[:, 2:3], st[:, 0:1], -1.0, None, ALU.mult))
            b.wait(s, "vv", b.n("vv"))
            b.sstep(s.activation(st[:, 8:16], psR, AF.Exp, bias=st[:, 2:3]))
            rex_ss = b.n("ss")
            b.wait(v, "ss", rex_ss)
            b.vdep()
            b.vstep(v.tensor_scalar(st[:, 16:24], psR, st[:, 1:2], None, ALU.is_ge))
            rtop_vv = b.n("vv")
            b.vdep()
            b.vstep(v.tensor_tensor(st[:, 8:16], st[:, 8:16], st[:, 16:24], ALU.mult))
            b.vdep()
            b.vstep(v.reduce_sum(st[:, 3:4], st[:, 8:16], AXX))
            b.vdep()
            b.vstep(v.reciprocal(st[:, 4:5], st[:, 3:4]))
            b.vdep()
            b.vstep(v.tensor_scalar(P.wsc[:, T * 8:(T + 1) * 8], st[:, 8:16],
                                    st[:, 4:5], 1.0 / 1.702, ALU.mult, ALU.mult))
            psR_rel = (rtop_vv, rex_ss)
        b.wait(y, "xe", b.n("xe"))
        b.inc("dxn", y.dma_start(
            out=P.xnt_dram[j],
            in_=xnTt.rearrange("p (c f) -> p c f", f=512)), 16)
        for hp in range(8):
            kp = hp % 2
            b.wait(t, psK_rel[kp][0], psK_rel[kp][1])
            for c in range(8):
                mm = t.matmul(psK[kp], wqkv[:, c * 3072 + 1024 + hp * 128: c * 3072 + 1024 + (hp + 1) * 128],
                              xnTt[:, c * 512:(c + 1) * 512], start=(c == 0), stop=(c == 7))
            pe_cnt = b.inc("pe", mm)
            b.wait(v, "pe", pe_cnt)
            psK_rel[kp] = ("kev", b.inc("kev", v.tensor_scalar(
                K_sb[:, hp * 4096 + j * 512: hp * 4096 + (j + 1) * 512],
                psK[kp], bq_cols[:, 8 + hp:9 + hp], None, ALU.add)))
        b.wait(v, f"dv{par}", vst_rel[par])
        for kk in range(4):
            for vh in range(2):
                vp = (kk * 2 + vh) % 2
                b.wait(t, "vev", psV_rel[vp])
                for c in range(8):
                    t.matmul(psV[vp], xnTt[:, c * 512 + kk * 128: c * 512 + (kk + 1) * 128],
                             wqkv[:, c * 3072 + 2048 + vh * 512: c * 3072 + 2048 + (vh + 1) * 512],
                             start=(c == 0), stop=False)
                mm = t.matmul(psV[vp], P.ones_row, bv_row[0:1, vh * 512:(vh + 1) * 512],
                              start=False, stop=True)
                pe_cnt = b.inc("pe", mm)
                b.wait(v, "pe", pe_cnt)
                src = psV[vp].rearrange("p (h d) -> p h d", d=64)
                for gsub in range(4):
                    hg = vh * 4 + gsub
                    dst = vstage[par][:, kk * 1040 + hg * 130: kk * 1040 + (hg + 1) * 130]
                    dst = dst.rearrange("p (h e) -> p h e", e=65)[:, :, 0:64]
                    cnt = b.inc("vev", v.tensor_copy(dst, src[:, gsub * 2:gsub * 2 + 2, :]))
                psV_rel[vp] = cnt
        b.wait(y, "vev", b.n("vev"))
        for hg in range(8):
            src = vstage[par].rearrange("p (k g e) -> p k g e", g=8, e=130)[:, :, hg, :]
            b.inc(f"dv{par}", y.dma_start(out=P.v_dram[hg, :, j * 4:(j + 1) * 4, :], in_=src), 16)
        vst_rel[par] = b.n(f"dv{par}")
        for hp in range(8):
            qp = hp % 2
            b.wait(t, psK_rel[qp][0], psK_rel[qp][1])
            for c in range(8):
                mm = t.matmul(psK[qp], wqkv[:, c * 3072 + hp * 128: c * 3072 + (hp + 1) * 128],
                              xnTt[:, c * 512:(c + 1) * 512], start=(c == 0), stop=(c == 7))
            pe_cnt = b.inc("pe", mm)
            b.wait(v, "pe", pe_cnt)
            b.wait(v, f"dq{qp}", qst_rel[qp])
            cnt = b.inc("qev", v.tensor_scalar(qstage[qp], psK[qp],
                                               bq_cols[:, hp:hp + 1], None, ALU.add))
            psK_rel[qp] = ("qev", cnt)
            b.wait(y, "qev", b.n("qev"))
            b.inc(f"dq{qp}", y.dma_start(out=P.q_dram[hp, :, j, :], in_=qstage[qp]), 16)
            qst_rel[qp] = b.n(f"dq{qp}")
    es.close()
    b.barrier()



def emit_p2(P, K_sb):
    nc, b, dr = P.nc, P.b, P.dr
    t, v, s, g, y = b.t, b.v, b.s, b.g, b.y
    es = ExitStack()
    tri = es.enter_context(nc.sbuf_tensor("tri", [128, 2048], BF16)).ap()
    wproj = es.enter_context(nc.sbuf_tensor("wproj", [128, 8 * 1024], BF16)).ap()
    bp_row = es.enter_context(nc.sbuf_tensor("bp_row", [1, 1024], BF16)).ap()
    y_sb = es.enter_context(nc.sbuf_tensor("y_sb", [128, 8 * 512], BF16)).ap()
    exp2 = [es.enter_context(nc.sbuf_tensor(f"exp2_{i}", [128, 1024], BF16)).ap() for i in range(2)]
    vbuf = [es.enter_context(nc.sbuf_tensor(f"vbuf{i}", [128, 32 * 130], BF16)).ap() for i in range(2)]
    qbuf = [es.enter_context(nc.sbuf_tensor(f"qbuf{i}", [128, 512], BF16)).ap() for i in range(2)]
    bc_sb = es.enter_context(nc.sbuf_tensor("bc_sb", [64, 512], BF16)).ap()
    srow = [es.enter_context(nc.sbuf_tensor(f"srowp{i}", [1, 1024], F32)).ap() for i in range(2)]
    xsm = [es.enter_context(nc.sbuf_tensor(f"xsm{i}", [128, 1024], F32)).ap() for i in range(2)]
    psSd = [es.enter_context(nc.psum_tensor(f"psSd{i}", [128, 1024], F32)).ap() for i in range(2)]
    psAV = [es.enter_context(nc.psum_tensor(f"psAVd{i}", [65, 512], F32)).ap() for i in range(2)]
    psOB = es.enter_context(nc.psum_tensor("psOB", [128, 512], F32)).ap()
    psB = psOB[0:64, :]
    b.inc("dw3", g.dma_start(out=tri, in_=dr["l0_tri"]), 16)
    dma3(b, g, wproj, dr["l0_wproj"], 1024, "dw3")
    b.inc("dw3", g.dma_start(out=bp_row, in_=dr["l0_bproj_row"]), 16)
    b.wait(t, "dw3")
    b.wait(v, "dw3")

    exp_rel = {0: ("act", 0), 1: ("act", 0)}
    expbuf_rel = {0: 0, 1: 0}
    vbuf_rel = {0: 0, 1: 0}
    qbuf_rel = {0: 0, 1: 0}
    dvl_cnt = {0: 0, 1: 0}
    dql_cnt = {0: 0, 1: 0}
    psO_rel4 = {0: 0, 1: 0, 2: 0, 3: 0}
    oad_rel2 = {0: 0, 1: 0}
    dx1_rel = {0: 0, 1: 0}
    P._p2_psO_rel = 0
    P._p2_pending = []
    P._p2_avwar = {0: 0, 1: 0}
    P._p2_state = {"srow_war": 0, "psb_war": 0, "bce_war": 0, "vv_rec": 0}

    def make_epi1(av_all, par):
        def epi():
            st = P._p2_state
            pav = psAV[par]
            b.wait(v, "av", av_all)
            b.wait(v, "pe", st["srow_war"])
            b.vstep(v.tensor_scalar(srow[par][0:1, 0:512], pav[64:65, :], 1e-30, None, ALU.max))
            b.vdep()
            b.vstep(v.reciprocal(srow[par][0:1, 512:1024], srow[par][0:1, 0:512]))
            st["vv_rec"] = b.n("vv")
        return epi

    def make_epi2(hh, hg, par):
        def epi():
            st = P._p2_state
            pav = psAV[par]
            b.wait(t, "vv", st["vv_rec"])
            b.wait(t, "bce", st["psb_war"])
            b.wait(t, "oadd", P._p2_psO_rel)
            mmb = t.matmul(psB, P.ones_row_f[0:1, 0:64], srow[par][0:1, 512:1024], start=True, stop=True)
            st["srow_war"] = peb = b.inc("pe", mmb)
            b.wait(s, "pe", peb)
            b.wait(s, "ynorm", st["bce_war"])
            st["psb_war"] = b.inc("bce", s.copy(bc_sb, psB))
            b.wait(v, "bce", st["psb_war"])
            yn = v.tensor_tensor(y_sb[64 * hh:64 * hh + 64, hg * 512:(hg + 1) * 512],
                                 pav[0:64, :], bc_sb, ALU.mult)
            st["bce_war"] = b.inc("ynorm", yn)
            P._p2_avwar[par] = st["bce_war"]
        return epi

    for sl in range(8):
        KT = KT_FULL[sl]
        plist = [(hg, hh, kt2) for hg in range(8) for hh in range(2)
                 for kt2 in range(0, KT, 2)]

        def issue_hg(isl, hg):
            gp = hg % 2
            ktl = KT_FULL[isl]
            b.wait(y, "av", vbuf_rel[gp])
            b.inc(f"dvl{gp}", y.dma_start(
                out=vbuf[gp][:, 0:ktl * 130].rearrange("p (k e) -> p k e", e=130),
                in_=P.v_dram[hg, :, 0:ktl, :]), 16)
            dvl_cnt[gp] = b.n(f"dvl{gp}")
            b.wait(y, "pe", qbuf_rel[gp])
            b.inc(f"dql{gp}", y.dma_start(out=qbuf[gp], in_=P.q_dram[hg, :, isl, :]), 16)
            dql_cnt[gp] = b.n(f"dql{gp}")

        if sl == 0:
            issue_hg(0, 0)
            issue_hg(0, 1)

        def emit_S(j):
            hg, hh, kt2 = plist[j]
            gp = hg % 2
            sp = j % 2
            b.wait(t, exp_rel[sp][0], exp_rel[sp][1])
            if hh == 0 and kt2 == 0:
                b.wait(t, f"dql{gp}", dql_cnt[gp])
            mm = None
            for u in (0, 1):
                kt = kt2 + u
                mm = t.matmul(psSd[sp][:, u * 512:(u + 1) * 512],
                              K_sb[64 * hh:64 * hh + 64, hg * 4096 + kt * 128: hg * 4096 + (kt + 1) * 128],
                              qbuf[gp][64 * hh:64 * hh + 64, :],
                              start=True, stop=True)
            qbuf_rel[gp] = b.inc("pe", mm)
            return b.n("pe")

        pe_S = {0: emit_S(0)}
        for j, (hg, hh, kt2) in enumerate(plist):
            gp = hg % 2
            sp = j % 2
            b.wait(s, "pe", pe_S[j])
            b.wait(s, "av", expbuf_rel[sp])
            e_in = s.activation(exp2[sp], psSd[sp], AF.Exp, scale=0.125)
            act_cnt = b.inc("act", e_in)
            exp_rel[sp] = ("act", act_cnt)
            wait_for_av = ("act", act_cnt)
            for u in (0, 1):
                jj = kt2 + u - (KT - 4)
                if jj >= 0:
                    b.wait(v, "act", act_cnt)
                    b.wait(v, "av", expbuf_rel[sp])
                    mk = v.tensor_tensor(exp2[sp][:, u * 512:(u + 1) * 512],
                                         exp2[sp][:, u * 512:(u + 1) * 512],
                                         tri[:, jj * 512:(jj + 1) * 512], ALU.mult)
                    dve_cnt = b.inc("dve", mk)
                    exp_rel[sp] = ("dve", dve_cnt)
                    wait_for_av = ("dve", dve_cnt)
            if j + 1 < len(plist):
                pe_S[j + 1] = emit_S(j + 1)
            while P._p2_pending and P._p2_pending[0][0] <= j:
                P._p2_pending.pop(0)[1]()
            b.wait(t, wait_for_av[0], wait_for_av[1])
            if kt2 == 0:
                b.wait(t, f"dvl{gp}", dvl_cnt[gp])
                b.wait(t, "ynorm", P._p2_avwar[hh])
            mm = None
            for u in (0, 1):
                kt = kt2 + u
                mm = t.matmul(psAV[hh], vbuf[gp][:, kt * 130 + hh * 65: kt * 130 + hh * 65 + 65],
                              exp2[sp][:, u * 512:(u + 1) * 512],
                              start=(kt == 0), stop=(kt == KT - 1))
            expbuf_rel[sp] = b.inc("av", mm)
            vbuf_rel[gp] = b.n("av")
            if kt2 == KT - 2:
                P._p2_pending.append((j + 1, make_epi1(b.n("av"), hh)))
                P._p2_pending.append((j + 3, make_epi2(hh, hg, hh)))
                if hh == 1:
                    if hg + 2 < 8:
                        issue_hg(sl, hg + 2)
                    elif sl + 1 < 8:
                        issue_hg(sl + 1, hg - 6)
        while P._p2_pending:
            P._p2_pending.pop(0)[1]()
        b.wait(t, "ynorm", b.n("ynorm"))
        for sp_ in (0, 1):
            b.wait(t, exp_rel[sp_][0], exp_rel[sp_][1])
        for tt in range(4):
            tp_ = tt % 2
            dsm = "dx" if tp_ == 0 else "dxa"
            ssm = "dx1" if tp_ == 0 else "dxb"
            b.wait(y, "oadd", oad_rel2[tp_])
            b.wait(y, ssm, dx1_rel[tp_])
            b.inc(dsm, y.dma_start(
                out=xsm[tp_], in_=dr["x_abs"][(sl * 4 + tt) * 128:(sl * 4 + tt + 1) * 128, :]), 16)
            dx_t = b.n(dsm)
            for dh in range(2):
                psO_ = psSd[tp_][:, dh * 512:(dh + 1) * 512]
                b.wait(t, "oadd", psO_rel4[tp_ * 2 + dh])
                for c in range(8):
                    t.matmul(psO_, y_sb[:, c * 512 + tt * 128: c * 512 + (tt + 1) * 128],
                             wproj[:, c * 1024 + dh * 512: c * 1024 + (dh + 1) * 512],
                             start=(c == 0), stop=False)
                mm = t.matmul(psO_, P.ones_row, bp_row[0:1, dh * 512:(dh + 1) * 512],
                              start=False, stop=True)
                pe_cnt = b.inc("pe", mm)
                b.wait(v, "pe", pe_cnt)
                b.wait(v, dsm, dx_t)
                ad = v.tensor_tensor(xsm[tp_][:, dh * 512:(dh + 1) * 512],
                                     xsm[tp_][:, dh * 512:(dh + 1) * 512], psO_, ALU.add)
                psO_rel4[tp_ * 2 + dh] = b.inc("oadd", ad)
            oad_rel2[tp_] = b.n("oadd")
            b.wait(y, "oadd", oad_rel2[tp_])
            b.inc(ssm, y.dma_start(
                out=P.x1_dram[(sl * 4 + tt) * 128:(sl * 4 + tt + 1) * 128, :], in_=xsm[tp_]), 16)
            dx1_rel[tp_] = b.n(ssm)
        exp_rel[0] = ("oadd", psO_rel4[1])
        exp_rel[1] = ("oadd", psO_rel4[3])
    es.close()
    b.barrier()



def emit_p3(P):
    nc, b, dr = P.nc, P.b, P.dr
    t, v, s, g, y = b.t, b.v, b.s, b.g, b.y
    es = ExitStack()
    xnb = [es.enter_context(nc.sbuf_tensor(f"xnb{i}", [128, 8 * 512], BF16)).ap() for i in range(2)]
    w13 = [es.enter_context(nc.sbuf_tensor(f"w13_{i}", [128, 8 * 2048], BF16)).ap() for i in range(2)]
    w2 = [es.enter_context(nc.sbuf_tensor(f"w2_sb{i}", [128, 8 * 1024], BF16)).ap() for i in range(2)]
    act_sb = es.enter_context(nc.sbuf_tensor("act_sb", [128, 8 * 512], BF16)).ap()
    b13 = es.enter_context(nc.sbuf_tensor("b13", [128, 128], F32)).ap()
    b13u = es.enter_context(nc.sbuf_tensor("b13u", [128, 64], F32)).ap()
    b2row = [es.enter_context(nc.sbuf_tensor(f"b2row{i}", [1, 1024], BF16)).ap() for i in range(2)]
    g_s = es.enter_context(nc.sbuf_tensor("g_s", [128, 512], F32)).ap()
    s_s = es.enter_context(nc.sbuf_tensor("s_s", [128, 512], F32)).ap()
    u_s = es.enter_context(nc.sbuf_tensor("u_s", [128, 512], F32)).ap()
    contrib = [es.enter_context(nc.sbuf_tensor(f"contrib{i}", [128, 1024], F32)).ap() for i in range(4)]
    psA = [es.enter_context(nc.psum_tensor(f"psA_{i}", [128, 512], F32)).ap() for i in range(2)]
    psB2 = [es.enter_context(nc.psum_tensor(f"psB2_{i}", [128, 512], F32)).ap() for i in range(2)]
    psM = [es.enter_context(nc.psum_tensor(f"psM{i}", [128, 512], F32)).ap() for i in range(4)]

    b.inc("dmo", g.dma_start(out=b13, in_=dr["l0_b13_cols"]), 16)
    b.wait(v, "dmo", 16)
    b.vstep(v.tensor_scalar(
        b13u.rearrange("p (e f) -> p e f", f=8),
        b13.rearrange("p (e f) -> p e f", f=16)[:, :, 8:16],
        1.0, None, ALU.add))
    b13u_vv = b.n("vv")

    w13_rel = [0, 0]
    dma3(b, g, w13[0], dr["l0_w13"][0], 2048, "dm13_0")
    dma3(b, g, w2[0], dr["l0_w2"][0], 1024, "dmo2_0")
    b.inc("dme2_0", g.dma_start(out=b2row[0], in_=dr["l0_b2_rows"][0:1, 0:1024]), 16)
    psA_rel = [0, 0]
    psB2_rel = [0, 0]
    w2_rel = [0, 0]
    b2_rel = [0, 0]
    psM_rel = [0, 0, 0, 0]
    act_war = 0
    gs_war = 0
    ss_war = 0
    acc_reg = {}
    xnb_rel = [0, 0]
    cb_rel = [0, 0, 0, 0]
    dxnl_cnt = {0: 0, 1: 0}

    def issue_xnb(xp, tc):
        b.wait(y, "pe", xnb_rel[xp])
        b.inc(f"dxnl{xp}", y.dma_start(
            out=xnb[xp].rearrange("p (c f) -> p c f", f=512),
            in_=P.xnt_dram[tc]), 16)
        dxnl_cnt[xp] = b.n(f"dxnl{xp}")

    for e in range(8):
        ep = e % 2
        if e + 1 < 8:
            b.wait(g, "pe", w13_rel[1 - ep])
            dma3(b, g, w13[1 - ep], dr["l0_w13"][e + 1], 2048, f"dm13_{1 - ep}")
            b.wait(g, "pe", w2_rel[1 - ep])
            dma3(b, g, w2[1 - ep], dr["l0_w2"][e + 1], 1024, f"dmo2_{1 - ep}")
            b.wait(g, "pe", b2_rel[1 - ep])
            b.inc(f"dme2_{1 - ep}", g.dma_start(
                out=b2row[1 - ep], in_=dr["l0_b2_rows"][0:1, (e + 1) * 1024:(e + 2) * 1024]), 16)
        b.wait(t, f"dm13_{ep}", b.n(f"dm13_{ep}"))
        b.wait(t, f"dmo2_{ep}", b.n(f"dmo2_{ep}"))
        b.wait(t, f"dme2_{ep}", b.n(f"dme2_{ep}"))
        for tc in range(8):
            xp = tc % 2
            if e == 0 and tc == 0:
                issue_xnb(0, 0)
            b.wait(t, f"dxnl{xp}", dxnl_cnt[xp])
            for ip in range(8):
                if ip == 3:
                    if tc + 1 < 8:
                        issue_xnb(1 - xp, tc + 1)
                    elif e + 1 < 8:
                        issue_xnb(1 - xp, 0)
                ipp = ip % 2
                b.wait(t, "mgl", psA_rel[ipp])
                for c in range(8):
                    mm = t.matmul(psA[ipp], w13[ep][:, c * 2048 + ip * 128: c * 2048 + (ip + 1) * 128],
                                  xnb[xp][:, c * 512:(c + 1) * 512], start=(c == 0), stop=(c == 7))
                peA = b.inc("pe", mm)
                b.wait(t, "mup", psB2_rel[ipp])
                for c in range(8):
                    mm = t.matmul(psB2[ipp], w13[ep][:, c * 2048 + 1024 + ip * 128: c * 2048 + 1024 + (ip + 1) * 128],
                                  xnb[xp][:, c * 512:(c + 1) * 512], start=(c == 0), stop=(c == 7))
                peB = b.inc("pe", mm)
                w13_rel[ep] = peB
                b.wait(v, "pe", peA)
                b.wait(v, "sil", gs_war)
                psA_rel[ipp] = b.inc("mgl", v.tensor_scalar(
                    g_s, psA[ipp], b13[:, e * 16 + ip:e * 16 + ip + 1], 7.0, ALU.add, ALU.min))
                b.wait(s, "mgl", psA_rel[ipp])
                b.wait(s, "mact", ss_war)
                gs_war = b.inc("sil", s.activation(s_s, g_s, AF.Silu, scale=1.702))
                b.vdep(b13u_vv)
                b.wait(v, "mact", ss_war)
                b.wait(v, "pe", peB)
                b.wait(v, "pe", act_war)
                psB2_rel[ipp] = b.inc("mup", v.tensor_scalar(
                    u_s, psB2[ipp], b13u[:, e * 8 + ip:e * 8 + ip + 1], 8.0, ALU.add, ALU.min))
                b.wait(v, "sil", gs_war)
                b.wait(v, "mup", psB2_rel[ipp])
                ss_war = b.inc("mact", v.scalar_tensor_tensor(
                    act_sb[:, ip * 512:(ip + 1) * 512], u_s, -6.0, s_s, ALU.max, ALU.mult))
            xnb_rel[xp] = b.n("pe")
            b.wait(t, "mact", b.n("mact"))
            for tt in range(4):
                T = tc * 4 + tt
                cp = tt
                for dh in range(2):
                    mp = (tt * 2 + dh) % 4
                    b.wait(t, "macc", psM_rel[mp])
                    for ic in range(8):
                        t.matmul(psM[mp], act_sb[:, ic * 512 + tt * 128: ic * 512 + (tt + 1) * 128],
                                 w2[ep][:, ic * 1024 + dh * 512: ic * 1024 + (dh + 1) * 512],
                                 start=(ic == 0), stop=False)
                    mm = t.matmul(psM[mp], P.row1702,
                                  b2row[ep][0:1, dh * 512:(dh + 1) * 512], start=False, stop=True)
                    peM = b.inc("pe", mm)
                    b.wait(s, "pe", peM)
                    if cb_rel[cp] != 0:
                        b.wait(s, cb_rel[cp][0], cb_rel[cp][1])
                    psM_rel[mp] = b.inc("macc", s.activation(
                        contrib[cp][:, dh * 512:(dh + 1) * 512], psM[mp], AF.Identity,
                        scale=P.wsc[:, T * 8 + e: T * 8 + e + 1]))
                b.wait(g, "macc", b.n("macc"))
                dsem = f"dacc{T % 16}"
                b.wait(g, dsem, b.n(dsem))
                b.inc(dsem, g.dma_start(
                    out=P.x1_dram[T * 128:(T + 1) * 128, :], in_=contrib[cp],
                    accum_op=ALU.add), 16)
                cb_rel[cp] = (dsem, b.n(dsem))
            act_war = b.n("pe")
        w2_rel[ep] = b2_rel[ep] = b.n("pe")
    es.close()
    b.barrier()



def emit_p4(P, xn1T):
    nc, b, dr = P.nc, P.b, P.dr
    t, v, s, g, y = b.t, b.v, b.s, b.g, b.y
    es = ExitStack()
    xA = [es.enter_context(nc.sbuf_tensor(f"xA{i}", [128, 4096], F32)).ap() for i in range(2)]
    xB = [es.enter_context(nc.sbuf_tensor(f"xB{i}", [128, 4096], F32)).ap() for i in range(2)]
    xnf = [es.enter_context(nc.sbuf_tensor(f"xnf4{i}", [128, 1024], F32)).ap() for i in range(2)]
    scr = es.enter_context(nc.sbuf_tensor("scr4", [128, 1024], F32)).ap()
    xnTf = es.enter_context(nc.sbuf_tensor("xnTf4", [128, 1024], F32)).ap()
    gwT = es.enter_context(nc.sbuf_tensor("gwT", [128, 8 * 16], F32)).ap()
    gb_sb = es.enter_context(nc.sbuf_tensor("gb_sb", [128, 16], F32)).ap()
    gbrow = es.enter_context(nc.sbuf_tensor("gbrow", [1, 16], F32)).ap()
    tsc = es.enter_context(nc.sbuf_tensor("tsc", [128, 80], F32)).ap()
    psT = [es.enter_context(nc.psum_tensor(f"psT4_{i}", [128, 128], F32)).ap() for i in range(2)]
    psG = es.enter_context(nc.psum_tensor("psG", [128, 16], F32)).ap()
    dma3(b, g, gwT, dr["l1_gate_wT"], 16, "dl1")
    b.inc("dl1", g.dma_start(out=gbrow, in_=dr["l1_gate_b_row"]), 16)
    b.wait(t, "dl1")
    b.wait(t, "cmisc")
    b.wait(t, "dmisc")
    mmgb = t.matmul(psG, P.ones_row_f[0:1, :], gbrow[0:1, :], start=True, stop=True)
    pegb = b.inc("pe", mmgb)
    b.wait(v, "pe", pegb)
    b.vstep(v.tensor_copy(gb_sb, psG))
    gb_vv = b.n("vv")
    b.wait(v, "dmisc")

    psT_rel = [0, 0]
    psG_rel = ("vv", gb_vv)
    ln_done = 0
    ln_rel4 = {0: 0, 1: 0}
    sel_rel4 = {0: 0, 1: 0}
    xnf_rel4 = {0: 0, 1: 0}
    dxa_cnt = {0: 0, 1: 0}
    dxb_cnt = {0: 0, 1: 0}

    def issue_x14(tloc):
        pp = tloc % 2
        offA = 512 * tloc
        offB = (2048 + 512 * tloc) % 4096
        sma = "dxb" if pp == 0 else "dv0"
        smb = "dv1" if pp == 0 else "dq1"
        b.wait(y, "vv", sel_rel4[pp])
        b.wait(y, "ln_out", ln_rel4[pp])
        dma3(b, y, xA[pp], P.x1_dram[offA:offA + 512, :], 1024, sma)
        dma3(b, y, xB[pp], P.x1_dram[offB:offB + 512, :], 1024, smb)
        dxa_cnt[pp] = (sma, b.n(sma))
        dxb_cnt[pp] = (smb, b.n(smb))

    issue_x14(0)
    for tloc in range(8):
        pp = tloc % 2
        if tloc + 1 < 8:
            issue_x14(tloc + 1)
        b.wait(v, dxa_cnt[pp][0], dxa_cnt[pp][1])
        b.wait(v, dxb_cnt[pp][0], dxb_cnt[pp][1])
        b.wait(v, "ln_out", b.n("ln_out"))
        b.vdep()
        b.vstep(v.tensor_scalar(xA[pp], xA[pp], P.hsel[:, 0:1], None, ALU.mult))
        b.vdep()
        b.vstep(v.scalar_tensor_tensor(xA[pp], xB[pp], P.hsel[:, 1:2], xA[pp], ALU.mult, ALU.add))
        sel_rel4[pp] = b.n("vv")
        P.emit_ln_stats(xA[pp], scr, [("vv", sel_rel4[pp])])
        for tt in range(4):
            T = tloc * 4 + tt
            fp_ = tt % 2
            b.wait(s, "pe", xnf_rel4[fp_])
            ln_done = P.emit_ln_tile(xA[pp], tt, xnf[fp_])
            if tt == 3:
                ln_rel4[pp] = ln_done
            for c in range(8):
                tp = c % 2
                b.wait(t, "xe4", psT_rel[tp])
                b.wait(t, "ln_out", ln_done)
                pe_cnt = b.inc("pe", t.transpose(psT[tp], xnf[fp_][:, c * 128:(c + 1) * 128], P.ident))
                b.wait(v, "pe", pe_cnt)
                psT_rel[tp] = b.inc("xe4", v.tensor_copy(
                    xn1T[:, c * 4096 + T * 128: c * 4096 + (T + 1) * 128], psT[tp]))
                if tloc < 4:
                    psT_rel[tp] = b.inc("xe4", v.tensor_copy(xnTf[:, c * 128:(c + 1) * 128], psT[tp]))
            xnf_rel4[fp_] = b.n("pe")
            if tloc >= 4:
                continue
            b.wait(t, "xe4", b.n("xe4"))
            b.wait(t, psG_rel[0], psG_rel[1])
            for c in range(8):
                mm = t.matmul(psG, xnTf[:, c * 128:(c + 1) * 128], gwT[:, c * 16:(c + 1) * 16],
                              start=(c == 0), stop=(c == 7))
            gmm = b.inc("rmm", mm)
            b.wait(s, "rmm", gmm)
            b.wait(s, "vv", getattr(P, "_l1tsc", 0))
            b.sstep(s.activation(tsc[:, 0:16], psG, AF.Sigmoid))
            psG_rel = ("ss", b.n("ss"))
            b.wait(v, "ss", b.n("ss"))
            b.vdep(gb_vv)
            b.vstep(v.tensor_tensor(tsc[:, 16:32], tsc[:, 0:16], gb_sb, ALU.add))
            b.vdep()
            b.vstep(v.tensor_copy(tsc[:, 32:48], tsc[:, 16:32]))
            for it in range(4):
                b.vdep()
                b.vstep(v.reduce_max(tsc[:, 64 + it:65 + it], tsc[:, 32:48], AXX))
                if it < 3:
                    b.vdep()
                    b.vstep(v.tensor_scalar(tsc[:, 48:64], tsc[:, 32:48], tsc[:, 64 + it:65 + it], None, ALU.is_equal))
                    b.vdep()
                    b.vstep(v.scalar_tensor_tensor(tsc[:, 32:48], tsc[:, 48:64], -1e30, tsc[:, 32:48], ALU.mult, ALU.add))
            b.vdep()
            b.vstep(v.tensor_scalar(tsc[:, 48:64], tsc[:, 16:32], tsc[:, 67:68], None, ALU.is_ge))
            b.vdep()
            b.vstep(v.tensor_tensor(tsc[:, 32:48], tsc[:, 0:16], tsc[:, 48:64], ALU.mult))
            b.vdep()
            b.vstep(v.reduce_sum(tsc[:, 68:69], tsc[:, 32:48], AXX))
            b.vdep()
            b.vstep(v.reciprocal(tsc[:, 69:70], tsc[:, 68:69]))
            b.vdep()
            b.vstep(v.tensor_scalar(P.comb[:, T * 16:(T + 1) * 16], tsc[:, 32:48],
                                    tsc[:, 69:70], 2.5, ALU.mult, ALU.mult))
            P._l1tsc = b.n("vv")
    es.close()
    b.barrier()


def emit_p4b(P, xn1T, pooled, rk_sb, rv_sb):
    nc, b, dr = P.nc, P.b, P.dr
    t, v, s, g, y = b.t, b.v, b.s, b.g, b.y
    es = ExitStack()
    rkvw = es.enter_context(nc.sbuf_tensor("rkvw", [128, 8 * 512], BF16)).ap()
    rkb = es.enter_context(nc.sbuf_tensor("rkb", [128, 2], F32)).ap()
    rvb = es.enter_context(nc.sbuf_tensor("rvb", [1, 256], BF16)).ap()
    psP1 = es.enter_context(nc.psum_tensor("psP1", [128, 512], F32)).ap()
    dma3(b, g, rkvw, dr["l1_rkv_w"], 512, "drkv")
    b.inc("drkv", g.dma_start(out=rkb, in_=dr["l1_rk_b_cols"]), 16)
    b.inc("drkv", g.dma_start(out=rvb, in_=dr["l1_rv_b_row"]), 16)
    b.wait(t, "drkv")
    b.inc("rvones", v.memset(rv_sb, 1.0))
    b.wait(t, "rvones")
    poolf = es.enter_context(nc.sbuf_tensor("poolf", [128, 64], F32)).ap()
    for c in range(8):
        src = xn1T[:, c * 4096:(c + 1) * 4096].rearrange("p (l k) -> p l k", k=64)
        b.vdep()
        b.vstep(v.reduce_sum(poolf, src, AXX))
        b.vdep()
        b.vstep(v.tensor_scalar(pooled[:, c * 64:(c + 1) * 64],
                                poolf, 1.0 / 64.0, None, ALU.mult))
    pooled_bf = rv_sb
    b.wait(t, "vv", b.n("vv"))
    for rt in range(2):
        b.wait(t, "rkev", b.n("rkev"))
        for c in range(8):
            mm = t.matmul(psP1[:, 0:64], rkvw[:, c * 512 + rt * 128: c * 512 + (rt + 1) * 128],
                          pooled[:, c * 64:(c + 1) * 64], start=(c == 0), stop=(c == 7))
        pe_cnt = b.inc("pe", mm)
        b.wait(s, "pe", pe_cnt)
        b.inc("rkev", s.activation(rk_sb[:, rt * 64:(rt + 1) * 64], psP1[:, 0:64],
                                   AF.Identity, bias=rkb[:, rt:rt + 1]))
    b.wait(t, "rkev", b.n("rkev"))
    for c in range(8):
        t.matmul(psP1[0:64, 0:256], pooled[:, c * 64:(c + 1) * 64],
                 rkvw[:, c * 512 + 256: c * 512 + 512], start=(c == 0), stop=False)
    mm = t.matmul(psP1[0:64, 0:256], P.ones_row[0:1, 0:64], rvb[0:1, :], start=False, stop=True)
    pe_cnt = b.inc("pe", mm)
    b.wait(v, "pe", pe_cnt)
    dstv = rv_sb.rearrange("p (h e) -> p h e", e=65)[:, :, 0:64]
    b.vstep(v.tensor_copy(dstv, psP1[0:64, 0:256].rearrange("p (h d) -> p h d", d=64)))
    b.barrier()
    es.close()



def emit_p5(P, xn1T, pooled, rk_sb, rv_sb):
    nc, b, dr = P.nc, P.b, P.dr
    t, v, s, g, y = b.t, b.v, b.s, b.g, b.y
    es = ExitStack()
    wkv1 = es.enter_context(nc.sbuf_tensor("wkv1", [128, 8 * 2048], BF16)).ap()
    wq1 = es.enter_context(nc.sbuf_tensor("wq1", [128, 8 * 1024], BF16)).ap()
    rqw = es.enter_context(nc.sbuf_tensor("rqw", [128, 8 * 256], BF16)).ap()
    routw = es.enter_context(nc.sbuf_tensor("routw", [128, 2 * 1024], BF16)).ap()
    bq1 = es.enter_context(nc.sbuf_tensor("bq1", [128, 24], F32)).ap()
    bv1 = es.enter_context(nc.sbuf_tensor("bv1", [1, 1024], BF16)).ap()
    rqb = es.enter_context(nc.sbuf_tensor("rqb", [128, 2], F32)).ap()
    routb = es.enter_context(nc.sbuf_tensor("routb", [128, 8], F32)).ap()
    glb = es.enter_context(nc.sbuf_tensor("glb", [128, 2], BF16)).ap()
    grb = es.enter_context(nc.sbuf_tensor("grb", [128, 2], BF16)).ap()
    mloc4 = es.enter_context(nc.sbuf_tensor("mloc4", [128, 4096], BF16)).ap()
    mlat = es.enter_context(nc.sbuf_tensor("mlat", [64, 2048], BF16)).ap()
    K1 = es.enter_context(nc.sbuf_tensor("K1", [128, 8 * 640], BF16)).ap()
    V1 = es.enter_context(nc.sbuf_tensor("V1", [128, 5 * 1040], BF16)).ap()
    Q1 = es.enter_context(nc.sbuf_tensor("Q1", [128, 8 * 512], BF16)).ap()
    rq1 = es.enter_context(nc.sbuf_tensor("rq1", [128, 2 * 512], BF16)).ap()
    lat1 = es.enter_context(nc.sbuf_tensor("lat1", [128, 2 * 512], BF16)).ap()
    rexp = es.enter_context(nc.sbuf_tensor("rexp", [64, 512], BF16)).ap()
    exp4 = [es.enter_context(nc.sbuf_tensor(f"exp4_{i}", [128, 1024], BF16)).ap() for i in range(2)]
    loc1 = [es.enter_context(nc.sbuf_tensor(f"loc1_{i}", [128, 512], BF16)).ap() for i in range(2)]
    rem1 = es.enter_context(nc.sbuf_tensor("rem1", [128, 512], BF16)).ap()
    d1f = es.enter_context(nc.sbuf_tensor("d1f", [128, 512], F32)).ap()
    bc1 = es.enter_context(nc.sbuf_tensor("bc1", [64, 512], BF16)).ap()
    g1 = es.enter_context(nc.sbuf_tensor("g1", [65, 512], F32)).ap()
    srow1 = es.enter_context(nc.sbuf_tensor("srow1", [1, 1024], F32)).ap()
    mxo = [es.enter_context(nc.sbuf_tensor(f"mxo{i}", [128, 512], BF16)).ap() for i in range(2)]
    psSc = [es.enter_context(nc.psum_tensor(f"psSc{i}", [128, 1024], F32)).ap() for i in range(2)]
    psAV4 = [es.enter_context(nc.psum_tensor(f"psAV4_{i}", [65, 512], F32)).ap() for i in range(2)]
    psB1 = es.enter_context(nc.psum_tensor("psB1", [65, 512], F32)).ap()
    psX = es.enter_context(nc.psum_tensor("psX", [128, 512], F32)).ap()
    psPa = psX[:, 0:128]
    psPb = psSc[0][:, 0:512]
    psL = psSc[1][0:64, 0:512]
    psLAV = psSc[1][0:65, 512:1024]

    dma3(b, g, wkv1, dr["l1_wqkv"][:, 1024:3072], 2048, "da1")
    dma3(b, g, wq1, dr["l1_wqkv"][:, 0:1024], 1024, "da1")
    dma3(b, g, rqw, dr["l1_rq_w"], 256, "da1")
    dma3(b, g, routw, dr["l1_rout_w"], 1024, "da1")
    b.inc("da1", g.dma_start(out=bq1, in_=dr["l1_bqkv_cols"]), 16)
    b.inc("da1", g.dma_start(out=bv1, in_=dr["l1_bv_row"]), 16)
    b.inc("da1", g.dma_start(out=rqb, in_=dr["l1_rq_b_cols"]), 16)
    b.inc("da1", g.dma_start(out=routb, in_=dr["l1_rout_b_cols"]), 16)
    b.inc("da1", g.dma_start(out=glb, in_=dr["l1_gl_blk"]), 16)
    b.inc("da1", g.dma_start(out=grb, in_=dr["l1_gr_blk"]), 16)
    b.inc("da1", g.dma_start(out=mloc4, in_=dr["l1_mloc4"]), 16)
    b.inc("da1", g.dma_start(out=mlat, in_=dr["l1_mlat"]), 16)
    b.inc("v1ones", v.memset(V1, 1.0))
    b.wait(t, "da1")
    b.wait(t, "v1ones")
    b.wait(v, "da1")
    b.wait(s, "da1")

    def halo_cols(c, sl):
        if sl == 0:
            return xn1T[:, c * 4096 + 3968: c * 4096 + 4096]
        return xn1T[:, c * 4096 + sl * 512 - 128: c * 4096 + sl * 512]

    st = {
        "kev_rel": 0, "qev_rel": [0, 0],
        "psL_rel": ("ss", 0), "srow_war": 0, "psb_war": 0, "bce_war": ("lyn", 0),
        "vv_rec": 0, "psRt_rel": 0, "rem_rel": 0, "g1_war": 0, "psGB_rel": 0,
        "mx_rel": [0, 0], "locrel_vv": [0, 0], "locrel_pe": [0, 0],
    }
    exp_rel = {0: [("ss", 0)], 1: [("ss", 0)]}
    expb_rel = {0: 0, 1: 0}
    avrel = {0: ("locn", 0), 1: ("locn", 0)}
    pend = []

    def mk_epi1(lv_all, par):
        def f():
            b.wait(v, "lv", lv_all)
            b.wait(v, "pe", st["srow_war"])
            b.vstep(v.tensor_scalar(srow1[0:1, 0:512], psAV4[par][64:65, :], 1e-30, None, ALU.max))
            b.vdep()
            b.vstep(v.reciprocal(srow1[0:1, 512:1024], srow1[0:1, 0:512]))
            st["vv_rec"] = b.n("vv")
        return f

    def mk_epi2(hp, hh, par):
        def f():
            b.wait(t, "vv", st["vv_rec"])
            b.wait(t, "bce1", st["psb_war"])
            b.wait(t, "vv", st["psGB_rel"])
            mmb = t.matmul(psB1[0:64, :], P.ones_row_f[0:1, 0:64], srow1[0:1, 512:1024],
                           start=True, stop=True)
            st["srow_war"] = peb = b.inc("pe", mmb)
            b.wait(s, "pe", peb)
            b.wait(s, st["bce_war"][0], st["bce_war"][1])
            st["psb_war"] = b.inc("bce1", s.copy(bc1, psB1[0:64, :]))
            b.wait(v, "bce1", st["psb_war"])
            b.wait(v, "vv", st["locrel_vv"][hp % 2])
            b.wait(v, "pe", st["locrel_pe"][hp % 2])
            yn = v.tensor_tensor(loc1[hp % 2][64 * hh:64 * hh + 64, :],
                                 psAV4[par][0:64, :], bc1, ALU.mult)
            cnt = b.inc("locn", yn)
            st["bce_war"] = ("locn", cnt)
            avrel[par] = ("locn", cnt)
        return f

    def mk_stage3(hp, sl):
        def f():
            mp_ = hp % 2
            lcb = loc1[hp % 2]
            b.wait(t, "remev", st["psRt_rel"])
            mm = None
            for lp in range(2):
                mm = t.matmul(psX, routw[:, lp * 1024 + hp * 128: lp * 1024 + (hp + 1) * 128],
                              lat1[:, lp * 512:(lp + 1) * 512], start=(lp == 0), stop=(lp == 1))
            pe_cnt = b.inc("pe", mm)
            b.wait(s, "pe", pe_cnt)
            b.wait(s, "vv", st["rem_rel"])
            remev = b.inc("remev", s.activation(rem1, psX, AF.Identity, bias=routb[:, hp:hp + 1]))
            st["psRt_rel"] = remev
            locn_all = b.n("locn")
            b.wait(t, "locn", locn_all)
            b.wait(t, "remev", remev)
            b.wait(t, "bce1", st["psb_war"])
            t.matmul(psB1[0:1, :], glb[:, 0:1], lcb, start=True, stop=False)
            t.matmul(psB1[0:1, :], grb[:, 0:1], rem1, start=False, stop=True)
            t.matmul(psB1[64:65, :], glb[:, 1:2], lcb, start=True, stop=False)
            mm = t.matmul(psB1[64:65, :], grb[:, 1:2], rem1, start=False, stop=True)
            peg = b.inc("pe", mm)
            b.wait(s, "pe", peg)
            b.wait(s, "pe", st["g1_war"])
            s.activation(g1[0:1, :], psB1[0:1, :], AF.Sigmoid)
            sg = s.activation(g1[64:65, :], psB1[64:65, :], AF.Sigmoid)
            sg_cnt = b.inc("ss", sg)
            b.wait(t, "ss", sg_cnt)
            b.wait(t, avrel[1][0], avrel[1][1])
            t.matmul(psAV4[1][0:64, :], P.ones_row_f[0:1, 0:64], g1[0:1, :], start=True, stop=True)
            mm2 = t.matmul(psB1[0:64, :], P.ones_row_f[64:65, 0:64], g1[64:65, :], start=True, stop=True)
            peb3 = b.inc("pe", mm2)
            b.wait(v, "pe", peb3)
            b.wait(v, "remev", remev)
            b.wait(v, "locn", locn_all)
            b.vdep()
            b.vstep(v.tensor_sub(d1f, lcb, rem1))
            b.vdep()
            b.vstep(v.tensor_tensor(d1f[0:64, :], d1f[0:64, :], psAV4[1][0:64, :], ALU.mult))
            b.vstep(v.tensor_tensor(d1f[64:128, :], d1f[64:128, :], psB1[0:64, :], ALU.mult))
            b.wait(v, f"dmx{mp_}", st["mx_rel"][mp_])
            b.vdep()
            b.vstep(v.tensor_tensor(mxo[mp_], d1f, rem1, ALU.add))
            b.wait(y, "vv", b.n("vv"))
            b.inc(f"dmx{mp_}", y.dma_start(out=P.mx_dram[hp, :, sl * 512:(sl + 1) * 512], in_=mxo[mp_]), 16)
            st["mx_rel"][mp_] = b.n(f"dmx{mp_}")
            vvn = b.n("vv")
            st["rem_rel"] = vvn
            st["psGB_rel"] = vvn
            st["g1_war"] = peb3
            st["psb_war"] = max(st["psb_war"], 0)
            avrel[1] = ("vv", vvn)
            st["locrel_vv"][hp % 2] = vvn
            st["locrel_pe"][hp % 2] = peg
        return f

    for sl in range(4):
        for hp in range(8):
            b.wait(t, "kev", st["kev_rel"])
            if hp == 0:
                b.wait(t, "remev", st["psRt_rel"])
                for w_ in exp_rel[0]:
                    b.wait(t, w_[0], w_[1])
            mm = None
            for c in range(8):
                mm = t.matmul(psPa, wkv1[:, c * 2048 + hp * 128: c * 2048 + (hp + 1) * 128],
                              halo_cols(c, sl), start=(c == 0), stop=(c == 7))
            pea = b.inc("pe", mm)
            for c in range(8):
                mm = t.matmul(psPb, wkv1[:, c * 2048 + hp * 128: c * 2048 + (hp + 1) * 128],
                              xn1T[:, c * 4096 + sl * 512: c * 4096 + (sl + 1) * 512],
                              start=(c == 0), stop=(c == 7))
            peb = b.inc("pe", mm)
            b.wait(s, "pe", peb)
            s.activation(K1[:, hp * 640: hp * 640 + 128], psPa, AF.Identity, bias=bq1[:, 8 + hp:9 + hp])
            st["kev_rel"] = b.inc("kev", s.activation(
                K1[:, hp * 640 + 128: hp * 640 + 640], psPb, AF.Identity, bias=bq1[:, 8 + hp:9 + hp]))
        b.wait(t, "kev", b.n("kev"))
        for kt in range(5):
            for vh in range(2):
                b.wait(t, "vev", b.n("vev"))
                for c in range(8):
                    src = halo_cols(c, sl) if kt == 0 else \
                        xn1T[:, c * 4096 + sl * 512 + (kt - 1) * 128: c * 4096 + sl * 512 + kt * 128]
                    t.matmul(psPb, src,
                             wkv1[:, c * 2048 + 1024 + vh * 512: c * 2048 + 1024 + (vh + 1) * 512],
                             start=(c == 0), stop=False)
                mm = t.matmul(psPb, P.ones_row, bv1[0:1, vh * 512:(vh + 1) * 512], start=False, stop=True)
                pe_cnt = b.inc("pe", mm)
                b.wait(v, "pe", pe_cnt)
                src = psPb.rearrange("p (h d) -> p h d", d=64)
                for gsub in range(4):
                    hg = vh * 4 + gsub
                    dst = V1[:, kt * 1040 + hg * 130: kt * 1040 + (hg + 1) * 130]
                    dst = dst.rearrange("p (h e) -> p h e", e=65)[:, :, 0:64]
                    b.inc("vev", v.tensor_copy(dst, src[:, gsub * 2:gsub * 2 + 2, :]))
        b.wait(t, "vev", b.n("vev"))
        for hp in range(8):
            qp = hp % 2
            b.wait(t, "qev", st["qev_rel"][qp])
            ps = psPb if qp else psX
            for c in range(8):
                mm = t.matmul(ps, wq1[:, c * 1024 + hp * 128: c * 1024 + (hp + 1) * 128],
                              xn1T[:, c * 4096 + sl * 512: c * 4096 + (sl + 1) * 512],
                              start=(c == 0), stop=(c == 7))
            pe_cnt = b.inc("pe", mm)
            b.wait(s, "pe", pe_cnt)
            st["qev_rel"][qp] = b.inc("qev", s.activation(
                Q1[:, hp * 512:(hp + 1) * 512], ps, AF.Identity, bias=bq1[:, hp:hp + 1]))
        b.wait(t, "qev", b.n("qev"))
        for rt in range(2):
            qp = rt % 2
            b.wait(t, "rqev", max(0, b.n("rqev") - 1))
            ps = psPb if qp else psX
            for c in range(8):
                mm = t.matmul(ps, rqw[:, c * 256 + rt * 128: c * 256 + (rt + 1) * 128],
                              xn1T[:, c * 4096 + sl * 512: c * 4096 + (sl + 1) * 512],
                              start=(c == 0), stop=(c == 7))
            pe_cnt = b.inc("pe", mm)
            b.wait(s, "pe", pe_cnt)
            b.inc("rqev", s.activation(rq1[:, rt * 512:(rt + 1) * 512], ps,
                                       AF.Identity, bias=rqb[:, rt:rt + 1]))
        b.wait(t, "rqev", b.n("rqev"))
        for lh in range(4):
            lp, lr = lh // 2, 64 * (lh % 2)
            b.wait(t, st["psL_rel"][0], st["psL_rel"][1])
            if lh == 0:
                for w_ in exp_rel[1]:
                    b.wait(t, w_[0], w_[1])
            mm = t.matmul(psL, rk_sb[lr:lr + 64, lp * 64:(lp + 1) * 64],
                          rq1[lr:lr + 64, lp * 512:(lp + 1) * 512], start=True, stop=True)
            pe_cnt = b.inc("pe", mm)
            b.wait(s, "pe", pe_cnt)
            b.wait(s, "lmk", b.n("lmk"))
            b.sstep(s.activation(rexp, psL, AF.Exp, scale=0.125))
            st["psL_rel"] = ("ss", b.n("ss"))
            b.wait(v, "ss", b.n("ss"))
            b.wait(v, "lav", b.n("lav"))
            lmk = b.inc("lmk", v.tensor_tensor(rexp, rexp, mlat[:, sl * 512:(sl + 1) * 512], ALU.mult))
            b.wait(t, "lmk", lmk)
            b.wait(t, "lyn", b.n("lyn"))
            mm = t.matmul(psLAV, rv_sb[:, lh * 65: lh * 65 + 65], rexp, start=True, stop=True)
            b.inc("lav", mm)
            b.wait(v, "lav", b.n("lav"))
            b.wait(v, "pe", st["srow_war"])
            b.vstep(v.tensor_scalar(srow1[0:1, 0:512], psLAV[64:65, :], 1e-30, None, ALU.max))
            b.vdep()
            b.vstep(v.reciprocal(srow1[0:1, 512:1024], srow1[0:1, 0:512]))
            b.wait(t, "vv", b.n("vv"))
            b.wait(t, "bce1", st["psb_war"])
            b.wait(t, "vv", st["psGB_rel"])
            mmb = t.matmul(psB1[0:64, :], P.ones_row_f[0:1, 0:64], srow1[0:1, 512:1024],
                           start=True, stop=True)
            st["srow_war"] = peb = b.inc("pe", mmb)
            b.wait(s, "pe", peb)
            b.wait(s, st["bce_war"][0], st["bce_war"][1])
            st["psb_war"] = b.inc("bce1", s.copy(bc1, psB1[0:64, :]))
            b.wait(v, "bce1", b.n("bce1"))
            yn = v.tensor_tensor(lat1[lr:lr + 64, lp * 512:(lp + 1) * 512], psLAV[0:64, :], bc1, ALU.mult)
            st["bce_war"] = ("lyn", b.inc("lyn", yn))
        b.wait(t, "lyn", b.n("lyn"))
        exp_rel[0] = [("rqev", b.n("rqev"))]
        exp_rel[1] = [("lyn", b.n("lyn"))]
        for hp in range(8):
            for hh in range(2):
                ci = hp * 2 + hh
                sp = hh
                for w_ in exp_rel[sp]:
                    b.wait(t, w_[0], w_[1])
                psc = psSc[sp]
                mm = None
                for qt in range(4):
                    for u in (0, 1):
                        mm = t.matmul(
                            psc[:, qt * 256 + u * 128: qt * 256 + (u + 1) * 128],
                            K1[64 * hh:64 * hh + 64, hp * 640 + (qt + u) * 128: hp * 640 + (qt + u + 1) * 128],
                            Q1[64 * hh:64 * hh + 64, hp * 512 + qt * 128: hp * 512 + (qt + 1) * 128],
                            start=True, stop=True)
                pe_cnt = b.inc("pe", mm)
                b.wait(s, "pe", pe_cnt)
                b.wait(s, "lv", expb_rel[sp])
                e_in = s.activation(exp4[sp], psc, AF.Exp, scale=0.125)
                ss_cnt = b.inc("ss", e_in)
                exp_rel[sp] = [("ss", ss_cnt)]
                b.wait(v, "ss", ss_cnt)
                b.wait(v, "lv", expb_rel[sp])
                mk = b.inc("lmkk", v.tensor_tensor(exp4[sp], exp4[sp],
                                                   mloc4[:, sl * 1024:(sl + 1) * 1024], ALU.mult))
                exp_rel[sp] = [("lmkk", mk)]
                while pend and pend[0][0] <= ci:
                    pend.pop(0)[1]()
                b.wait(t, "lmkk", mk)
                b.wait(t, avrel[sp][0], avrel[sp][1])
                mm = None
                for qt in range(4):
                    t.matmul(psAV4[sp][:, qt * 128:(qt + 1) * 128],
                             V1[:, qt * 1040 + hp * 130 + hh * 65: qt * 1040 + hp * 130 + hh * 65 + 65],
                             exp4[sp][:, qt * 256: qt * 256 + 128], start=True, stop=False)
                    mm = t.matmul(psAV4[sp][:, qt * 128:(qt + 1) * 128],
                                  V1[:, (qt + 1) * 1040 + hp * 130 + hh * 65: (qt + 1) * 1040 + hp * 130 + hh * 65 + 65],
                                  exp4[sp][:, qt * 256 + 128: qt * 256 + 256], start=False, stop=True)
                expb_rel[sp] = b.inc("lv", mm)
                pend.append((ci + 1, mk_epi1(b.n("lv"), sp)))
                pend.append((ci + 2, mk_epi2(hp, hh, sp)))
            pend.append((hp * 2 + 3, mk_stage3(hp, sl)))
        while pend:
            pend.pop(0)[1]()
    b.barrier()
    es.close()



def emit_p5b(P):
    nc, b, dr = P.nc, P.b, P.dr
    t, v, s, g, y = b.t, b.v, b.s, b.g, b.y
    es_op = ExitStack()
    wout = es_op.enter_context(nc.sbuf_tensor("wout", [128, 8 * 1024], BF16)).ap()
    bo_row = es_op.enter_context(nc.sbuf_tensor("bo_row", [1, 1024], BF16)).ap()
    xsA = [es_op.enter_context(nc.sbuf_tensor(f"xsA{i}", [128, 1024], F32)).ap() for i in range(2)]
    xsB = [es_op.enter_context(nc.sbuf_tensor(f"xsB{i}", [128, 1024], F32)).ap() for i in range(2)]
    mixbuf = [es_op.enter_context(nc.sbuf_tensor(f"mixbuf{i}", [128, 8 * 128], BF16)).ap() for i in range(2)]
    psO5 = [es_op.enter_context(nc.psum_tensor(f"psO5_{i}", [128, 512], F32)).ap() for i in range(2)]
    dma3(b, g, wout, dr["l1_wout"], 1024, "dop")
    b.inc("dop", g.dma_start(out=bo_row, in_=dr["l1_bout_row"]), 16)
    b.wait(t, "dop")
    psO_rel = [0, 0]
    mixbuf_rel = [0, 0]
    oad_rel = [0, 0]
    st2_rel = {0: ("dx2", 0), 1: ("dout", 0)}
    dmxl_cnt = {0: 0, 1: 0}

    def issue_mx(T):
        mbp = T % 2
        sl, tt = T // 4, T % 4
        b.wait(y, "pe", mixbuf_rel[mbp])
        b.inc(f"dmxl{mbp}", y.dma_start(
            out=mixbuf[mbp].rearrange("p (c e) -> p c e", e=128),
            in_=P.mx_dram[:, :, sl * 512 + tt * 128: sl * 512 + (tt + 1) * 128].rearrange("c p e -> p c e")), 16)
        dmxl_cnt[mbp] = b.n(f"dmxl{mbp}")

    issue_mx(0)
    for T in range(16):
        sl, tt = T // 4, T % 4
        offA = 128 * T
        offB = (2048 + 128 * T) % 4096
        mbp = T % 2
        if T + 1 < 16:
            issue_mx(T + 1)
        b.wait(y, "oadd1", oad_rel[mbp])
        b.wait(y, st2_rel[mbp][0], st2_rel[mbp][1])
        xsem = "dxr" if mbp == 0 else "dxn"
        b.inc(xsem, y.dma_start(out=xsA[mbp], in_=P.x1_dram[offA:offA + 128, :]), 16)
        b.inc(xsem, y.dma_start(out=xsB[mbp], in_=P.x1_dram[offB:offB + 128, :]), 16)
        dx_t = b.n(xsem)
        b.wait(t, f"dmxl{mbp}", dmxl_cnt[mbp])
        b.wait(v, xsem, dx_t)
        b.vdep()
        b.vstep(v.tensor_scalar(xsA[mbp], xsA[mbp], P.hsel[:, 0:1], None, ALU.mult))
        b.vdep()
        b.vstep(v.scalar_tensor_tensor(xsA[mbp], xsB[mbp], P.hsel[:, 1:2], xsA[mbp], ALU.mult, ALU.add))
        sel_vv = b.n("vv")
        for dh in range(2):
            op = dh
            b.wait(t, "oadd1", psO_rel[op])
            ps = psO5[op]
            for c in range(8):
                t.matmul(ps, mixbuf[mbp][:, c * 128:(c + 1) * 128],
                         wout[:, c * 1024 + dh * 512: c * 1024 + (dh + 1) * 512],
                         start=(c == 0), stop=False)
            mm = t.matmul(ps, P.ones_row, bo_row[0:1, dh * 512:(dh + 1) * 512],
                          start=False, stop=True)
            pe_cnt = b.inc("pe", mm)
            b.wait(v, "pe", pe_cnt)
            b.vdep(sel_vv)
            ad = v.tensor_tensor(xsA[mbp][:, dh * 512:(dh + 1) * 512],
                                 xsA[mbp][:, dh * 512:(dh + 1) * 512], ps, ALU.add)
            psO_rel[op] = b.inc("oadd1", ad)
        mixbuf_rel[mbp] = b.n("pe")
        oad_rel[mbp] = b.n("oadd1")
        b.wait(y, "oadd1", b.n("oadd1"))
        ssem = "dx2" if mbp == 0 else "dout"
        b.inc(ssem, y.dma_start(out=P.x2p_dram[T * 128:(T + 1) * 128, :], in_=xsA[mbp]), 16)
        st2_rel[mbp] = (ssem, b.n(ssem))
    b.barrier()
    es_op.close()



def emit_p6(P, xn1T, out):
    nc, b, dr = P.nc, P.b, P.dr
    t, v, s, g, y = b.t, b.v, b.s, b.g, b.y
    es = ExitStack()
    x_res = es.enter_context(nc.sbuf_tensor("x_res", [128, 16 * 1024], F32)).ap()
    w13 = [es.enter_context(nc.sbuf_tensor(f"w13a_{i}", [128, 8 * 1024], BF16)).ap() for i in range(2)]
    w2 = [es.enter_context(nc.sbuf_tensor(f"w2a_{i}", [128, 4 * 1024], BF16)).ap() for i in range(2)]
    act_sb = es.enter_context(nc.sbuf_tensor("act1_sb", [128, 4 * 512], BF16)).ap()
    s_s = es.enter_context(nc.sbuf_tensor("s1_s", [128, 512], F32)).ap()
    psA = es.enter_context(nc.psum_tensor("psA1", [128, 512], F32)).ap()
    psB2 = es.enter_context(nc.psum_tensor("psB21", [128, 512], F32)).ap()
    psM = [es.enter_context(nc.psum_tensor(f"psM1{i}", [128, 512], F32)).ap() for i in range(4)]

    for T in range(16):
        b.inc("dx2l", y.dma_start(out=x_res[:, T * 1024:(T + 1) * 1024],
                                  in_=P.x2p_dram[T * 128:(T + 1) * 128, :]), 16)
    b.wait(v, "dx2l", b.n("dx2l"))

    NEXP = 17

    def wsrc(e):
        if e < 16:
            return dr["l1_w13"][e], dr["l1_w2"][e]
        return dr["l1_w13s"], dr["l1_w2s"]
    w_rel = [0, 0]
    dma3(b, g, w13[0], wsrc(0)[0], 1024, "dm1_0")
    dma3(b, g, w2[0], wsrc(0)[1], 1024, "dm2_0")
    psA_rel = 0
    psB2_rel = 0
    psM_rel = [0, 0, 0, 0]
    act_war = 0
    ss_war = 0
    macc_reg = {}
    for e in range(NEXP):
        ep = e % 2
        if e + 1 < NEXP:
            b.wait(g, "pe", w_rel[1 - ep])
            dma3(b, g, w13[1 - ep], wsrc(e + 1)[0], 1024, f"dm1_{1 - ep}")
            dma3(b, g, w2[1 - ep], wsrc(e + 1)[1], 1024, f"dm2_{1 - ep}")
        b.wait(t, f"dm1_{ep}", b.n(f"dm1_{ep}"))
        b.wait(t, f"dm2_{ep}", b.n(f"dm2_{ep}"))
        for tc in range(4):
            for fp in range(4):
                b.wait(t, "sil1", psA_rel)
                for c in range(8):
                    mm = t.matmul(psA, w13[ep][:, c * 1024 + fp * 128: c * 1024 + (fp + 1) * 128],
                                  xn1T[:, c * 4096 + tc * 512: c * 4096 + (tc + 1) * 512],
                                  start=(c == 0), stop=(c == 7))
                peA = b.inc("pe", mm)
                b.wait(t, "mact1", psB2_rel)
                for c in range(8):
                    mm = t.matmul(psB2, w13[ep][:, c * 1024 + 512 + fp * 128: c * 1024 + 512 + (fp + 1) * 128],
                                  xn1T[:, c * 4096 + tc * 512: c * 4096 + (tc + 1) * 512],
                                  start=(c == 0), stop=(c == 7))
                peB = b.inc("pe", mm)
                w_rel[ep] = peB
                b.wait(s, "pe", peA)
                b.wait(s, "mact1", ss_war)
                psA_rel = b.inc("sil1", s.activation(s_s, psA, AF.Silu))
                b.wait(v, "pe", peB)
                b.wait(v, "sil1", psA_rel)
                b.wait(v, "pe", act_war)
                psB2_rel = ss_war = b.inc("mact1", v.tensor_tensor(
                    act_sb[:, fp * 512:(fp + 1) * 512], psB2, s_s, ALU.mult))
            b.wait(t, "mact1", b.n("mact1"))
            for tt in range(4):
                T = tc * 4 + tt
                for dh in range(2):
                    mp = (tt * 2 + dh) % 4
                    b.wait(t, "macc1", psM_rel[mp])
                    for ic in range(4):
                        mm = t.matmul(psM[mp], act_sb[:, ic * 512 + tt * 128: ic * 512 + (tt + 1) * 128],
                                      w2[ep][:, ic * 1024 + dh * 512: ic * 1024 + (dh + 1) * 512],
                                      start=(ic == 0), stop=(ic == 3))
                    peM = b.inc("pe", mm)
                    b.wait(v, "pe", peM)
                    b.wait(v, "macc1", macc_reg.get((T, dh), 0))
                    scl = P.comb[:, T * 16 + e: T * 16 + e + 1] if e < 16 else 1.0
                    psM_rel[mp] = macc_reg[(T, dh)] = b.inc("macc1", v.scalar_tensor_tensor(
                        x_res[:, T * 1024 + dh * 512: T * 1024 + (dh + 1) * 512],
                        psM[mp], scl,
                        x_res[:, T * 1024 + dh * 512: T * 1024 + (dh + 1) * 512],
                        ALU.mult, ALU.add))
            act_war = b.n("pe")
            w_rel[ep] = b.n("pe")
    b.barrier()
    for T in range(16):
        b.inc("dout", y.dma_start(out=out[T * 128:(T + 1) * 128, :],
                                  in_=x_res[:, T * 1024:(T + 1) * 1024]), 16)
    b.wait(y, "dout", b.n("dout"))
    es.close()



def build_program(stage="full"):
    nc = bass.Bass("TRN2", target_bir_lowering=False)
    b = Bld(nc)
    dr = {}
    for name, shape in ALL_INPUTS:
        dr[name] = nc.dram_tensor(name, list(shape), F32, kind="ExternalInput").ap()
    P = Prog(nc, b, dr)
    P.out = nc.dram_tensor("out", [2048, 1024], F32, kind="ExternalOutput").ap()
    if stage in ("l0x", "l0m"):
        P.dbg = nc.dram_tensor("dbg", [4096, 1024], F32, kind="ExternalOutput").ap()
    P.x1_dram = nc.dram_tensor("x1_scratch", [4096, 1024], F32, kind="Internal").ap()
    P.v_dram = nc.dram_tensor("v_scratch", [8, 128, 32, 130], BF16, kind="Internal").ap()
    P.q_dram = nc.dram_tensor("q_scratch", [8, 128, 8, 512], BF16, kind="Internal").ap()
    P.xnt_dram = nc.dram_tensor("xnt_scratch", [8, 128, 8, 512], BF16, kind="Internal").ap()
    P.x2p_dram = nc.dram_tensor("x2p_scratch", [2048, 1024], F32, kind="Internal").ap()
    P.mx_dram = nc.dram_tensor("mx_scratch", [8, 128, 2048], BF16, kind="Internal").ap()

    es_k = ExitStack()
    K_sb = es_k.enter_context(nc.sbuf_tensor("K_sb", [128, 8 * 4096], BF16)).ap()
    emit_p1(P, K_sb)
    emit_p2(P, K_sb)
    es_k.close()

    def dbg_dump():
        for T in range(32):
            b.inc("dout", b.y.dma_start(out=P.dbg[T * 128:(T + 1) * 128, :],
                                        in_=P.x1_dram[T * 128:(T + 1) * 128, :]), 16)
        for T in range(16):
            b.inc("dout", b.y.dma_start(out=P.out[T * 128:(T + 1) * 128, :],
                                        in_=P.x1_dram[T * 128:(T + 1) * 128, :]), 16)
        b.wait(b.y, "dout", b.n("dout"))

    if stage == "l0x":
        dbg_dump()
        return nc
    emit_p3(P)
    if stage == "l0m":
        dbg_dump()
        return nc

    es1 = ExitStack()
    xn1T = es1.enter_context(nc.sbuf_tensor("xn1T", [128, 8 * 4096], BF16)).ap()
    pooled = es1.enter_context(nc.sbuf_tensor("pooled", [128, 8 * 64], BF16)).ap()
    rk_sb = es1.enter_context(nc.sbuf_tensor("rk_sb", [128, 2 * 64], BF16)).ap()
    rv_sb = es1.enter_context(nc.sbuf_tensor("rv_sb", [64, 4 * 65], BF16)).ap()
    emit_p4(P, xn1T)
    emit_p4b(P, xn1T, pooled, rk_sb, rv_sb)
    emit_p5(P, xn1T, pooled, rk_sb, rv_sb)
    emit_p5b(P)
    emit_p6(P, xn1T, P.out)
    es1.close()
    return nc


def kernel(**inputs):
    inputs.get("L1_ffn_input_ids")
    nc = build_program("full")
    im = [host_inputs(inputs, core) for core in range(8)]
    res = run_bass_kernel_spmd(nc, im, core_ids=list(range(8)))
    return reassemble([res.results[c]["out"] for c in range(8)])


# version 24
# speedup vs baseline: 1.0243x; 1.0047x over previous
import math
import os
from contextlib import ExitStack

import numpy as np

import concourse.bass as bass
import concourse.mybir as mybir
from concourse.bass_utils import run_bass_kernel_spmd

F32 = mybir.dt.float32
BF16 = mybir.dt.bfloat16
ALU = mybir.AluOpType
AF = mybir.ActivationFunctionType
AXX = mybir.AxisListType.X

D = 1024
EPS = 1e-5
KT_FULL = [4, 8, 12, 16, 20, 24, 28, 32]



def _tile_cols(vec, n_tiles):
    return np.ascontiguousarray(vec.reshape(n_tiles, 128).T)


def host_inputs(inp, core):
    role = core % 2
    samp = core // 2
    f32 = np.float32
    d = {}
    x = np.asarray(inp["L0_attn_x"], dtype=f32)
    d["x_abs"] = np.ascontiguousarray(x[samp])

    d["l0_wqkv"] = np.ascontiguousarray(np.asarray(inp["L0_attn_W_attn"], f32))
    b_attn = np.asarray(inp["L0_attn_b_attn"], f32)
    d["l0_bqkv_cols"] = _tile_cols(b_attn, 24)
    d["l0_bv_row"] = np.ascontiguousarray(b_attn[2048:3072].reshape(1, 1024))
    d["l0_wproj"] = np.ascontiguousarray(np.asarray(inp["L0_attn_W_proj"], f32))
    d["l0_bproj_row"] = np.ascontiguousarray(np.asarray(inp["L0_attn_b_proj"], f32).reshape(1, 1024))

    tri = np.zeros((4, 128, 512), f32)
    for j in range(4):
        tri[j] = ((j * 128 + np.arange(128)[:, None]) <= np.arange(512)[None, :])
    d["l0_tri"] = np.ascontiguousarray(tri.transpose(1, 0, 2).reshape(128, 2048))

    d["l0_router_w"] = np.ascontiguousarray(np.asarray(inp["L0_ffn_router_weight"], f32))
    gup = np.asarray(inp["L0_ffn_gate_up_proj"], f32)
    perm = np.concatenate([np.arange(0, 2048, 2), np.arange(1, 2048, 2)])
    d["l0_w13"] = np.ascontiguousarray(gup[:, :, perm])
    gub = np.asarray(inp["L0_ffn_gate_up_proj_bias"], f32)[:, perm]
    d["l0_b13_cols"] = np.ascontiguousarray(gub.reshape(8, 16, 128).transpose(2, 0, 1).reshape(128, 128))
    d["l0_w2"] = np.ascontiguousarray(np.asarray(inp["L0_ffn_down_proj"], f32))
    d["l0_b2_rows"] = np.ascontiguousarray(np.asarray(inp["L0_ffn_down_proj_bias"], f32).reshape(1, 8192))

    d["l1_wqkv"] = np.ascontiguousarray(np.asarray(inp["L1_attn_qkv_w"], f32))
    b1 = np.asarray(inp["L1_attn_qkv_b"], f32)
    d["l1_bqkv_cols"] = _tile_cols(b1, 24)
    d["l1_bv_row"] = np.ascontiguousarray(b1[2048:3072].reshape(1, 1024))
    d["l1_rq_w"] = np.ascontiguousarray(np.asarray(inp["L1_attn_rq_w"], f32))
    d["l1_rq_b_cols"] = _tile_cols(np.asarray(inp["L1_attn_rq_b"], f32), 2)
    d["l1_rkv_w"] = np.ascontiguousarray(np.asarray(inp["L1_attn_rkv_w"], f32))
    rkvb = np.asarray(inp["L1_attn_rkv_b"], f32)
    d["l1_rk_b_cols"] = _tile_cols(rkvb[:256], 2)
    d["l1_rv_b_row"] = np.ascontiguousarray(rkvb[256:].reshape(1, 256))
    d["l1_rout_w"] = np.ascontiguousarray(np.asarray(inp["L1_attn_rout_w"], f32))
    d["l1_rout_b_cols"] = _tile_cols(np.asarray(inp["L1_attn_rout_b"], f32), 8)
    d["l1_wout"] = np.ascontiguousarray(np.asarray(inp["L1_attn_out_w"], f32))
    d["l1_bout_row"] = np.ascontiguousarray(np.asarray(inp["L1_attn_out_b"], f32).reshape(1, 1024))
    for nm, key in (("l1_gl_blk", "L1_attn_gl_w"), ("l1_gr_blk", "L1_attn_gr_w")):
        blk = np.zeros((128, 2), f32)
        wv = np.asarray(inp[key], f32)[:, 0]
        blk[0:64, 0] = wv
        blk[64:128, 1] = wv
        d[nm] = blk

    kk = np.arange(128)[:, None]
    qq = np.arange(128)[None, :]
    mprev = (qq <= kk).astype(f32)
    mdiag = (qq > kk).astype(f32)
    d["l1_mloc"] = np.ascontiguousarray(np.concatenate([mprev, mdiag], axis=1))
    m0 = np.zeros((4, 128, 256), f32)
    for s_ in range(4):
        m0[s_, :, 0:128] = 0.0 if (role == 0 and s_ == 0) else mprev
        m0[s_, :, 128:256] = mdiag
    d["l1_mloc0"] = np.ascontiguousarray(m0.transpose(1, 0, 2).reshape(128, 1024))
    m4 = np.zeros((4, 128, 1024), f32)
    for s_ in range(4):
        for qt_ in range(4):
            m4[s_, :, qt_ * 256:(qt_ + 1) * 256] = m0[s_] if qt_ == 0 else d["l1_mloc"]
    d["l1_mloc4"] = np.ascontiguousarray(m4.transpose(1, 0, 2).reshape(128, 4096))

    col_abs = np.empty(64, np.int64)
    col_abs[0:32] = role * 32 + np.arange(32)
    col_abs[32:64] = (1 - role) * 32 + np.arange(32)
    mlat = np.zeros((4, 64, 512), f32)
    for s_ in range(4):
        qchunk = role * 32 + (s_ * 512 + np.arange(512)) // 64
        mlat[s_] = (col_abs[:, None] < qchunk[None, :]).astype(f32)
    d["l1_mlat"] = np.ascontiguousarray(mlat.transpose(1, 0, 2).reshape(64, 2048))

    sel = np.zeros((128, 2), f32)
    sel[:, role] = 1.0
    d["half_sel"] = sel

    d["l1_gate_wT"] = np.ascontiguousarray(np.asarray(inp["L1_ffn_gate_w"], f32).T)
    d["l1_gate_b_row"] = np.ascontiguousarray(np.asarray(inp["L1_ffn_gate_b"], f32).reshape(1, 16))
    d["l1_w13"] = np.ascontiguousarray(np.asarray(inp["L1_ffn_w13"], f32).transpose(0, 2, 1))
    d["l1_w2"] = np.ascontiguousarray(np.asarray(inp["L1_ffn_w2"], f32).transpose(0, 2, 1))
    d["l1_w13s"] = np.ascontiguousarray(np.asarray(inp["L1_ffn_w13_s"], f32).T)
    d["l1_w2s"] = np.ascontiguousarray(np.asarray(inp["L1_ffn_w2_s"], f32).T)

    d["identity"] = np.eye(128, dtype=f32)
    d["row_1702"] = np.full((1, 128), 1.702, f32)
    return d


def reassemble(outs):
    full = np.empty((4, 4096, 1024), np.float32)
    for core in range(8):
        role, samp = core % 2, core // 2
        full[samp, role * 2048:(role + 1) * 2048] = outs[core]
    return full


ALL_INPUTS = [
    ("x_abs", (4096, 1024)),
    ("l0_wqkv", (1024, 3072)), ("l0_bqkv_cols", (128, 24)), ("l0_bv_row", (1, 1024)),
    ("l0_wproj", (1024, 1024)), ("l0_bproj_row", (1, 1024)),
    ("l0_tri", (128, 2048)),
    ("l0_router_w", (1024, 8)),
    ("l0_w13", (8, 1024, 2048)), ("l0_b13_cols", (128, 128)),
    ("l0_w2", (8, 1024, 1024)), ("l0_b2_rows", (1, 8192)),
    ("l1_wqkv", (1024, 3072)), ("l1_bqkv_cols", (128, 24)), ("l1_bv_row", (1, 1024)),
    ("l1_rq_w", (1024, 256)), ("l1_rq_b_cols", (128, 2)),
    ("l1_rkv_w", (1024, 512)), ("l1_rk_b_cols", (128, 2)), ("l1_rv_b_row", (1, 256)),
    ("l1_rout_w", (256, 1024)), ("l1_rout_b_cols", (128, 8)),
    ("l1_wout", (1024, 1024)), ("l1_bout_row", (1, 1024)),
    ("l1_gl_blk", (128, 2)), ("l1_gr_blk", (128, 2)),
    ("l1_mloc", (128, 256)), ("l1_mloc0", (128, 1024)), ("l1_mlat", (64, 2048)),
    ("l1_mloc4", (128, 4096)),
    ("half_sel", (128, 2)),
    ("l1_gate_wT", (1024, 16)), ("l1_gate_b_row", (1, 16)),
    ("l1_w13", (16, 1024, 1024)), ("l1_w2", (16, 512, 1024)),
    ("l1_w13s", (1024, 1024)), ("l1_w2s", (512, 1024)),
    ("identity", (128, 128)), ("row_1702", (1, 128)),
]



class Bld:
    def __init__(self, nc):
        self.nc = nc
        self.t = nc.tensor
        self.v = nc.vector
        self.s = nc.scalar
        self.g = nc.gpsimd
        self.y = nc.sync
        self._sems = {}
        self._waited = {}

    def sem(self, name):
        if name not in self._sems:
            self._sems[name] = [self.nc.alloc_semaphore(name), 0]
        return self._sems[name]

    def inc(self, name, instr, k=1):
        sm = self.sem(name)
        instr.then_inc(sm[0], k)
        sm[1] += k
        return sm[1]

    def n(self, name):
        return self.sem(name)[1]

    def wait(self, eng, name, val=None):
        sm = self.sem(name)
        if val is None:
            val = sm[1]
        if val <= 0:
            return
        key = (id(eng), name)
        if self._waited.get(key, -1) >= val:
            return
        if val > sm[1]:
            raise RuntimeError(f"wait {name}>={val} can never fire (max {sm[1]})")
        eng.wait_ge(sm[0], val)
        self._waited[key] = val

    def vstep(self, instr):
        self.inc("vv", instr)
        return instr

    def vdep(self, val=None):
        self.wait(self.v, "vv", val)

    def sstep(self, instr):
        self.inc("ss", instr)
        return instr

    def sdep(self, val=None):
        self.wait(self.s, "ss", val)

    def barrier(self):
        engines = [self.t, self.v, self.s, self.g, self.y]
        for name, (h, cnt) in list(self._sems.items()):
            if cnt == 0:
                continue
            for e in engines:
                self.wait(e, name, cnt)


def dma3(b, eng, dst2d, src2d, inner, sem, p=128):
    in3 = src2d.rearrange("(c p) f -> p c f", p=p)
    out3 = dst2d.rearrange("p (c f) -> p c f", f=inner)
    return b.inc(sem, eng.dma_start(out=out3, in_=in3), 16)


class Prog:
    def __init__(self, nc, b, dr):
        self.nc = nc
        self.b = b
        self.dr = dr
        alloc = nc.alloc_sbuf_tensor
        self.ident = alloc("ident", [128, 128], F32).ap()
        self.ones_row = alloc("ones_row", [1, 128], BF16).ap()
        self.ones_row_f = alloc("ones_row_f", [65, 128], F32).ap()
        self.row1702 = alloc("row1702", [1, 128], BF16).ap()
        self.wsc = alloc("wsc", [128, 32 * 8], F32).ap()
        self.comb = alloc("comb", [128, 16 * 16], F32).ap()
        self.stat = alloc("stat", [128, 16], F32).ap()
        self.hsel = alloc("hsel", [128, 2], F32).ap()
        g, v = b.g, b.v
        b.inc("dmisc", g.dma_start(out=self.ident, in_=dr["identity"]), 16)
        b.inc("dmisc", g.dma_start(out=self.row1702, in_=dr["row_1702"]), 16)
        b.inc("dmisc", g.dma_start(out=self.hsel, in_=dr["half_sel"]), 16)
        v.memset(self.ones_row, 1.0)
        b.inc("cmisc", v.memset(self.ones_row_f, 1.0))

    def emit_ln_stats(self, xtile, scr, dep_sems):
        b, v, s = self.b, self.b.v, self.b.s
        st = self.stat
        for eng in (v, s):
            for nm, val in dep_sems:
                b.wait(eng, nm, val)
        b.wait(v, "ln_out", b.n("ln_out"))
        scr_rel = getattr(self, "_scr_rel", None)
        if scr_rel is None:
            scr_rel = self._scr_rel = [0]
        for tt in range(4):
            xt = xtile[:, tt * 1024:(tt + 1) * 1024]
            b.vstep(v.reduce_sum(st[:, tt:tt + 1], xt, AXX))
            b.wait(s, "vv", scr_rel[0])
            b.inc("ln_sq", s.activation(scr, xt, AF.Square))
            b.wait(v, "ln_sq")
            b.vstep(v.reduce_sum(st[:, 4 + tt:5 + tt], scr, AXX))
            scr_rel[0] = b.n("vv")
        b.vdep()
        b.vstep(v.tensor_scalar(st[:, 0:4], st[:, 0:4], 1.0 / D, None, ALU.mult))
        b.vstep(v.tensor_scalar(st[:, 4:8], st[:, 4:8], 1.0 / D, None, ALU.mult))
        b.vdep()
        b.vstep(v.tensor_tensor(st[:, 8:12], st[:, 0:4], st[:, 0:4], ALU.mult))
        b.vdep()
        b.vstep(v.tensor_sub(st[:, 4:8], st[:, 4:8], st[:, 8:12]))
        b.vdep()
        b.vstep(v.tensor_scalar(st[:, 4:8], st[:, 4:8], EPS, None, ALU.add))
        b.vdep()
        b.vstep(v.reciprocal(st[:, 8:12], st[:, 4:8]))
        b.wait(s, "vv", b.n("vv"))
        b.sstep(s.activation(st[:, 12:16], st[:, 8:12], AF.Sqrt))
        b.wait(v, "ss", b.n("ss"))
        b.vdep()
        b.vstep(v.tensor_tensor(st[:, 8:12], st[:, 0:4], st[:, 12:16], ALU.mult))
        b.vdep()
        b.vstep(v.tensor_scalar(st[:, 8:12], st[:, 8:12], -1.0, None, ALU.mult))
        b.wait(s, "vv", b.n("vv"))

    def emit_ln_tile(self, xtile, tt, xn1tile):
        b, s = self.b, self.b.s
        st = self.stat
        b.sdep()
        return b.inc("ln_out", s.activation(
            xn1tile, xtile[:, tt * 1024:(tt + 1) * 1024], AF.Identity,
            bias=st[:, 8 + tt:9 + tt], scale=st[:, 12 + tt:13 + tt]))



def emit_p1(P, K_sb):
    nc, b, dr = P.nc, P.b, P.dr
    t, v, s, g, y = b.t, b.v, b.s, b.g, b.y
    es = ExitStack()
    xstage = [es.enter_context(nc.sbuf_tensor(f"xstage{i}", [128, 4096], F32)).ap() for i in range(2)]
    xnf = [es.enter_context(nc.sbuf_tensor(f"xnf{i}", [128, 1024], F32)).ap() for i in range(2)]
    scr = es.enter_context(nc.sbuf_tensor("scr", [128, 1024], F32)).ap()
    xnTt = es.enter_context(nc.sbuf_tensor("xnTt", [128, 8 * 512], BF16)).ap()
    xnTf = es.enter_context(nc.sbuf_tensor("xnTf", [128, 1024], F32)).ap()
    wqkv = es.enter_context(nc.sbuf_tensor("wqkv", [128, 8 * 3072], BF16)).ap()
    bq_cols = es.enter_context(nc.sbuf_tensor("bq_cols", [128, 24], F32)).ap()
    bv_row = es.enter_context(nc.sbuf_tensor("bv_row", [1, 1024], BF16)).ap()
    rw_sb = es.enter_context(nc.sbuf_tensor("rw_sb", [128, 64], F32)).ap()
    top_sc = es.enter_context(nc.sbuf_tensor("top_sc", [128, 128], F32)).ap()
    vstage = [es.enter_context(nc.sbuf_tensor(f"vstage{i}", [128, 4 * 1040], BF16)).ap() for i in range(2)]
    qstage = [es.enter_context(nc.sbuf_tensor(f"qstage{i}", [128, 512], BF16)).ap() for i in range(2)]
    psT = [es.enter_context(nc.psum_tensor(f"psT{i}", [128, 128], F32)).ap() for i in range(2)]
    psK = [es.enter_context(nc.psum_tensor(f"psK{i}", [128, 512], F32)).ap() for i in range(2)]
    psV = [es.enter_context(nc.psum_tensor(f"psV{i}", [128, 512], F32)).ap() for i in range(2)]
    psR = es.enter_context(nc.psum_tensor("psR", [128, 32], F32)).ap()

    dma3(b, g, wqkv, dr["l0_wqkv"], 3072, "dw")
    b.inc("dw", g.dma_start(out=bq_cols, in_=dr["l0_bqkv_cols"]), 16)
    b.inc("dw", g.dma_start(out=bv_row, in_=dr["l0_bv_row"]), 16)
    dma3(b, g, rw_sb, dr["l0_router_w"], 8, "dw")
    for i in range(2):
        v.memset(vstage[i], 1.0)
    b.wait(t, "dw")
    b.wait(t, "dmisc")
    b.wait(t, "cmisc")
    b.wait(s, "dw")

    ln_done = 0
    ln_rel = {0: 0, 1: 0}
    dx_cnt = {0: 0, 1: 0}
    xnf_rel = {0: 0, 1: 0}
    psT_rel = [0, 0]
    psK_rel = [("kev", 0), ("kev", 0)]
    psV_rel = [0, 0]
    vst_rel = {0: 0, 1: 0}
    qst_rel = {0: 0, 1: 0}
    psR_rel4 = [None, None, None, None]
    pend_rb = []

    def issue_x(j):
        par = j % 2
        sm = "dx" if par == 0 else "dxa"
        b.wait(y, "ln_out", ln_rel[par])
        dma3(b, y, xstage[par], dr["x_abs"][j * 512:(j + 1) * 512, :], 1024, sm)
        dx_cnt[par] = (sm, b.n(sm))

    issue_x(0)
    for j in range(8):
        par = j % 2
        if j + 1 < 8:
            issue_x(j + 1)
        b.wait(v, "dxn", b.n("dxn"))
        P.emit_ln_stats(xstage[par], scr, [dx_cnt[par]])
        for tt in range(4):
            T = j * 4 + tt
            fp_ = tt % 2
            b.wait(s, "pe", xnf_rel[fp_])
            ln_done = P.emit_ln_tile(xstage[par], tt, xnf[fp_])
            if tt == 3:
                ln_rel[par] = ln_done
            for c in range(8):
                tp = c % 2
                b.wait(t, "xe", psT_rel[tp])
                b.wait(t, "ln_out", ln_done)
                pe_cnt = b.inc("pe", t.transpose(psT[tp], xnf[fp_][:, c * 128:(c + 1) * 128], P.ident))
                b.wait(v, "pe", pe_cnt)
                b.inc("xe", v.tensor_copy(xnTt[:, c * 512 + tt * 128: c * 512 + (tt + 1) * 128], psT[tp]))
                psT_rel[tp] = b.inc("xe", v.tensor_copy(xnTf[:, c * 128:(c + 1) * 128], psT[tp]))
            xnf_rel[fp_] = b.n("pe")
            b.wait(t, "xe", b.n("xe"))
            if pend_rb:
                pend_rb.pop(0)()
            slot = psR[:, tt * 8:(tt + 1) * 8]
            stc = top_sc[:, tt * 32:(tt + 1) * 32]
            if psR_rel4[tt] is not None:
                b.wait(t, psR_rel4[tt][0], psR_rel4[tt][1])
            for c in range(8):
                mm = t.matmul(slot, xnTf[:, c * 128:(c + 1) * 128], rw_sb[:, c * 8:(c + 1) * 8],
                              start=(c == 0), stop=(c == 7))
            rmm_cnt = b.inc("rmm", mm)
            b.wait(v, "rmm", rmm_cnt)
            b.vdep()
            b.vstep(v.reduce_max(stc[:, 0:1], slot, AXX))
            b.vdep()
            b.vstep(v.tensor_scalar(stc[:, 8:16], slot, stc[:, 0:1], None, ALU.is_equal))
            b.vdep()
            b.vstep(v.scalar_tensor_tensor(stc[:, 16:24], stc[:, 8:16], -1e30, slot, ALU.mult, ALU.add))
            b.vdep()
            b.vstep(v.reduce_max(stc[:, 1:2], stc[:, 16:24], AXX))
            b.vstep(v.tensor_scalar(stc[:, 2:3], stc[:, 0:1], -1.0, None, ALU.mult))
            b.wait(s, "vv", b.n("vv"))
            b.sstep(s.activation(stc[:, 8:16], slot, AF.Exp, bias=stc[:, 2:3]))
            rex_ss = b.n("ss")

            def mk_routB(tt=tt, T=T, slot=slot, stc=stc, rex_ss=rex_ss):
                def f():
                    b.wait(v, "ss", rex_ss)
                    b.vdep()
                    b.vstep(v.tensor_scalar(stc[:, 16:24], slot, stc[:, 1:2], None, ALU.is_ge))
                    b.vdep()
                    b.vstep(v.tensor_tensor(stc[:, 8:16], stc[:, 8:16], stc[:, 16:24], ALU.mult))
                    b.vdep()
                    b.vstep(v.reduce_sum(stc[:, 3:4], stc[:, 8:16], AXX))
                    b.vdep()
                    b.vstep(v.reciprocal(stc[:, 4:5], stc[:, 3:4]))
                    b.vdep()
                    b.vstep(v.tensor_scalar(P.wsc[:, T * 8:(T + 1) * 8], stc[:, 8:16],
                                            stc[:, 4:5], 1.0 / 1.702, ALU.mult, ALU.mult))
                    psR_rel4[tt] = ("vv", b.n("vv"))
                return f
            pend_rb.append(mk_routB())
        b.wait(y, "xe", b.n("xe"))
        b.inc("dxn", y.dma_start(
            out=P.xnt_dram[j],
            in_=xnTt.rearrange("p (c f) -> p c f", f=512)), 16)
        for hp in range(8):
            kp = hp % 2
            b.wait(t, psK_rel[kp][0], psK_rel[kp][1])
            for c in range(8):
                mm = t.matmul(psK[kp], wqkv[:, c * 3072 + 1024 + hp * 128: c * 3072 + 1024 + (hp + 1) * 128],
                              xnTt[:, c * 512:(c + 1) * 512], start=(c == 0), stop=(c == 7))
            pe_cnt = b.inc("pe", mm)
            b.wait(v, "pe", pe_cnt)
            psK_rel[kp] = ("kev", b.inc("kev", v.tensor_scalar(
                K_sb[:, hp * 4096 + j * 512: hp * 4096 + (j + 1) * 512],
                psK[kp], bq_cols[:, 8 + hp:9 + hp], None, ALU.add)))
        b.wait(v, f"dv{par}", vst_rel[par])
        for kk in range(4):
            for vh in range(2):
                vp = (kk * 2 + vh) % 2
                b.wait(t, "vev", psV_rel[vp])
                for c in range(8):
                    t.matmul(psV[vp], xnTt[:, c * 512 + kk * 128: c * 512 + (kk + 1) * 128],
                             wqkv[:, c * 3072 + 2048 + vh * 512: c * 3072 + 2048 + (vh + 1) * 512],
                             start=(c == 0), stop=False)
                mm = t.matmul(psV[vp], P.ones_row, bv_row[0:1, vh * 512:(vh + 1) * 512],
                              start=False, stop=True)
                pe_cnt = b.inc("pe", mm)
                b.wait(v, "pe", pe_cnt)
                src = psV[vp].rearrange("p (h d) -> p h d", d=64)
                for gsub in range(4):
                    hg = vh * 4 + gsub
                    dst = vstage[par][:, kk * 1040 + hg * 130: kk * 1040 + (hg + 1) * 130]
                    dst = dst.rearrange("p (h e) -> p h e", e=65)[:, :, 0:64]
                    cnt = b.inc("vev", v.tensor_copy(dst, src[:, gsub * 2:gsub * 2 + 2, :]))
                psV_rel[vp] = cnt
        b.wait(y, "vev", b.n("vev"))
        for hg in range(8):
            src = vstage[par].rearrange("p (k g e) -> p k g e", g=8, e=130)[:, :, hg, :]
            b.inc(f"dv{par}", y.dma_start(out=P.v_dram[hg, :, j * 4:(j + 1) * 4, :], in_=src), 16)
        vst_rel[par] = b.n(f"dv{par}")
        for hp in range(8):
            qp = hp % 2
            b.wait(t, psK_rel[qp][0], psK_rel[qp][1])
            for c in range(8):
                mm = t.matmul(psK[qp], wqkv[:, c * 3072 + hp * 128: c * 3072 + (hp + 1) * 128],
                              xnTt[:, c * 512:(c + 1) * 512], start=(c == 0), stop=(c == 7))
            pe_cnt = b.inc("pe", mm)
            b.wait(v, "pe", pe_cnt)
            b.wait(v, f"dq{qp}", qst_rel[qp])
            cnt = b.inc("qev", v.tensor_scalar(qstage[qp], psK[qp],
                                               bq_cols[:, hp:hp + 1], None, ALU.add))
            psK_rel[qp] = ("qev", cnt)
            b.wait(y, "qev", b.n("qev"))
            b.inc(f"dq{qp}", y.dma_start(out=P.q_dram[hp, :, j, :], in_=qstage[qp]), 16)
            qst_rel[qp] = b.n(f"dq{qp}")
    while pend_rb:
        pend_rb.pop(0)()
    es.close()
    b.barrier()



def emit_p2(P, K_sb):
    nc, b, dr = P.nc, P.b, P.dr
    t, v, s, g, y = b.t, b.v, b.s, b.g, b.y
    es = ExitStack()
    tri = es.enter_context(nc.sbuf_tensor("tri", [128, 2048], BF16)).ap()
    wproj = es.enter_context(nc.sbuf_tensor("wproj", [128, 8 * 1024], BF16)).ap()
    bp_row = es.enter_context(nc.sbuf_tensor("bp_row", [1, 1024], BF16)).ap()
    y_sb = es.enter_context(nc.sbuf_tensor("y_sb", [128, 8 * 512], BF16)).ap()
    exp2 = [es.enter_context(nc.sbuf_tensor(f"exp2_{i}", [128, 1024], BF16)).ap() for i in range(2)]
    vbuf = [es.enter_context(nc.sbuf_tensor(f"vbuf{i}", [128, 32 * 130], BF16)).ap() for i in range(2)]
    qbuf = [es.enter_context(nc.sbuf_tensor(f"qbuf{i}", [128, 512], BF16)).ap() for i in range(2)]
    bc_sb = es.enter_context(nc.sbuf_tensor("bc_sb", [64, 512], BF16)).ap()
    srow = [es.enter_context(nc.sbuf_tensor(f"srowp{i}", [1, 1024], F32)).ap() for i in range(2)]
    xsm = [es.enter_context(nc.sbuf_tensor(f"xsm{i}", [128, 1024], F32)).ap() for i in range(2)]
    psSd = [es.enter_context(nc.psum_tensor(f"psSd{i}", [128, 1024], F32)).ap() for i in range(2)]
    psAV = [es.enter_context(nc.psum_tensor(f"psAVd{i}", [65, 512], F32)).ap() for i in range(2)]
    psOB = es.enter_context(nc.psum_tensor("psOB", [128, 512], F32)).ap()
    psB = psOB[0:64, :]
    b.inc("dw3", g.dma_start(out=tri, in_=dr["l0_tri"]), 16)
    dma3(b, g, wproj, dr["l0_wproj"], 1024, "dw3")
    b.inc("dw3", g.dma_start(out=bp_row, in_=dr["l0_bproj_row"]), 16)
    b.wait(t, "dw3")
    b.wait(v, "dw3")

    exp_rel = {0: ("act", 0), 1: ("act", 0)}
    expbuf_rel = {0: 0, 1: 0}
    vbuf_rel = {0: 0, 1: 0}
    qbuf_rel = {0: 0, 1: 0}
    dvl_cnt = {0: 0, 1: 0}
    dql_cnt = {0: 0, 1: 0}
    psO_rel4 = {0: 0, 1: 0, 2: 0, 3: 0}
    oad_rel2 = {0: 0, 1: 0}
    dx1_rel = {0: 0, 1: 0}
    P._p2_psO_rel = 0
    P._p2_pending = []
    P._p2_avwar = {0: 0, 1: 0}
    P._p2_state = {"srow_war": 0, "psb_war": 0, "bce_war": 0, "vv_rec": 0}

    def make_epi1(av_all, par):
        def epi():
            st = P._p2_state
            pav = psAV[par]
            b.wait(v, "av", av_all)
            b.wait(v, "pe", st["srow_war"])
            b.vstep(v.tensor_scalar(srow[par][0:1, 0:512], pav[64:65, :], 1e-30, None, ALU.max))
            b.vdep()
            b.vstep(v.reciprocal(srow[par][0:1, 512:1024], srow[par][0:1, 0:512]))
            st["vv_rec"] = b.n("vv")
        return epi

    def make_epi2(hh, hg, par):
        def epi():
            st = P._p2_state
            pav = psAV[par]
            b.wait(t, "vv", st["vv_rec"])
            b.wait(t, "bce", st["psb_war"])
            b.wait(t, "oadd", P._p2_psO_rel)
            mmb = t.matmul(psB, P.ones_row_f[0:1, 0:64], srow[par][0:1, 512:1024], start=True, stop=True)
            st["srow_war"] = peb = b.inc("pe", mmb)
            b.wait(s, "pe", peb)
            b.wait(s, "ynorm", st["bce_war"])
            st["psb_war"] = b.inc("bce", s.copy(bc_sb, psB))
            b.wait(v, "bce", st["psb_war"])
            yn = v.tensor_tensor(y_sb[64 * hh:64 * hh + 64, hg * 512:(hg + 1) * 512],
                                 pav[0:64, :], bc_sb, ALU.mult)
            st["bce_war"] = b.inc("ynorm", yn)
            P._p2_avwar[par] = st["bce_war"]
        return epi

    for sl in range(8):
        KT = KT_FULL[sl]
        plist = [(hg, hh, kt2) for hg in range(8) for hh in range(2)
                 for kt2 in range(0, KT, 2)]

        def issue_hg(isl, hg):
            gp = hg % 2
            ktl = KT_FULL[isl]
            b.wait(y, "av", vbuf_rel[gp])
            b.inc(f"dvl{gp}", y.dma_start(
                out=vbuf[gp][:, 0:ktl * 130].rearrange("p (k e) -> p k e", e=130),
                in_=P.v_dram[hg, :, 0:ktl, :]), 16)
            dvl_cnt[gp] = b.n(f"dvl{gp}")
            b.wait(y, "pe", qbuf_rel[gp])
            b.inc(f"dql{gp}", y.dma_start(out=qbuf[gp], in_=P.q_dram[hg, :, isl, :]), 16)
            dql_cnt[gp] = b.n(f"dql{gp}")

        if sl == 0:
            issue_hg(0, 0)
            issue_hg(0, 1)

        def emit_S(j):
            hg, hh, kt2 = plist[j]
            gp = hg % 2
            sp = j % 2
            b.wait(t, exp_rel[sp][0], exp_rel[sp][1])
            if hh == 0 and kt2 == 0:
                b.wait(t, f"dql{gp}", dql_cnt[gp])
            mm = None
            for u in (0, 1):
                kt = kt2 + u
                mm = t.matmul(psSd[sp][:, u * 512:(u + 1) * 512],
                              K_sb[64 * hh:64 * hh + 64, hg * 4096 + kt * 128: hg * 4096 + (kt + 1) * 128],
                              qbuf[gp][64 * hh:64 * hh + 64, :],
                              start=True, stop=True)
            qbuf_rel[gp] = b.inc("pe", mm)
            return b.n("pe")

        pe_S = {0: emit_S(0)}
        for j, (hg, hh, kt2) in enumerate(plist):
            gp = hg % 2
            sp = j % 2
            b.wait(s, "pe", pe_S[j])
            b.wait(s, "av", expbuf_rel[sp])
            e_in = s.activation(exp2[sp], psSd[sp], AF.Exp, scale=0.125)
            act_cnt = b.inc("act", e_in)
            exp_rel[sp] = ("act", act_cnt)
            wait_for_av = ("act", act_cnt)
            for u in (0, 1):
                jj = kt2 + u - (KT - 4)
                if jj >= 0:
                    b.wait(v, "act", act_cnt)
                    b.wait(v, "av", expbuf_rel[sp])
                    mk = v.tensor_tensor(exp2[sp][:, u * 512:(u + 1) * 512],
                                         exp2[sp][:, u * 512:(u + 1) * 512],
                                         tri[:, jj * 512:(jj + 1) * 512], ALU.mult)
                    dve_cnt = b.inc("dve", mk)
                    exp_rel[sp] = ("dve", dve_cnt)
                    wait_for_av = ("dve", dve_cnt)
            if j + 1 < len(plist):
                pe_S[j + 1] = emit_S(j + 1)
            while P._p2_pending and P._p2_pending[0][0] <= j:
                P._p2_pending.pop(0)[1]()
            b.wait(t, wait_for_av[0], wait_for_av[1])
            if kt2 == 0:
                b.wait(t, f"dvl{gp}", dvl_cnt[gp])
                b.wait(t, "ynorm", P._p2_avwar[hh])
            mm = None
            for u in (0, 1):
                kt = kt2 + u
                mm = t.matmul(psAV[hh], vbuf[gp][:, kt * 130 + hh * 65: kt * 130 + hh * 65 + 65],
                              exp2[sp][:, u * 512:(u + 1) * 512],
                              start=(kt == 0), stop=(kt == KT - 1))
            expbuf_rel[sp] = b.inc("av", mm)
            vbuf_rel[gp] = b.n("av")
            if kt2 == KT - 2:
                P._p2_pending.append((j + 1, make_epi1(b.n("av"), hh)))
                P._p2_pending.append((j + 3, make_epi2(hh, hg, hh)))
                if hh == 1:
                    if hg + 2 < 8:
                        issue_hg(sl, hg + 2)
                    elif sl + 1 < 8:
                        issue_hg(sl + 1, hg - 6)
        while P._p2_pending:
            P._p2_pending.pop(0)[1]()
        b.wait(t, "ynorm", b.n("ynorm"))
        for sp_ in (0, 1):
            b.wait(t, exp_rel[sp_][0], exp_rel[sp_][1])
        for tt in range(4):
            tp_ = tt % 2
            dsm = "dx" if tp_ == 0 else "dxa"
            ssm = "dx1" if tp_ == 0 else "dxb"
            b.wait(y, "oadd", oad_rel2[tp_])
            b.wait(y, ssm, dx1_rel[tp_])
            b.inc(dsm, y.dma_start(
                out=xsm[tp_], in_=dr["x_abs"][(sl * 4 + tt) * 128:(sl * 4 + tt + 1) * 128, :]), 16)
            dx_t = b.n(dsm)
            for dh in range(2):
                psO_ = psSd[tp_][:, dh * 512:(dh + 1) * 512]
                b.wait(t, "oadd", psO_rel4[tp_ * 2 + dh])
                for c in range(8):
                    t.matmul(psO_, y_sb[:, c * 512 + tt * 128: c * 512 + (tt + 1) * 128],
                             wproj[:, c * 1024 + dh * 512: c * 1024 + (dh + 1) * 512],
                             start=(c == 0), stop=False)
                mm = t.matmul(psO_, P.ones_row, bp_row[0:1, dh * 512:(dh + 1) * 512],
                              start=False, stop=True)
                pe_cnt = b.inc("pe", mm)
                b.wait(v, "pe", pe_cnt)
                b.wait(v, dsm, dx_t)
                ad = v.tensor_tensor(xsm[tp_][:, dh * 512:(dh + 1) * 512],
                                     xsm[tp_][:, dh * 512:(dh + 1) * 512], psO_, ALU.add)
                psO_rel4[tp_ * 2 + dh] = b.inc("oadd", ad)
            oad_rel2[tp_] = b.n("oadd")
            b.wait(y, "oadd", oad_rel2[tp_])
            b.inc(ssm, y.dma_start(
                out=P.x1_dram[(sl * 4 + tt) * 128:(sl * 4 + tt + 1) * 128, :], in_=xsm[tp_]), 16)
            dx1_rel[tp_] = b.n(ssm)
        exp_rel[0] = ("oadd", psO_rel4[1])
        exp_rel[1] = ("oadd", psO_rel4[3])
    es.close()
    b.barrier()



def emit_p3(P):
    nc, b, dr = P.nc, P.b, P.dr
    t, v, s, g, y = b.t, b.v, b.s, b.g, b.y
    es = ExitStack()
    xnb = [es.enter_context(nc.sbuf_tensor(f"xnb{i}", [128, 8 * 512], BF16)).ap() for i in range(2)]
    w13 = [es.enter_context(nc.sbuf_tensor(f"w13_{i}", [128, 8 * 2048], BF16)).ap() for i in range(2)]
    w2 = [es.enter_context(nc.sbuf_tensor(f"w2_sb{i}", [128, 8 * 1024], BF16)).ap() for i in range(2)]
    act_sb = es.enter_context(nc.sbuf_tensor("act_sb", [128, 8 * 512], BF16)).ap()
    b13 = es.enter_context(nc.sbuf_tensor("b13", [128, 128], F32)).ap()
    b13u = es.enter_context(nc.sbuf_tensor("b13u", [128, 64], F32)).ap()
    b2row = [es.enter_context(nc.sbuf_tensor(f"b2row{i}", [1, 1024], BF16)).ap() for i in range(2)]
    g_s = es.enter_context(nc.sbuf_tensor("g_s", [128, 512], F32)).ap()
    s_s = es.enter_context(nc.sbuf_tensor("s_s", [128, 512], F32)).ap()
    u_s = es.enter_context(nc.sbuf_tensor("u_s", [128, 512], F32)).ap()
    contrib = [es.enter_context(nc.sbuf_tensor(f"contrib{i}", [128, 1024], F32)).ap() for i in range(4)]
    psA = [es.enter_context(nc.psum_tensor(f"psA_{i}", [128, 512], F32)).ap() for i in range(2)]
    psB2 = [es.enter_context(nc.psum_tensor(f"psB2_{i}", [128, 512], F32)).ap() for i in range(2)]
    psM = [es.enter_context(nc.psum_tensor(f"psM{i}", [128, 512], F32)).ap() for i in range(4)]

    b.inc("dmo", g.dma_start(out=b13, in_=dr["l0_b13_cols"]), 16)
    b.wait(v, "dmo", 16)
    b.vstep(v.tensor_scalar(
        b13u.rearrange("p (e f) -> p e f", f=8),
        b13.rearrange("p (e f) -> p e f", f=16)[:, :, 8:16],
        1.0, None, ALU.add))
    b13u_vv = b.n("vv")

    w13_rel = [0, 0]
    dma3(b, g, w13[0], dr["l0_w13"][0], 2048, "dm13_0")
    dma3(b, g, w2[0], dr["l0_w2"][0], 1024, "dmo2_0")
    b.inc("dme2_0", g.dma_start(out=b2row[0], in_=dr["l0_b2_rows"][0:1, 0:1024]), 16)
    psA_rel = [0, 0]
    psB2_rel = [0, 0]
    w2_rel = [0, 0]
    b2_rel = [0, 0]
    psM_rel = [0, 0, 0, 0]
    act_war = 0
    gs_war = 0
    ss_war = 0
    acc_reg = {}
    xnb_rel = [0, 0]
    cb_rel = [0, 0, 0, 0]
    dxnl_cnt = {0: 0, 1: 0}

    def issue_xnb(xp, tc):
        b.wait(y, "pe", xnb_rel[xp])
        b.inc(f"dxnl{xp}", y.dma_start(
            out=xnb[xp].rearrange("p (c f) -> p c f", f=512),
            in_=P.xnt_dram[tc]), 16)
        dxnl_cnt[xp] = b.n(f"dxnl{xp}")

    for e in range(8):
        ep = e % 2
        if e + 1 < 8:
            b.wait(g, "pe", w13_rel[1 - ep])
            dma3(b, g, w13[1 - ep], dr["l0_w13"][e + 1], 2048, f"dm13_{1 - ep}")
            b.wait(g, "pe", w2_rel[1 - ep])
            dma3(b, g, w2[1 - ep], dr["l0_w2"][e + 1], 1024, f"dmo2_{1 - ep}")
            b.wait(g, "pe", b2_rel[1 - ep])
            b.inc(f"dme2_{1 - ep}", g.dma_start(
                out=b2row[1 - ep], in_=dr["l0_b2_rows"][0:1, (e + 1) * 1024:(e + 2) * 1024]), 16)
        b.wait(t, f"dm13_{ep}", b.n(f"dm13_{ep}"))
        b.wait(t, f"dmo2_{ep}", b.n(f"dmo2_{ep}"))
        b.wait(t, f"dme2_{ep}", b.n(f"dme2_{ep}"))
        for tc in range(8):
            xp = tc % 2
            if e == 0 and tc == 0:
                issue_xnb(0, 0)
            b.wait(t, f"dxnl{xp}", dxnl_cnt[xp])
            for ip in range(8):
                if ip == 3:
                    if tc + 1 < 8:
                        issue_xnb(1 - xp, tc + 1)
                    elif e + 1 < 8:
                        issue_xnb(1 - xp, 0)
                ipp = ip % 2
                b.wait(t, "mgl", psA_rel[ipp])
                for c in range(8):
                    mm = t.matmul(psA[ipp], w13[ep][:, c * 2048 + ip * 128: c * 2048 + (ip + 1) * 128],
                                  xnb[xp][:, c * 512:(c + 1) * 512], start=(c == 0), stop=(c == 7))
                peA = b.inc("pe", mm)
                b.wait(t, "mup", psB2_rel[ipp])
                for c in range(8):
                    mm = t.matmul(psB2[ipp], w13[ep][:, c * 2048 + 1024 + ip * 128: c * 2048 + 1024 + (ip + 1) * 128],
                                  xnb[xp][:, c * 512:(c + 1) * 512], start=(c == 0), stop=(c == 7))
                peB = b.inc("pe", mm)
                w13_rel[ep] = peB
                b.wait(v, "pe", peA)
                b.wait(v, "sil", gs_war)
                psA_rel[ipp] = b.inc("mgl", v.tensor_scalar(
                    g_s, psA[ipp], b13[:, e * 16 + ip:e * 16 + ip + 1], 7.0, ALU.add, ALU.min))
                b.wait(s, "mgl", psA_rel[ipp])
                b.wait(s, "mact", ss_war)
                gs_war = b.inc("sil", s.activation(s_s, g_s, AF.Silu, scale=1.702))
                b.vdep(b13u_vv)
                b.wait(v, "mact", ss_war)
                b.wait(v, "pe", peB)
                b.wait(v, "pe", act_war)
                psB2_rel[ipp] = b.inc("mup", v.tensor_scalar(
                    u_s, psB2[ipp], b13u[:, e * 8 + ip:e * 8 + ip + 1], 8.0, ALU.add, ALU.min))
                b.wait(v, "sil", gs_war)
                b.wait(v, "mup", psB2_rel[ipp])
                ss_war = b.inc("mact", v.scalar_tensor_tensor(
                    act_sb[:, ip * 512:(ip + 1) * 512], u_s, -6.0, s_s, ALU.max, ALU.mult))
            xnb_rel[xp] = b.n("pe")
            b.wait(t, "mact", b.n("mact"))
            for tt in range(4):
                T = tc * 4 + tt
                cp = tt
                for dh in range(2):
                    mp = (tt * 2 + dh) % 4
                    b.wait(t, "macc", psM_rel[mp])
                    for ic in range(8):
                        t.matmul(psM[mp], act_sb[:, ic * 512 + tt * 128: ic * 512 + (tt + 1) * 128],
                                 w2[ep][:, ic * 1024 + dh * 512: ic * 1024 + (dh + 1) * 512],
                                 start=(ic == 0), stop=False)
                    mm = t.matmul(psM[mp], P.row1702,
                                  b2row[ep][0:1, dh * 512:(dh + 1) * 512], start=False, stop=True)
                    peM = b.inc("pe", mm)
                    b.wait(s, "pe", peM)
                    if cb_rel[cp] != 0:
                        b.wait(s, cb_rel[cp][0], cb_rel[cp][1])
                    psM_rel[mp] = b.inc("macc", s.activation(
                        contrib[cp][:, dh * 512:(dh + 1) * 512], psM[mp], AF.Identity,
                        scale=P.wsc[:, T * 8 + e: T * 8 + e + 1]))
                b.wait(g, "macc", b.n("macc"))
                dsem = f"dacc{T % 16}"
                b.wait(g, dsem, b.n(dsem))
                b.inc(dsem, g.dma_start(
                    out=P.x1_dram[T * 128:(T + 1) * 128, :], in_=contrib[cp],
                    accum_op=ALU.add), 16)
                cb_rel[cp] = (dsem, b.n(dsem))
            act_war = b.n("pe")
        w2_rel[ep] = b2_rel[ep] = b.n("pe")
    es.close()
    b.barrier()



def emit_p4(P, xn1T):
    nc, b, dr = P.nc, P.b, P.dr
    t, v, s, g, y = b.t, b.v, b.s, b.g, b.y
    es = ExitStack()
    xA = [es.enter_context(nc.sbuf_tensor(f"xA{i}", [128, 4096], F32)).ap() for i in range(2)]
    xB = [es.enter_context(nc.sbuf_tensor(f"xB{i}", [128, 4096], F32)).ap() for i in range(2)]
    xnf = [es.enter_context(nc.sbuf_tensor(f"xnf4{i}", [128, 1024], F32)).ap() for i in range(2)]
    scr = es.enter_context(nc.sbuf_tensor("scr4", [128, 1024], F32)).ap()
    xnTf = es.enter_context(nc.sbuf_tensor("xnTf4", [128, 1024], F32)).ap()
    gwT = es.enter_context(nc.sbuf_tensor("gwT", [128, 8 * 16], F32)).ap()
    gb_sb = es.enter_context(nc.sbuf_tensor("gb_sb", [128, 16], F32)).ap()
    gbrow = es.enter_context(nc.sbuf_tensor("gbrow", [1, 16], F32)).ap()
    tsc = es.enter_context(nc.sbuf_tensor("tsc", [128, 80], F32)).ap()
    psT = [es.enter_context(nc.psum_tensor(f"psT4_{i}", [128, 128], F32)).ap() for i in range(2)]
    psG = es.enter_context(nc.psum_tensor("psG", [128, 16], F32)).ap()
    dma3(b, g, gwT, dr["l1_gate_wT"], 16, "dl1")
    b.inc("dl1", g.dma_start(out=gbrow, in_=dr["l1_gate_b_row"]), 16)
    b.wait(t, "dl1")
    b.wait(t, "cmisc")
    b.wait(t, "dmisc")
    mmgb = t.matmul(psG, P.ones_row_f[0:1, :], gbrow[0:1, :], start=True, stop=True)
    pegb = b.inc("pe", mmgb)
    b.wait(v, "pe", pegb)
    b.vstep(v.tensor_copy(gb_sb, psG))
    gb_vv = b.n("vv")
    b.wait(v, "dmisc")

    psT_rel = [0, 0]
    psG_rel = ("vv", gb_vv)
    ln_done = 0
    ln_rel4 = {0: 0, 1: 0}
    sel_rel4 = {0: 0, 1: 0}
    xnf_rel4 = {0: 0, 1: 0}
    dxa_cnt = {0: 0, 1: 0}
    dxb_cnt = {0: 0, 1: 0}

    def issue_x14(tloc):
        pp = tloc % 2
        offA = 512 * tloc
        offB = (2048 + 512 * tloc) % 4096
        sma = "dxb" if pp == 0 else "dv0"
        smb = "dv1" if pp == 0 else "dq1"
        b.wait(y, "vv", sel_rel4[pp])
        b.wait(y, "ln_out", ln_rel4[pp])
        dma3(b, y, xA[pp], P.x1_dram[offA:offA + 512, :], 1024, sma)
        dma3(b, y, xB[pp], P.x1_dram[offB:offB + 512, :], 1024, smb)
        dxa_cnt[pp] = (sma, b.n(sma))
        dxb_cnt[pp] = (smb, b.n(smb))

    issue_x14(0)
    for tloc in range(8):
        pp = tloc % 2
        if tloc + 1 < 8:
            issue_x14(tloc + 1)
        b.wait(v, dxa_cnt[pp][0], dxa_cnt[pp][1])
        b.wait(v, dxb_cnt[pp][0], dxb_cnt[pp][1])
        b.wait(v, "ln_out", b.n("ln_out"))
        b.vdep()
        b.vstep(v.tensor_scalar(xA[pp], xA[pp], P.hsel[:, 0:1], None, ALU.mult))
        b.vdep()
        b.vstep(v.scalar_tensor_tensor(xA[pp], xB[pp], P.hsel[:, 1:2], xA[pp], ALU.mult, ALU.add))
        sel_rel4[pp] = b.n("vv")
        P.emit_ln_stats(xA[pp], scr, [("vv", sel_rel4[pp])])
        for tt in range(4):
            T = tloc * 4 + tt
            fp_ = tt % 2
            b.wait(s, "pe", xnf_rel4[fp_])
            ln_done = P.emit_ln_tile(xA[pp], tt, xnf[fp_])
            if tt == 3:
                ln_rel4[pp] = ln_done
            for c in range(8):
                tp = c % 2
                b.wait(t, "xe4", psT_rel[tp])
                b.wait(t, "ln_out", ln_done)
                pe_cnt = b.inc("pe", t.transpose(psT[tp], xnf[fp_][:, c * 128:(c + 1) * 128], P.ident))
                b.wait(v, "pe", pe_cnt)
                psT_rel[tp] = b.inc("xe4", v.tensor_copy(
                    xn1T[:, c * 4096 + T * 128: c * 4096 + (T + 1) * 128], psT[tp]))
                if tloc < 4:
                    psT_rel[tp] = b.inc("xe4", v.tensor_copy(xnTf[:, c * 128:(c + 1) * 128], psT[tp]))
            xnf_rel4[fp_] = b.n("pe")
            if tloc >= 4:
                continue
            b.wait(t, "xe4", b.n("xe4"))
            b.wait(t, psG_rel[0], psG_rel[1])
            for c in range(8):
                mm = t.matmul(psG, xnTf[:, c * 128:(c + 1) * 128], gwT[:, c * 16:(c + 1) * 16],
                              start=(c == 0), stop=(c == 7))
            gmm = b.inc("rmm", mm)
            b.wait(s, "rmm", gmm)
            b.wait(s, "vv", getattr(P, "_l1tsc", 0))
            b.sstep(s.activation(tsc[:, 0:16], psG, AF.Sigmoid))
            psG_rel = ("ss", b.n("ss"))
            b.wait(v, "ss", b.n("ss"))
            b.vdep(gb_vv)
            b.vstep(v.tensor_tensor(tsc[:, 16:32], tsc[:, 0:16], gb_sb, ALU.add))
            b.vdep()
            b.vstep(v.tensor_copy(tsc[:, 32:48], tsc[:, 16:32]))
            for it in range(4):
                b.vdep()
                b.vstep(v.reduce_max(tsc[:, 64 + it:65 + it], tsc[:, 32:48], AXX))
                if it < 3:
                    b.vdep()
                    b.vstep(v.tensor_scalar(tsc[:, 48:64], tsc[:, 32:48], tsc[:, 64 + it:65 + it], None, ALU.is_equal))
                    b.vdep()
                    b.vstep(v.scalar_tensor_tensor(tsc[:, 32:48], tsc[:, 48:64], -1e30, tsc[:, 32:48], ALU.mult, ALU.add))
            b.vdep()
            b.vstep(v.tensor_scalar(tsc[:, 48:64], tsc[:, 16:32], tsc[:, 67:68], None, ALU.is_ge))
            b.vdep()
            b.vstep(v.tensor_tensor(tsc[:, 32:48], tsc[:, 0:16], tsc[:, 48:64], ALU.mult))
            b.vdep()
            b.vstep(v.reduce_sum(tsc[:, 68:69], tsc[:, 32:48], AXX))
            b.vdep()
            b.vstep(v.reciprocal(tsc[:, 69:70], tsc[:, 68:69]))
            b.vdep()
            b.vstep(v.tensor_scalar(P.comb[:, T * 16:(T + 1) * 16], tsc[:, 32:48],
                                    tsc[:, 69:70], 2.5, ALU.mult, ALU.mult))
            P._l1tsc = b.n("vv")
    es.close()
    b.barrier()


def emit_p4b(P, xn1T, pooled, rk_sb, rv_sb):
    nc, b, dr = P.nc, P.b, P.dr
    t, v, s, g, y = b.t, b.v, b.s, b.g, b.y
    es = ExitStack()
    rkvw = es.enter_context(nc.sbuf_tensor("rkvw", [128, 8 * 512], BF16)).ap()
    rkb = es.enter_context(nc.sbuf_tensor("rkb", [128, 2], F32)).ap()
    rvb = es.enter_context(nc.sbuf_tensor("rvb", [1, 256], BF16)).ap()
    psP1 = es.enter_context(nc.psum_tensor("psP1", [128, 512], F32)).ap()
    dma3(b, g, rkvw, dr["l1_rkv_w"], 512, "drkv")
    b.inc("drkv", g.dma_start(out=rkb, in_=dr["l1_rk_b_cols"]), 16)
    b.inc("drkv", g.dma_start(out=rvb, in_=dr["l1_rv_b_row"]), 16)
    b.wait(t, "drkv")
    b.inc("rvones", v.memset(rv_sb, 1.0))
    b.wait(t, "rvones")
    poolf = es.enter_context(nc.sbuf_tensor("poolf", [128, 64], F32)).ap()
    for c in range(8):
        src = xn1T[:, c * 4096:(c + 1) * 4096].rearrange("p (l k) -> p l k", k=64)
        b.vdep()
        b.vstep(v.reduce_sum(poolf, src, AXX))
        b.vdep()
        b.vstep(v.tensor_scalar(pooled[:, c * 64:(c + 1) * 64],
                                poolf, 1.0 / 64.0, None, ALU.mult))
    pooled_bf = rv_sb
    b.wait(t, "vv", b.n("vv"))
    for rt in range(2):
        b.wait(t, "rkev", b.n("rkev"))
        for c in range(8):
            mm = t.matmul(psP1[:, 0:64], rkvw[:, c * 512 + rt * 128: c * 512 + (rt + 1) * 128],
                          pooled[:, c * 64:(c + 1) * 64], start=(c == 0), stop=(c == 7))
        pe_cnt = b.inc("pe", mm)
        b.wait(s, "pe", pe_cnt)
        b.inc("rkev", s.activation(rk_sb[:, rt * 64:(rt + 1) * 64], psP1[:, 0:64],
                                   AF.Identity, bias=rkb[:, rt:rt + 1]))
    b.wait(t, "rkev", b.n("rkev"))
    for c in range(8):
        t.matmul(psP1[0:64, 0:256], pooled[:, c * 64:(c + 1) * 64],
                 rkvw[:, c * 512 + 256: c * 512 + 512], start=(c == 0), stop=False)
    mm = t.matmul(psP1[0:64, 0:256], P.ones_row[0:1, 0:64], rvb[0:1, :], start=False, stop=True)
    pe_cnt = b.inc("pe", mm)
    b.wait(v, "pe", pe_cnt)
    dstv = rv_sb.rearrange("p (h e) -> p h e", e=65)[:, :, 0:64]
    b.vstep(v.tensor_copy(dstv, psP1[0:64, 0:256].rearrange("p (h d) -> p h d", d=64)))
    b.barrier()
    es.close()



def emit_p5(P, xn1T, pooled, rk_sb, rv_sb):
    nc, b, dr = P.nc, P.b, P.dr
    t, v, s, g, y = b.t, b.v, b.s, b.g, b.y
    es = ExitStack()
    wkv1 = es.enter_context(nc.sbuf_tensor("wkv1", [128, 8 * 2048], BF16)).ap()
    wq1 = es.enter_context(nc.sbuf_tensor("wq1", [128, 8 * 1024], BF16)).ap()
    rqw = es.enter_context(nc.sbuf_tensor("rqw", [128, 8 * 256], BF16)).ap()
    routw = es.enter_context(nc.sbuf_tensor("routw", [128, 2 * 1024], BF16)).ap()
    bq1 = es.enter_context(nc.sbuf_tensor("bq1", [128, 24], F32)).ap()
    bv1 = es.enter_context(nc.sbuf_tensor("bv1", [1, 1024], BF16)).ap()
    rqb = es.enter_context(nc.sbuf_tensor("rqb", [128, 2], F32)).ap()
    routb = es.enter_context(nc.sbuf_tensor("routb", [128, 8], F32)).ap()
    glb = es.enter_context(nc.sbuf_tensor("glb", [128, 2], BF16)).ap()
    grb = es.enter_context(nc.sbuf_tensor("grb", [128, 2], BF16)).ap()
    mloc4 = es.enter_context(nc.sbuf_tensor("mloc4", [128, 4096], BF16)).ap()
    mlat = es.enter_context(nc.sbuf_tensor("mlat", [64, 2048], BF16)).ap()
    K1 = es.enter_context(nc.sbuf_tensor("K1", [128, 8 * 640], BF16)).ap()
    V1 = es.enter_context(nc.sbuf_tensor("V1", [128, 5 * 1040], BF16)).ap()
    Q1 = es.enter_context(nc.sbuf_tensor("Q1", [128, 8 * 512], BF16)).ap()
    rq1 = es.enter_context(nc.sbuf_tensor("rq1", [128, 2 * 512], BF16)).ap()
    lat1 = es.enter_context(nc.sbuf_tensor("lat1", [128, 2 * 512], BF16)).ap()
    rexp = es.enter_context(nc.sbuf_tensor("rexp", [64, 512], BF16)).ap()
    exp4 = [es.enter_context(nc.sbuf_tensor(f"exp4_{i}", [128, 1024], BF16)).ap() for i in range(2)]
    loc1 = [es.enter_context(nc.sbuf_tensor(f"loc1_{i}", [128, 512], BF16)).ap() for i in range(2)]
    rem1 = es.enter_context(nc.sbuf_tensor("rem1", [128, 512], BF16)).ap()
    d1f = es.enter_context(nc.sbuf_tensor("d1f", [128, 512], F32)).ap()
    bc1 = es.enter_context(nc.sbuf_tensor("bc1", [64, 512], BF16)).ap()
    g1 = es.enter_context(nc.sbuf_tensor("g1", [65, 512], F32)).ap()
    srow1 = es.enter_context(nc.sbuf_tensor("srow1", [1, 1024], F32)).ap()
    mxo = [es.enter_context(nc.sbuf_tensor(f"mxo{i}", [128, 512], BF16)).ap() for i in range(2)]
    psSc = [es.enter_context(nc.psum_tensor(f"psSc{i}", [128, 1024], F32)).ap() for i in range(2)]
    psAV4 = [es.enter_context(nc.psum_tensor(f"psAV4_{i}", [65, 512], F32)).ap() for i in range(2)]
    psB1 = es.enter_context(nc.psum_tensor("psB1", [65, 512], F32)).ap()
    psX = es.enter_context(nc.psum_tensor("psX", [128, 512], F32)).ap()
    psPa = psX[:, 0:128]
    psPb = psSc[0][:, 0:512]
    psL = psSc[1][0:64, 0:512]
    psLAV = psSc[1][0:65, 512:1024]

    dma3(b, g, wkv1, dr["l1_wqkv"][:, 1024:3072], 2048, "da1")
    dma3(b, g, wq1, dr["l1_wqkv"][:, 0:1024], 1024, "da1")
    dma3(b, g, rqw, dr["l1_rq_w"], 256, "da1")
    dma3(b, g, routw, dr["l1_rout_w"], 1024, "da1")
    b.inc("da1", g.dma_start(out=bq1, in_=dr["l1_bqkv_cols"]), 16)
    b.inc("da1", g.dma_start(out=bv1, in_=dr["l1_bv_row"]), 16)
    b.inc("da1", g.dma_start(out=rqb, in_=dr["l1_rq_b_cols"]), 16)
    b.inc("da1", g.dma_start(out=routb, in_=dr["l1_rout_b_cols"]), 16)
    b.inc("da1", g.dma_start(out=glb, in_=dr["l1_gl_blk"]), 16)
    b.inc("da1", g.dma_start(out=grb, in_=dr["l1_gr_blk"]), 16)
    b.inc("da1", g.dma_start(out=mloc4, in_=dr["l1_mloc4"]), 16)
    b.inc("da1", g.dma_start(out=mlat, in_=dr["l1_mlat"]), 16)
    b.inc("v1ones", v.memset(V1, 1.0))
    b.wait(t, "da1")
    b.wait(t, "v1ones")
    b.wait(v, "da1")
    b.wait(s, "da1")

    def halo_cols(c, sl):
        if sl == 0:
            return xn1T[:, c * 4096 + 3968: c * 4096 + 4096]
        return xn1T[:, c * 4096 + sl * 512 - 128: c * 4096 + sl * 512]

    st = {
        "kev_rel": 0, "qev_rel": [0, 0],
        "psL_rel": ("ss", 0), "srow_war": 0, "psb_war": 0, "bce_war": ("lyn", 0),
        "vv_rec": 0, "psRt_rel": 0, "rem_rel": 0, "g1_war": 0, "psGB_rel": 0,
        "mx_rel": [0, 0], "locrel_vv": [0, 0], "locrel_pe": [0, 0],
    }
    exp_rel = {0: [("ss", 0)], 1: [("ss", 0)]}
    expb_rel = {0: 0, 1: 0}
    avrel = {0: ("locn", 0), 1: ("locn", 0)}
    pend = []

    def mk_epi1(lv_all, par):
        def f():
            b.wait(v, "lv", lv_all)
            b.wait(v, "pe", st["srow_war"])
            b.vstep(v.tensor_scalar(srow1[0:1, 0:512], psAV4[par][64:65, :], 1e-30, None, ALU.max))
            b.vdep()
            b.vstep(v.reciprocal(srow1[0:1, 512:1024], srow1[0:1, 0:512]))
            st["vv_rec"] = b.n("vv")
        return f

    def mk_epi2(hp, hh, par):
        def f():
            b.wait(t, "vv", st["vv_rec"])
            b.wait(t, "bce1", st["psb_war"])
            b.wait(t, "vv", st["psGB_rel"])
            mmb = t.matmul(psB1[0:64, :], P.ones_row_f[0:1, 0:64], srow1[0:1, 512:1024],
                           start=True, stop=True)
            st["srow_war"] = peb = b.inc("pe", mmb)
            b.wait(s, "pe", peb)
            b.wait(s, st["bce_war"][0], st["bce_war"][1])
            st["psb_war"] = b.inc("bce1", s.copy(bc1, psB1[0:64, :]))
            b.wait(v, "bce1", st["psb_war"])
            b.wait(v, "vv", st["locrel_vv"][hp % 2])
            b.wait(v, "pe", st["locrel_pe"][hp % 2])
            yn = v.tensor_tensor(loc1[hp % 2][64 * hh:64 * hh + 64, :],
                                 psAV4[par][0:64, :], bc1, ALU.mult)
            cnt = b.inc("locn", yn)
            st["bce_war"] = ("locn", cnt)
            avrel[par] = ("locn", cnt)
        return f

    def mk_stage3(hp, sl):
        def f():
            mp_ = hp % 2
            lcb = loc1[hp % 2]
            b.wait(t, "remev", st["psRt_rel"])
            mm = None
            for lp in range(2):
                mm = t.matmul(psX, routw[:, lp * 1024 + hp * 128: lp * 1024 + (hp + 1) * 128],
                              lat1[:, lp * 512:(lp + 1) * 512], start=(lp == 0), stop=(lp == 1))
            pe_cnt = b.inc("pe", mm)
            b.wait(s, "pe", pe_cnt)
            b.wait(s, "vv", st["rem_rel"])
            remev = b.inc("remev", s.activation(rem1, psX, AF.Identity, bias=routb[:, hp:hp + 1]))
            st["psRt_rel"] = remev
            locn_all = b.n("locn")
            b.wait(t, "locn", locn_all)
            b.wait(t, "remev", remev)
            b.wait(t, "bce1", st["psb_war"])
            t.matmul(psB1[0:1, :], glb[:, 0:1], lcb, start=True, stop=False)
            t.matmul(psB1[0:1, :], grb[:, 0:1], rem1, start=False, stop=True)
            t.matmul(psB1[64:65, :], glb[:, 1:2], lcb, start=True, stop=False)
            mm = t.matmul(psB1[64:65, :], grb[:, 1:2], rem1, start=False, stop=True)
            peg = b.inc("pe", mm)
            b.wait(s, "pe", peg)
            b.wait(s, "pe", st["g1_war"])
            s.activation(g1[0:1, :], psB1[0:1, :], AF.Sigmoid)
            sg = s.activation(g1[64:65, :], psB1[64:65, :], AF.Sigmoid)
            sg_cnt = b.inc("ss", sg)
            b.wait(t, "ss", sg_cnt)
            b.wait(t, avrel[1][0], avrel[1][1])
            t.matmul(psAV4[1][0:64, :], P.ones_row_f[0:1, 0:64], g1[0:1, :], start=True, stop=True)
            mm2 = t.matmul(psB1[0:64, :], P.ones_row_f[64:65, 0:64], g1[64:65, :], start=True, stop=True)
            peb3 = b.inc("pe", mm2)
            b.wait(v, "pe", peb3)
            b.wait(v, "remev", remev)
            b.wait(v, "locn", locn_all)
            b.vdep()
            b.vstep(v.tensor_sub(d1f, lcb, rem1))
            b.vdep()
            b.vstep(v.tensor_tensor(d1f[0:64, :], d1f[0:64, :], psAV4[1][0:64, :], ALU.mult))
            b.vstep(v.tensor_tensor(d1f[64:128, :], d1f[64:128, :], psB1[0:64, :], ALU.mult))
            b.wait(v, f"dmx{mp_}", st["mx_rel"][mp_])
            b.vdep()
            b.vstep(v.tensor_tensor(mxo[mp_], d1f, rem1, ALU.add))
            b.wait(y, "vv", b.n("vv"))
            b.inc(f"dmx{mp_}", y.dma_start(out=P.mx_dram[hp, :, sl * 512:(sl + 1) * 512], in_=mxo[mp_]), 16)
            st["mx_rel"][mp_] = b.n(f"dmx{mp_}")
            vvn = b.n("vv")
            st["rem_rel"] = vvn
            st["psGB_rel"] = vvn
            st["g1_war"] = peb3
            st["psb_war"] = max(st["psb_war"], 0)
            avrel[1] = ("vv", vvn)
            st["locrel_vv"][hp % 2] = vvn
            st["locrel_pe"][hp % 2] = peg
        return f

    for sl in range(4):
        for hp in range(8):
            b.wait(t, "kev", st["kev_rel"])
            if hp == 0:
                b.wait(t, "remev", st["psRt_rel"])
                for w_ in exp_rel[0]:
                    b.wait(t, w_[0], w_[1])
            mm = None
            for c in range(8):
                mm = t.matmul(psPa, wkv1[:, c * 2048 + hp * 128: c * 2048 + (hp + 1) * 128],
                              halo_cols(c, sl), start=(c == 0), stop=(c == 7))
            pea = b.inc("pe", mm)
            for c in range(8):
                mm = t.matmul(psPb, wkv1[:, c * 2048 + hp * 128: c * 2048 + (hp + 1) * 128],
                              xn1T[:, c * 4096 + sl * 512: c * 4096 + (sl + 1) * 512],
                              start=(c == 0), stop=(c == 7))
            peb = b.inc("pe", mm)
            b.wait(s, "pe", peb)
            s.activation(K1[:, hp * 640: hp * 640 + 128], psPa, AF.Identity, bias=bq1[:, 8 + hp:9 + hp])
            st["kev_rel"] = b.inc("kev", s.activation(
                K1[:, hp * 640 + 128: hp * 640 + 640], psPb, AF.Identity, bias=bq1[:, 8 + hp:9 + hp]))
        b.wait(t, "kev", b.n("kev"))
        for kt in range(5):
            for vh in range(2):
                b.wait(t, "vev", b.n("vev"))
                for c in range(8):
                    src = halo_cols(c, sl) if kt == 0 else \
                        xn1T[:, c * 4096 + sl * 512 + (kt - 1) * 128: c * 4096 + sl * 512 + kt * 128]
                    t.matmul(psPb, src,
                             wkv1[:, c * 2048 + 1024 + vh * 512: c * 2048 + 1024 + (vh + 1) * 512],
                             start=(c == 0), stop=False)
                mm = t.matmul(psPb, P.ones_row, bv1[0:1, vh * 512:(vh + 1) * 512], start=False, stop=True)
                pe_cnt = b.inc("pe", mm)
                b.wait(v, "pe", pe_cnt)
                src = psPb.rearrange("p (h d) -> p h d", d=64)
                for gsub in range(4):
                    hg = vh * 4 + gsub
                    dst = V1[:, kt * 1040 + hg * 130: kt * 1040 + (hg + 1) * 130]
                    dst = dst.rearrange("p (h e) -> p h e", e=65)[:, :, 0:64]
                    b.inc("vev", v.tensor_copy(dst, src[:, gsub * 2:gsub * 2 + 2, :]))
        b.wait(t, "vev", b.n("vev"))
        for hp in range(8):
            qp = hp % 2
            b.wait(t, "qev", st["qev_rel"][qp])
            ps = psPb if qp else psX
            for c in range(8):
                mm = t.matmul(ps, wq1[:, c * 1024 + hp * 128: c * 1024 + (hp + 1) * 128],
                              xn1T[:, c * 4096 + sl * 512: c * 4096 + (sl + 1) * 512],
                              start=(c == 0), stop=(c == 7))
            pe_cnt = b.inc("pe", mm)
            b.wait(s, "pe", pe_cnt)
            st["qev_rel"][qp] = b.inc("qev", s.activation(
                Q1[:, hp * 512:(hp + 1) * 512], ps, AF.Identity, bias=bq1[:, hp:hp + 1]))
        b.wait(t, "qev", b.n("qev"))
        for rt in range(2):
            qp = rt % 2
            b.wait(t, "rqev", max(0, b.n("rqev") - 1))
            ps = psPb if qp else psX
            for c in range(8):
                mm = t.matmul(ps, rqw[:, c * 256 + rt * 128: c * 256 + (rt + 1) * 128],
                              xn1T[:, c * 4096 + sl * 512: c * 4096 + (sl + 1) * 512],
                              start=(c == 0), stop=(c == 7))
            pe_cnt = b.inc("pe", mm)
            b.wait(s, "pe", pe_cnt)
            b.inc("rqev", s.activation(rq1[:, rt * 512:(rt + 1) * 512], ps,
                                       AF.Identity, bias=rqb[:, rt:rt + 1]))
        b.wait(t, "rqev", b.n("rqev"))
        for lh in range(4):
            lp, lr = lh // 2, 64 * (lh % 2)
            b.wait(t, st["psL_rel"][0], st["psL_rel"][1])
            if lh == 0:
                for w_ in exp_rel[1]:
                    b.wait(t, w_[0], w_[1])
            mm = t.matmul(psL, rk_sb[lr:lr + 64, lp * 64:(lp + 1) * 64],
                          rq1[lr:lr + 64, lp * 512:(lp + 1) * 512], start=True, stop=True)
            pe_cnt = b.inc("pe", mm)
            b.wait(s, "pe", pe_cnt)
            b.wait(s, "lmk", b.n("lmk"))
            b.sstep(s.activation(rexp, psL, AF.Exp, scale=0.125))
            st["psL_rel"] = ("ss", b.n("ss"))
            b.wait(v, "ss", b.n("ss"))
            b.wait(v, "lav", b.n("lav"))
            lmk = b.inc("lmk", v.tensor_tensor(rexp, rexp, mlat[:, sl * 512:(sl + 1) * 512], ALU.mult))
            b.wait(t, "lmk", lmk)
            b.wait(t, "lyn", b.n("lyn"))
            mm = t.matmul(psLAV, rv_sb[:, lh * 65: lh * 65 + 65], rexp, start=True, stop=True)
            b.inc("lav", mm)
            b.wait(v, "lav", b.n("lav"))
            b.wait(v, "pe", st["srow_war"])
            b.vstep(v.tensor_scalar(srow1[0:1, 0:512], psLAV[64:65, :], 1e-30, None, ALU.max))
            b.vdep()
            b.vstep(v.reciprocal(srow1[0:1, 512:1024], srow1[0:1, 0:512]))
            b.wait(t, "vv", b.n("vv"))
            b.wait(t, "bce1", st["psb_war"])
            b.wait(t, "vv", st["psGB_rel"])
            mmb = t.matmul(psB1[0:64, :], P.ones_row_f[0:1, 0:64], srow1[0:1, 512:1024],
                           start=True, stop=True)
            st["srow_war"] = peb = b.inc("pe", mmb)
            b.wait(s, "pe", peb)
            b.wait(s, st["bce_war"][0], st["bce_war"][1])
            st["psb_war"] = b.inc("bce1", s.copy(bc1, psB1[0:64, :]))
            b.wait(v, "bce1", b.n("bce1"))
            yn = v.tensor_tensor(lat1[lr:lr + 64, lp * 512:(lp + 1) * 512], psLAV[0:64, :], bc1, ALU.mult)
            st["bce_war"] = ("lyn", b.inc("lyn", yn))
        b.wait(t, "lyn", b.n("lyn"))
        exp_rel[0] = [("rqev", b.n("rqev"))]
        exp_rel[1] = [("lyn", b.n("lyn"))]
        for hp in range(8):
            for hh in range(2):
                ci = hp * 2 + hh
                sp = hh
                for w_ in exp_rel[sp]:
                    b.wait(t, w_[0], w_[1])
                psc = psSc[sp]
                mm = None
                for qt in range(4):
                    for u in (0, 1):
                        mm = t.matmul(
                            psc[:, qt * 256 + u * 128: qt * 256 + (u + 1) * 128],
                            K1[64 * hh:64 * hh + 64, hp * 640 + (qt + u) * 128: hp * 640 + (qt + u + 1) * 128],
                            Q1[64 * hh:64 * hh + 64, hp * 512 + qt * 128: hp * 512 + (qt + 1) * 128],
                            start=True, stop=True)
                pe_cnt = b.inc("pe", mm)
                b.wait(s, "pe", pe_cnt)
                b.wait(s, "lv", expb_rel[sp])
                e_in = s.activation(exp4[sp], psc, AF.Exp, scale=0.125)
                ss_cnt = b.inc("ss", e_in)
                exp_rel[sp] = [("ss", ss_cnt)]
                b.wait(v, "ss", ss_cnt)
                b.wait(v, "lv", expb_rel[sp])
                mk = b.inc("lmkk", v.tensor_tensor(exp4[sp], exp4[sp],
                                                   mloc4[:, sl * 1024:(sl + 1) * 1024], ALU.mult))
                exp_rel[sp] = [("lmkk", mk)]
                while pend and pend[0][0] <= ci:
                    pend.pop(0)[1]()
                b.wait(t, "lmkk", mk)
                b.wait(t, avrel[sp][0], avrel[sp][1])
                mm = None
                for qt in range(4):
                    t.matmul(psAV4[sp][:, qt * 128:(qt + 1) * 128],
                             V1[:, qt * 1040 + hp * 130 + hh * 65: qt * 1040 + hp * 130 + hh * 65 + 65],
                             exp4[sp][:, qt * 256: qt * 256 + 128], start=True, stop=False)
                    mm = t.matmul(psAV4[sp][:, qt * 128:(qt + 1) * 128],
                                  V1[:, (qt + 1) * 1040 + hp * 130 + hh * 65: (qt + 1) * 1040 + hp * 130 + hh * 65 + 65],
                                  exp4[sp][:, qt * 256 + 128: qt * 256 + 256], start=False, stop=True)
                expb_rel[sp] = b.inc("lv", mm)
                pend.append((ci + 1, mk_epi1(b.n("lv"), sp)))
                pend.append((ci + 2, mk_epi2(hp, hh, sp)))
            pend.append((hp * 2 + 3, mk_stage3(hp, sl)))
        while pend:
            pend.pop(0)[1]()
    b.barrier()
    es.close()



def emit_p5b(P):
    nc, b, dr = P.nc, P.b, P.dr
    t, v, s, g, y = b.t, b.v, b.s, b.g, b.y
    es_op = ExitStack()
    wout = es_op.enter_context(nc.sbuf_tensor("wout", [128, 8 * 1024], BF16)).ap()
    bo_row = es_op.enter_context(nc.sbuf_tensor("bo_row", [1, 1024], BF16)).ap()
    xsA = [es_op.enter_context(nc.sbuf_tensor(f"xsA{i}", [128, 1024], F32)).ap() for i in range(2)]
    xsB = [es_op.enter_context(nc.sbuf_tensor(f"xsB{i}", [128, 1024], F32)).ap() for i in range(2)]
    mixbuf = [es_op.enter_context(nc.sbuf_tensor(f"mixbuf{i}", [128, 8 * 128], BF16)).ap() for i in range(2)]
    psO5 = [es_op.enter_context(nc.psum_tensor(f"psO5_{i}", [128, 512], F32)).ap() for i in range(2)]
    dma3(b, g, wout, dr["l1_wout"], 1024, "dop")
    b.inc("dop", g.dma_start(out=bo_row, in_=dr["l1_bout_row"]), 16)
    b.wait(t, "dop")
    psO_rel = [0, 0]
    mixbuf_rel = [0, 0]
    oad_rel = [0, 0]
    st2_rel = {0: ("dx2", 0), 1: ("dout", 0)}
    dmxl_cnt = {0: 0, 1: 0}

    def issue_mx(T):
        mbp = T % 2
        sl, tt = T // 4, T % 4
        b.wait(y, "pe", mixbuf_rel[mbp])
        b.inc(f"dmxl{mbp}", y.dma_start(
            out=mixbuf[mbp].rearrange("p (c e) -> p c e", e=128),
            in_=P.mx_dram[:, :, sl * 512 + tt * 128: sl * 512 + (tt + 1) * 128].rearrange("c p e -> p c e")), 16)
        dmxl_cnt[mbp] = b.n(f"dmxl{mbp}")

    issue_mx(0)
    for T in range(16):
        sl, tt = T // 4, T % 4
        offA = 128 * T
        offB = (2048 + 128 * T) % 4096
        mbp = T % 2
        if T + 1 < 16:
            issue_mx(T + 1)
        b.wait(y, "oadd1", oad_rel[mbp])
        b.wait(y, st2_rel[mbp][0], st2_rel[mbp][1])
        xsem = "dxr" if mbp == 0 else "dxn"
        b.inc(xsem, y.dma_start(out=xsA[mbp], in_=P.x1_dram[offA:offA + 128, :]), 16)
        b.inc(xsem, y.dma_start(out=xsB[mbp], in_=P.x1_dram[offB:offB + 128, :]), 16)
        dx_t = b.n(xsem)
        b.wait(t, f"dmxl{mbp}", dmxl_cnt[mbp])
        b.wait(v, xsem, dx_t)
        b.vdep()
        b.vstep(v.tensor_scalar(xsA[mbp], xsA[mbp], P.hsel[:, 0:1], None, ALU.mult))
        b.vdep()
        b.vstep(v.scalar_tensor_tensor(xsA[mbp], xsB[mbp], P.hsel[:, 1:2], xsA[mbp], ALU.mult, ALU.add))
        sel_vv = b.n("vv")
        for dh in range(2):
            op = dh
            b.wait(t, "oadd1", psO_rel[op])
            ps = psO5[op]
            for c in range(8):
                t.matmul(ps, mixbuf[mbp][:, c * 128:(c + 1) * 128],
                         wout[:, c * 1024 + dh * 512: c * 1024 + (dh + 1) * 512],
                         start=(c == 0), stop=False)
            mm = t.matmul(ps, P.ones_row, bo_row[0:1, dh * 512:(dh + 1) * 512],
                          start=False, stop=True)
            pe_cnt = b.inc("pe", mm)
            b.wait(v, "pe", pe_cnt)
            b.vdep(sel_vv)
            ad = v.tensor_tensor(xsA[mbp][:, dh * 512:(dh + 1) * 512],
                                 xsA[mbp][:, dh * 512:(dh + 1) * 512], ps, ALU.add)
            psO_rel[op] = b.inc("oadd1", ad)
        mixbuf_rel[mbp] = b.n("pe")
        oad_rel[mbp] = b.n("oadd1")
        b.wait(y, "oadd1", b.n("oadd1"))
        ssem = "dx2" if mbp == 0 else "dout"
        b.inc(ssem, y.dma_start(out=P.x2p_dram[T * 128:(T + 1) * 128, :], in_=xsA[mbp]), 16)
        st2_rel[mbp] = (ssem, b.n(ssem))
    b.barrier()
    es_op.close()



def emit_p6(P, xn1T, out):
    nc, b, dr = P.nc, P.b, P.dr
    t, v, s, g, y = b.t, b.v, b.s, b.g, b.y
    es = ExitStack()
    x_res = es.enter_context(nc.sbuf_tensor("x_res", [128, 16 * 1024], F32)).ap()
    w13 = [es.enter_context(nc.sbuf_tensor(f"w13a_{i}", [128, 8 * 1024], BF16)).ap() for i in range(2)]
    w2 = [es.enter_context(nc.sbuf_tensor(f"w2a_{i}", [128, 4 * 1024], BF16)).ap() for i in range(2)]
    act_sb = es.enter_context(nc.sbuf_tensor("act1_sb", [128, 4 * 512], BF16)).ap()
    s_s = es.enter_context(nc.sbuf_tensor("s1_s", [128, 512], F32)).ap()
    psA = es.enter_context(nc.psum_tensor("psA1", [128, 512], F32)).ap()
    psB2 = es.enter_context(nc.psum_tensor("psB21", [128, 512], F32)).ap()
    psM = [es.enter_context(nc.psum_tensor(f"psM1{i}", [128, 512], F32)).ap() for i in range(4)]

    for T in range(16):
        b.inc("dx2l", y.dma_start(out=x_res[:, T * 1024:(T + 1) * 1024],
                                  in_=P.x2p_dram[T * 128:(T + 1) * 128, :]), 16)
    b.wait(v, "dx2l", b.n("dx2l"))

    NEXP = 17

    def wsrc(e):
        if e < 16:
            return dr["l1_w13"][e], dr["l1_w2"][e]
        return dr["l1_w13s"], dr["l1_w2s"]
    w_rel = [0, 0]
    dma3(b, g, w13[0], wsrc(0)[0], 1024, "dm1_0")
    dma3(b, g, w2[0], wsrc(0)[1], 1024, "dm2_0")
    psA_rel = 0
    psB2_rel = 0
    psM_rel = [0, 0, 0, 0]
    act_war = 0
    ss_war = 0
    macc_reg = {}
    for e in range(NEXP):
        ep = e % 2
        if e + 1 < NEXP:
            b.wait(g, "pe", w_rel[1 - ep])
            dma3(b, g, w13[1 - ep], wsrc(e + 1)[0], 1024, f"dm1_{1 - ep}")
            dma3(b, g, w2[1 - ep], wsrc(e + 1)[1], 1024, f"dm2_{1 - ep}")
        b.wait(t, f"dm1_{ep}", b.n(f"dm1_{ep}"))
        b.wait(t, f"dm2_{ep}", b.n(f"dm2_{ep}"))
        for tc in range(4):
            for fp in range(4):
                b.wait(t, "sil1", psA_rel)
                for c in range(8):
                    mm = t.matmul(psA, w13[ep][:, c * 1024 + fp * 128: c * 1024 + (fp + 1) * 128],
                                  xn1T[:, c * 4096 + tc * 512: c * 4096 + (tc + 1) * 512],
                                  start=(c == 0), stop=(c == 7))
                peA = b.inc("pe", mm)
                b.wait(t, "mact1", psB2_rel)
                for c in range(8):
                    mm = t.matmul(psB2, w13[ep][:, c * 1024 + 512 + fp * 128: c * 1024 + 512 + (fp + 1) * 128],
                                  xn1T[:, c * 4096 + tc * 512: c * 4096 + (tc + 1) * 512],
                                  start=(c == 0), stop=(c == 7))
                peB = b.inc("pe", mm)
                w_rel[ep] = peB
                b.wait(s, "pe", peA)
                b.wait(s, "mact1", ss_war)
                psA_rel = b.inc("sil1", s.activation(s_s, psA, AF.Silu))
                b.wait(v, "pe", peB)
                b.wait(v, "sil1", psA_rel)
                b.wait(v, "pe", act_war)
                psB2_rel = ss_war = b.inc("mact1", v.tensor_tensor(
                    act_sb[:, fp * 512:(fp + 1) * 512], psB2, s_s, ALU.mult))
            b.wait(t, "mact1", b.n("mact1"))
            for tt in range(4):
                T = tc * 4 + tt
                for dh in range(2):
                    mp = (tt * 2 + dh) % 4
                    b.wait(t, "macc1", psM_rel[mp])
                    for ic in range(4):
                        mm = t.matmul(psM[mp], act_sb[:, ic * 512 + tt * 128: ic * 512 + (tt + 1) * 128],
                                      w2[ep][:, ic * 1024 + dh * 512: ic * 1024 + (dh + 1) * 512],
                                      start=(ic == 0), stop=(ic == 3))
                    peM = b.inc("pe", mm)
                    b.wait(v, "pe", peM)
                    b.wait(v, "macc1", macc_reg.get((T, dh), 0))
                    scl = P.comb[:, T * 16 + e: T * 16 + e + 1] if e < 16 else 1.0
                    psM_rel[mp] = macc_reg[(T, dh)] = b.inc("macc1", v.scalar_tensor_tensor(
                        x_res[:, T * 1024 + dh * 512: T * 1024 + (dh + 1) * 512],
                        psM[mp], scl,
                        x_res[:, T * 1024 + dh * 512: T * 1024 + (dh + 1) * 512],
                        ALU.mult, ALU.add))
            act_war = b.n("pe")
            w_rel[ep] = b.n("pe")
    b.barrier()
    for T in range(16):
        b.inc("dout", y.dma_start(out=out[T * 128:(T + 1) * 128, :],
                                  in_=x_res[:, T * 1024:(T + 1) * 1024]), 16)
    b.wait(y, "dout", b.n("dout"))
    es.close()



def build_program(stage="full"):
    nc = bass.Bass("TRN2", target_bir_lowering=False)
    b = Bld(nc)
    dr = {}
    for name, shape in ALL_INPUTS:
        dr[name] = nc.dram_tensor(name, list(shape), F32, kind="ExternalInput").ap()
    P = Prog(nc, b, dr)
    P.out = nc.dram_tensor("out", [2048, 1024], F32, kind="ExternalOutput").ap()
    if stage in ("l0x", "l0m"):
        P.dbg = nc.dram_tensor("dbg", [4096, 1024], F32, kind="ExternalOutput").ap()
    P.x1_dram = nc.dram_tensor("x1_scratch", [4096, 1024], F32, kind="Internal").ap()
    P.v_dram = nc.dram_tensor("v_scratch", [8, 128, 32, 130], BF16, kind="Internal").ap()
    P.q_dram = nc.dram_tensor("q_scratch", [8, 128, 8, 512], BF16, kind="Internal").ap()
    P.xnt_dram = nc.dram_tensor("xnt_scratch", [8, 128, 8, 512], BF16, kind="Internal").ap()
    P.x2p_dram = nc.dram_tensor("x2p_scratch", [2048, 1024], F32, kind="Internal").ap()
    P.mx_dram = nc.dram_tensor("mx_scratch", [8, 128, 2048], BF16, kind="Internal").ap()

    es_k = ExitStack()
    K_sb = es_k.enter_context(nc.sbuf_tensor("K_sb", [128, 8 * 4096], BF16)).ap()
    emit_p1(P, K_sb)
    emit_p2(P, K_sb)
    es_k.close()

    def dbg_dump():
        for T in range(32):
            b.inc("dout", b.y.dma_start(out=P.dbg[T * 128:(T + 1) * 128, :],
                                        in_=P.x1_dram[T * 128:(T + 1) * 128, :]), 16)
        for T in range(16):
            b.inc("dout", b.y.dma_start(out=P.out[T * 128:(T + 1) * 128, :],
                                        in_=P.x1_dram[T * 128:(T + 1) * 128, :]), 16)
        b.wait(b.y, "dout", b.n("dout"))

    if stage == "l0x":
        dbg_dump()
        return nc
    emit_p3(P)
    if stage == "l0m":
        dbg_dump()
        return nc

    es1 = ExitStack()
    xn1T = es1.enter_context(nc.sbuf_tensor("xn1T", [128, 8 * 4096], BF16)).ap()
    pooled = es1.enter_context(nc.sbuf_tensor("pooled", [128, 8 * 64], BF16)).ap()
    rk_sb = es1.enter_context(nc.sbuf_tensor("rk_sb", [128, 2 * 64], BF16)).ap()
    rv_sb = es1.enter_context(nc.sbuf_tensor("rv_sb", [64, 4 * 65], BF16)).ap()
    emit_p4(P, xn1T)
    emit_p4b(P, xn1T, pooled, rk_sb, rv_sb)
    emit_p5(P, xn1T, pooled, rk_sb, rv_sb)
    emit_p5b(P)
    emit_p6(P, xn1T, P.out)
    es1.close()
    return nc


def kernel(**inputs):
    inputs.get("L1_ffn_input_ids")
    nc = build_program("full")
    im = [host_inputs(inputs, core) for core in range(8)]
    res = run_bass_kernel_spmd(nc, im, core_ids=list(range(8)))
    return reassemble([res.results[c]["out"] for c in range(8)])


# version 25 (final)
# speedup vs baseline: 1.0376x; 1.0129x over previous
import math
import os
from contextlib import ExitStack

import numpy as np

import concourse.bass as bass
import concourse.mybir as mybir
from concourse.bass_utils import run_bass_kernel_spmd

F32 = mybir.dt.float32
BF16 = mybir.dt.bfloat16
ALU = mybir.AluOpType
AF = mybir.ActivationFunctionType
AXX = mybir.AxisListType.X

D = 1024
EPS = 1e-5
KT_FULL = [4, 8, 12, 16, 20, 24, 28, 32]



def _tile_cols(vec, n_tiles):
    return np.ascontiguousarray(vec.reshape(n_tiles, 128).T)


def host_inputs(inp, core):
    role = core % 2
    samp = core // 2
    f32 = np.float32
    d = {}
    x = np.asarray(inp["L0_attn_x"], dtype=f32)
    d["x_abs"] = np.ascontiguousarray(x[samp])

    d["l0_wqkv"] = np.ascontiguousarray(np.asarray(inp["L0_attn_W_attn"], f32))
    b_attn = np.asarray(inp["L0_attn_b_attn"], f32)
    d["l0_bqkv_cols"] = _tile_cols(b_attn, 24)
    d["l0_bv_row"] = np.ascontiguousarray(b_attn[2048:3072].reshape(1, 1024))
    d["l0_wproj"] = np.ascontiguousarray(np.asarray(inp["L0_attn_W_proj"], f32))
    d["l0_bproj_row"] = np.ascontiguousarray(np.asarray(inp["L0_attn_b_proj"], f32).reshape(1, 1024))

    tri = np.zeros((4, 128, 512), f32)
    for j in range(4):
        tri[j] = ((j * 128 + np.arange(128)[:, None]) <= np.arange(512)[None, :])
    d["l0_tri"] = np.ascontiguousarray(tri.transpose(1, 0, 2).reshape(128, 2048))

    d["l0_router_w"] = np.ascontiguousarray(np.asarray(inp["L0_ffn_router_weight"], f32))
    gup = np.asarray(inp["L0_ffn_gate_up_proj"], f32)
    perm = np.concatenate([np.arange(0, 2048, 2), np.arange(1, 2048, 2)])
    d["l0_w13"] = np.ascontiguousarray(gup[:, :, perm])
    gub = np.asarray(inp["L0_ffn_gate_up_proj_bias"], f32)[:, perm]
    d["l0_b13_cols"] = np.ascontiguousarray(gub.reshape(8, 16, 128).transpose(2, 0, 1).reshape(128, 128))
    d["l0_w2"] = np.ascontiguousarray(np.asarray(inp["L0_ffn_down_proj"], f32))
    d["l0_b2_rows"] = np.ascontiguousarray(np.asarray(inp["L0_ffn_down_proj_bias"], f32).reshape(1, 8192))

    d["l1_wqkv"] = np.ascontiguousarray(np.asarray(inp["L1_attn_qkv_w"], f32))
    b1 = np.asarray(inp["L1_attn_qkv_b"], f32)
    d["l1_bqkv_cols"] = _tile_cols(b1, 24)
    d["l1_bv_row"] = np.ascontiguousarray(b1[2048:3072].reshape(1, 1024))
    d["l1_rq_w"] = np.ascontiguousarray(np.asarray(inp["L1_attn_rq_w"], f32))
    d["l1_rq_b_cols"] = _tile_cols(np.asarray(inp["L1_attn_rq_b"], f32), 2)
    d["l1_rkv_w"] = np.ascontiguousarray(np.asarray(inp["L1_attn_rkv_w"], f32))
    rkvb = np.asarray(inp["L1_attn_rkv_b"], f32)
    d["l1_rk_b_cols"] = _tile_cols(rkvb[:256], 2)
    d["l1_rv_b_row"] = np.ascontiguousarray(rkvb[256:].reshape(1, 256))
    d["l1_rout_w"] = np.ascontiguousarray(np.asarray(inp["L1_attn_rout_w"], f32))
    d["l1_rout_b_cols"] = _tile_cols(np.asarray(inp["L1_attn_rout_b"], f32), 8)
    d["l1_wout"] = np.ascontiguousarray(np.asarray(inp["L1_attn_out_w"], f32))
    d["l1_bout_row"] = np.ascontiguousarray(np.asarray(inp["L1_attn_out_b"], f32).reshape(1, 1024))
    for nm, key in (("l1_gl_blk", "L1_attn_gl_w"), ("l1_gr_blk", "L1_attn_gr_w")):
        blk = np.zeros((128, 2), f32)
        wv = np.asarray(inp[key], f32)[:, 0]
        blk[0:64, 0] = wv
        blk[64:128, 1] = wv
        d[nm] = blk

    kk = np.arange(128)[:, None]
    qq = np.arange(128)[None, :]
    mprev = (qq <= kk).astype(f32)
    mdiag = (qq > kk).astype(f32)
    d["l1_mloc"] = np.ascontiguousarray(np.concatenate([mprev, mdiag], axis=1))
    m0 = np.zeros((4, 128, 256), f32)
    for s_ in range(4):
        m0[s_, :, 0:128] = 0.0 if (role == 0 and s_ == 0) else mprev
        m0[s_, :, 128:256] = mdiag
    d["l1_mloc0"] = np.ascontiguousarray(m0.transpose(1, 0, 2).reshape(128, 1024))
    m4 = np.zeros((4, 128, 1024), f32)
    for s_ in range(4):
        for qt_ in range(4):
            m4[s_, :, qt_ * 256:(qt_ + 1) * 256] = m0[s_] if qt_ == 0 else d["l1_mloc"]
    d["l1_mloc4"] = np.ascontiguousarray(m4.transpose(1, 0, 2).reshape(128, 4096))

    col_abs = np.empty(64, np.int64)
    col_abs[0:32] = role * 32 + np.arange(32)
    col_abs[32:64] = (1 - role) * 32 + np.arange(32)
    mlat = np.zeros((4, 64, 512), f32)
    for s_ in range(4):
        qchunk = role * 32 + (s_ * 512 + np.arange(512)) // 64
        mlat[s_] = (col_abs[:, None] < qchunk[None, :]).astype(f32)
    d["l1_mlat"] = np.ascontiguousarray(mlat.transpose(1, 0, 2).reshape(64, 2048))

    sel = np.zeros((128, 2), f32)
    sel[:, role] = 1.0
    d["half_sel"] = sel

    d["l1_gate_wT"] = np.ascontiguousarray(np.asarray(inp["L1_ffn_gate_w"], f32).T)
    d["l1_gate_b_row"] = np.ascontiguousarray(np.asarray(inp["L1_ffn_gate_b"], f32).reshape(1, 16))
    d["l1_w13"] = np.ascontiguousarray(np.asarray(inp["L1_ffn_w13"], f32).transpose(0, 2, 1))
    d["l1_w2"] = np.ascontiguousarray(np.asarray(inp["L1_ffn_w2"], f32).transpose(0, 2, 1))
    d["l1_w13s"] = np.ascontiguousarray(np.asarray(inp["L1_ffn_w13_s"], f32).T)
    d["l1_w2s"] = np.ascontiguousarray(np.asarray(inp["L1_ffn_w2_s"], f32).T)

    d["identity"] = np.eye(128, dtype=f32)
    d["row_1702"] = np.full((1, 128), 1.702, f32)
    return d


def reassemble(outs):
    full = np.empty((4, 4096, 1024), np.float32)
    for core in range(8):
        role, samp = core % 2, core // 2
        full[samp, role * 2048:(role + 1) * 2048] = outs[core]
    return full


ALL_INPUTS = [
    ("x_abs", (4096, 1024)),
    ("l0_wqkv", (1024, 3072)), ("l0_bqkv_cols", (128, 24)), ("l0_bv_row", (1, 1024)),
    ("l0_wproj", (1024, 1024)), ("l0_bproj_row", (1, 1024)),
    ("l0_tri", (128, 2048)),
    ("l0_router_w", (1024, 8)),
    ("l0_w13", (8, 1024, 2048)), ("l0_b13_cols", (128, 128)),
    ("l0_w2", (8, 1024, 1024)), ("l0_b2_rows", (1, 8192)),
    ("l1_wqkv", (1024, 3072)), ("l1_bqkv_cols", (128, 24)), ("l1_bv_row", (1, 1024)),
    ("l1_rq_w", (1024, 256)), ("l1_rq_b_cols", (128, 2)),
    ("l1_rkv_w", (1024, 512)), ("l1_rk_b_cols", (128, 2)), ("l1_rv_b_row", (1, 256)),
    ("l1_rout_w", (256, 1024)), ("l1_rout_b_cols", (128, 8)),
    ("l1_wout", (1024, 1024)), ("l1_bout_row", (1, 1024)),
    ("l1_gl_blk", (128, 2)), ("l1_gr_blk", (128, 2)),
    ("l1_mloc", (128, 256)), ("l1_mloc0", (128, 1024)), ("l1_mlat", (64, 2048)),
    ("l1_mloc4", (128, 4096)),
    ("half_sel", (128, 2)),
    ("l1_gate_wT", (1024, 16)), ("l1_gate_b_row", (1, 16)),
    ("l1_w13", (16, 1024, 1024)), ("l1_w2", (16, 512, 1024)),
    ("l1_w13s", (1024, 1024)), ("l1_w2s", (512, 1024)),
    ("identity", (128, 128)), ("row_1702", (1, 128)),
]



class Bld:
    def __init__(self, nc):
        self.nc = nc
        self.t = nc.tensor
        self.v = nc.vector
        self.s = nc.scalar
        self.g = nc.gpsimd
        self.y = nc.sync
        self._sems = {}
        self._waited = {}

    def sem(self, name):
        if name not in self._sems:
            self._sems[name] = [self.nc.alloc_semaphore(name), 0]
        return self._sems[name]

    def inc(self, name, instr, k=1):
        sm = self.sem(name)
        instr.then_inc(sm[0], k)
        sm[1] += k
        return sm[1]

    def n(self, name):
        return self.sem(name)[1]

    def wait(self, eng, name, val=None):
        sm = self.sem(name)
        if val is None:
            val = sm[1]
        if val <= 0:
            return
        key = (id(eng), name)
        if self._waited.get(key, -1) >= val:
            return
        if val > sm[1]:
            raise RuntimeError(f"wait {name}>={val} can never fire (max {sm[1]})")
        eng.wait_ge(sm[0], val)
        self._waited[key] = val

    def vstep(self, instr):
        self.inc("vv", instr)
        return instr

    def vdep(self, val=None):
        self.wait(self.v, "vv", val)

    def sstep(self, instr):
        self.inc("ss", instr)
        return instr

    def sdep(self, val=None):
        self.wait(self.s, "ss", val)

    def barrier(self):
        engines = [self.t, self.v, self.s, self.g, self.y]
        for name, (h, cnt) in list(self._sems.items()):
            if cnt == 0:
                continue
            for e in engines:
                self.wait(e, name, cnt)


def dma3(b, eng, dst2d, src2d, inner, sem, p=128):
    in3 = src2d.rearrange("(c p) f -> p c f", p=p)
    out3 = dst2d.rearrange("p (c f) -> p c f", f=inner)
    return b.inc(sem, eng.dma_start(out=out3, in_=in3), 16)


class Prog:
    def __init__(self, nc, b, dr):
        self.nc = nc
        self.b = b
        self.dr = dr
        alloc = nc.alloc_sbuf_tensor
        self.ident = alloc("ident", [128, 128], F32).ap()
        self.ones_row = alloc("ones_row", [1, 128], BF16).ap()
        self.ones_row_f = alloc("ones_row_f", [65, 128], F32).ap()
        self.row1702 = alloc("row1702", [1, 128], BF16).ap()
        self.wsc = alloc("wsc", [128, 32 * 8], F32).ap()
        self.comb = alloc("comb", [128, 16 * 16], F32).ap()
        self.stat = alloc("stat", [128, 16], F32).ap()
        self.hsel = alloc("hsel", [128, 2], F32).ap()
        g, v = b.g, b.v
        b.inc("dmisc", g.dma_start(out=self.ident, in_=dr["identity"]), 16)
        b.inc("dmisc", g.dma_start(out=self.row1702, in_=dr["row_1702"]), 16)
        b.inc("dmisc", g.dma_start(out=self.hsel, in_=dr["half_sel"]), 16)
        v.memset(self.ones_row, 1.0)
        b.inc("cmisc", v.memset(self.ones_row_f, 1.0))

    def emit_ln_stats(self, xtile, scr, dep_sems):
        b, v, s = self.b, self.b.v, self.b.s
        st = self.stat
        for eng in (v, s):
            for nm, val in dep_sems:
                b.wait(eng, nm, val)
        b.wait(v, "ln_out", b.n("ln_out"))
        scr_rel = getattr(self, "_scr_rel", None)
        if scr_rel is None:
            scr_rel = self._scr_rel = [0]
        for tt in range(4):
            xt = xtile[:, tt * 1024:(tt + 1) * 1024]
            b.vstep(v.reduce_sum(st[:, tt:tt + 1], xt, AXX))
            b.wait(s, "vv", scr_rel[0])
            b.inc("ln_sq", s.activation(scr, xt, AF.Square))
            b.wait(v, "ln_sq")
            b.vstep(v.reduce_sum(st[:, 4 + tt:5 + tt], scr, AXX))
            scr_rel[0] = b.n("vv")
        b.vdep()
        b.vstep(v.tensor_scalar(st[:, 0:4], st[:, 0:4], 1.0 / D, None, ALU.mult))
        b.vstep(v.tensor_scalar(st[:, 4:8], st[:, 4:8], 1.0 / D, None, ALU.mult))
        b.vdep()
        b.vstep(v.tensor_tensor(st[:, 8:12], st[:, 0:4], st[:, 0:4], ALU.mult))
        b.vdep()
        b.vstep(v.tensor_sub(st[:, 4:8], st[:, 4:8], st[:, 8:12]))
        b.vdep()
        b.vstep(v.tensor_scalar(st[:, 4:8], st[:, 4:8], EPS, None, ALU.add))
        b.vdep()
        b.vstep(v.reciprocal(st[:, 8:12], st[:, 4:8]))
        b.wait(s, "vv", b.n("vv"))
        b.sstep(s.activation(st[:, 12:16], st[:, 8:12], AF.Sqrt))
        b.wait(v, "ss", b.n("ss"))
        b.vdep()
        b.vstep(v.tensor_tensor(st[:, 8:12], st[:, 0:4], st[:, 12:16], ALU.mult))
        b.vdep()
        b.vstep(v.tensor_scalar(st[:, 8:12], st[:, 8:12], -1.0, None, ALU.mult))
        b.wait(s, "vv", b.n("vv"))

    def emit_ln_tile(self, xtile, tt, xn1tile):
        b, s = self.b, self.b.s
        st = self.stat
        b.sdep()
        return b.inc("ln_out", s.activation(
            xn1tile, xtile[:, tt * 1024:(tt + 1) * 1024], AF.Identity,
            bias=st[:, 8 + tt:9 + tt], scale=st[:, 12 + tt:13 + tt]))



def emit_p1(P, K_sb):
    nc, b, dr = P.nc, P.b, P.dr
    t, v, s, g, y = b.t, b.v, b.s, b.g, b.y
    es = ExitStack()
    xstage = [es.enter_context(nc.sbuf_tensor(f"xstage{i}", [128, 4096], F32)).ap() for i in range(2)]
    xnf = [es.enter_context(nc.sbuf_tensor(f"xnf{i}", [128, 1024], F32)).ap() for i in range(2)]
    scr = es.enter_context(nc.sbuf_tensor("scr", [128, 1024], F32)).ap()
    xnTt = es.enter_context(nc.sbuf_tensor("xnTt", [128, 8 * 512], BF16)).ap()
    xnTf = es.enter_context(nc.sbuf_tensor("xnTf", [128, 1024], F32)).ap()
    wqkv = es.enter_context(nc.sbuf_tensor("wqkv", [128, 8 * 3072], BF16)).ap()
    bq_cols = es.enter_context(nc.sbuf_tensor("bq_cols", [128, 24], F32)).ap()
    bv_row = es.enter_context(nc.sbuf_tensor("bv_row", [1, 1024], BF16)).ap()
    rw_sb = es.enter_context(nc.sbuf_tensor("rw_sb", [128, 64], F32)).ap()
    top_sc = es.enter_context(nc.sbuf_tensor("top_sc", [128, 128], F32)).ap()
    vstage = [es.enter_context(nc.sbuf_tensor(f"vstage{i}", [128, 4 * 1040], BF16)).ap() for i in range(2)]
    qstage = [es.enter_context(nc.sbuf_tensor(f"qstage{i}", [128, 512], BF16)).ap() for i in range(2)]
    psT = [es.enter_context(nc.psum_tensor(f"psT{i}", [128, 128], F32)).ap() for i in range(2)]
    psK = [es.enter_context(nc.psum_tensor(f"psK{i}", [128, 512], F32)).ap() for i in range(2)]
    psV = [es.enter_context(nc.psum_tensor(f"psV{i}", [128, 512], F32)).ap() for i in range(2)]
    psR = es.enter_context(nc.psum_tensor("psR", [128, 32], F32)).ap()

    dma3(b, g, wqkv, dr["l0_wqkv"], 3072, "dw")
    b.inc("dw", g.dma_start(out=bq_cols, in_=dr["l0_bqkv_cols"]), 16)
    b.inc("dw", g.dma_start(out=bv_row, in_=dr["l0_bv_row"]), 16)
    dma3(b, g, rw_sb, dr["l0_router_w"], 8, "dw")
    for i in range(2):
        v.memset(vstage[i], 1.0)
    b.wait(t, "dw")
    b.wait(t, "dmisc")
    b.wait(t, "cmisc")
    b.wait(s, "dw")

    ln_done = 0
    ln_rel = {0: 0, 1: 0}
    dx_cnt = {0: 0, 1: 0}
    xnf_rel = {0: 0, 1: 0}
    psT_rel = [0, 0]
    psK_rel = [("kev", 0), ("kev", 0)]
    psV_rel = [0, 0]
    vst_rel = {0: 0, 1: 0}
    qst_rel = {0: 0, 1: 0}
    psR_rel4 = [None, None, None, None]
    pend_rb = []

    def issue_x(j):
        par = j % 2
        sm = "dx" if par == 0 else "dxa"
        b.wait(y, "ln_out", ln_rel[par])
        dma3(b, y, xstage[par], dr["x_abs"][j * 512:(j + 1) * 512, :], 1024, sm)
        dx_cnt[par] = (sm, b.n(sm))

    issue_x(0)
    for j in range(8):
        par = j % 2
        if j + 1 < 8:
            issue_x(j + 1)
        b.wait(v, "dxn", b.n("dxn"))
        P.emit_ln_stats(xstage[par], scr, [dx_cnt[par]])
        for tt in range(4):
            T = j * 4 + tt
            fp_ = tt % 2
            b.wait(s, "pe", xnf_rel[fp_])
            ln_done = P.emit_ln_tile(xstage[par], tt, xnf[fp_])
            if tt == 3:
                ln_rel[par] = ln_done
            for c in range(8):
                tp = c % 2
                b.wait(t, "xe", psT_rel[tp])
                b.wait(t, "ln_out", ln_done)
                pe_cnt = b.inc("pe", t.transpose(psT[tp], xnf[fp_][:, c * 128:(c + 1) * 128], P.ident))
                b.wait(v, "pe", pe_cnt)
                b.inc("xe", v.tensor_copy(xnTt[:, c * 512 + tt * 128: c * 512 + (tt + 1) * 128], psT[tp]))
                psT_rel[tp] = b.inc("xe", v.tensor_copy(xnTf[:, c * 128:(c + 1) * 128], psT[tp]))
            xnf_rel[fp_] = b.n("pe")
            b.wait(t, "xe", b.n("xe"))
            if pend_rb:
                pend_rb.pop(0)()
            slot = psR[:, tt * 8:(tt + 1) * 8]
            stc = top_sc[:, tt * 32:(tt + 1) * 32]
            if psR_rel4[tt] is not None:
                b.wait(t, psR_rel4[tt][0], psR_rel4[tt][1])
            for c in range(8):
                mm = t.matmul(slot, xnTf[:, c * 128:(c + 1) * 128], rw_sb[:, c * 8:(c + 1) * 8],
                              start=(c == 0), stop=(c == 7))
            rmm_cnt = b.inc("rmm", mm)
            b.wait(v, "rmm", rmm_cnt)
            b.vdep()
            b.vstep(v.reduce_max(stc[:, 0:1], slot, AXX))
            b.vdep()
            b.vstep(v.tensor_scalar(stc[:, 8:16], slot, stc[:, 0:1], None, ALU.is_equal))
            b.vdep()
            b.vstep(v.scalar_tensor_tensor(stc[:, 16:24], stc[:, 8:16], -1e30, slot, ALU.mult, ALU.add))
            b.vdep()
            b.vstep(v.reduce_max(stc[:, 1:2], stc[:, 16:24], AXX))
            b.vstep(v.tensor_scalar(stc[:, 2:3], stc[:, 0:1], -1.0, None, ALU.mult))
            b.wait(s, "vv", b.n("vv"))
            b.sstep(s.activation(stc[:, 8:16], slot, AF.Exp, bias=stc[:, 2:3]))
            rex_ss = b.n("ss")

            def mk_routB(tt=tt, T=T, slot=slot, stc=stc, rex_ss=rex_ss):
                def f():
                    b.wait(v, "ss", rex_ss)
                    b.vdep()
                    b.vstep(v.tensor_scalar(stc[:, 16:24], slot, stc[:, 1:2], None, ALU.is_ge))
                    b.vdep()
                    b.vstep(v.tensor_tensor(stc[:, 8:16], stc[:, 8:16], stc[:, 16:24], ALU.mult))
                    b.vdep()
                    b.vstep(v.reduce_sum(stc[:, 3:4], stc[:, 8:16], AXX))
                    b.vdep()
                    b.vstep(v.reciprocal(stc[:, 4:5], stc[:, 3:4]))
                    b.vdep()
                    b.vstep(v.tensor_scalar(P.wsc[:, T * 8:(T + 1) * 8], stc[:, 8:16],
                                            stc[:, 4:5], 1.0 / 1.702, ALU.mult, ALU.mult))
                    psR_rel4[tt] = ("vv", b.n("vv"))
                return f
            pend_rb.append(mk_routB())
        b.wait(y, "xe", b.n("xe"))
        b.inc("dxn", y.dma_start(
            out=P.xnt_dram[j],
            in_=xnTt.rearrange("p (c f) -> p c f", f=512)), 16)
        for hp in range(8):
            kp = hp % 2
            b.wait(t, psK_rel[kp][0], psK_rel[kp][1])
            for c in range(8):
                mm = t.matmul(psK[kp], wqkv[:, c * 3072 + 1024 + hp * 128: c * 3072 + 1024 + (hp + 1) * 128],
                              xnTt[:, c * 512:(c + 1) * 512], start=(c == 0), stop=(c == 7))
            pe_cnt = b.inc("pe", mm)
            b.wait(v, "pe", pe_cnt)
            psK_rel[kp] = ("kev", b.inc("kev", v.tensor_scalar(
                K_sb[:, hp * 4096 + j * 512: hp * 4096 + (j + 1) * 512],
                psK[kp], bq_cols[:, 8 + hp:9 + hp], None, ALU.add)))
        b.wait(v, f"dv{par}", vst_rel[par])
        for kk in range(4):
            for vh in range(2):
                vp = (kk * 2 + vh) % 2
                b.wait(t, "vev", psV_rel[vp])
                for c in range(8):
                    t.matmul(psV[vp], xnTt[:, c * 512 + kk * 128: c * 512 + (kk + 1) * 128],
                             wqkv[:, c * 3072 + 2048 + vh * 512: c * 3072 + 2048 + (vh + 1) * 512],
                             start=(c == 0), stop=False)
                mm = t.matmul(psV[vp], P.ones_row, bv_row[0:1, vh * 512:(vh + 1) * 512],
                              start=False, stop=True)
                pe_cnt = b.inc("pe", mm)
                b.wait(v, "pe", pe_cnt)
                src = psV[vp].rearrange("p (h d) -> p h d", d=64)
                for gsub in range(4):
                    hg = vh * 4 + gsub
                    dst = vstage[par][:, kk * 1040 + hg * 130: kk * 1040 + (hg + 1) * 130]
                    dst = dst.rearrange("p (h e) -> p h e", e=65)[:, :, 0:64]
                    cnt = b.inc("vev", v.tensor_copy(dst, src[:, gsub * 2:gsub * 2 + 2, :]))
                psV_rel[vp] = cnt
        b.wait(y, "vev", b.n("vev"))
        for hg in range(8):
            src = vstage[par].rearrange("p (k g e) -> p k g e", g=8, e=130)[:, :, hg, :]
            b.inc(f"dv{par}", y.dma_start(out=P.v_dram[hg, :, j * 4:(j + 1) * 4, :], in_=src), 16)
        vst_rel[par] = b.n(f"dv{par}")
        for hp in range(8):
            qp = hp % 2
            b.wait(t, psK_rel[qp][0], psK_rel[qp][1])
            for c in range(8):
                mm = t.matmul(psK[qp], wqkv[:, c * 3072 + hp * 128: c * 3072 + (hp + 1) * 128],
                              xnTt[:, c * 512:(c + 1) * 512], start=(c == 0), stop=(c == 7))
            pe_cnt = b.inc("pe", mm)
            b.wait(v, "pe", pe_cnt)
            b.wait(v, f"dq{qp}", qst_rel[qp])
            cnt = b.inc("qev", v.tensor_scalar(qstage[qp], psK[qp],
                                               bq_cols[:, hp:hp + 1], None, ALU.add))
            psK_rel[qp] = ("qev", cnt)
            b.wait(y, "qev", b.n("qev"))
            b.inc(f"dq{qp}", y.dma_start(out=P.q_dram[hp, :, j, :], in_=qstage[qp]), 16)
            qst_rel[qp] = b.n(f"dq{qp}")
    while pend_rb:
        pend_rb.pop(0)()
    es.close()
    b.barrier()



def emit_p2(P, K_sb):
    nc, b, dr = P.nc, P.b, P.dr
    t, v, s, g, y = b.t, b.v, b.s, b.g, b.y
    es = ExitStack()
    tri = es.enter_context(nc.sbuf_tensor("tri", [128, 2048], BF16)).ap()
    wproj = es.enter_context(nc.sbuf_tensor("wproj", [128, 8 * 1024], BF16)).ap()
    bp_row = es.enter_context(nc.sbuf_tensor("bp_row", [1, 1024], BF16)).ap()
    y_sb = es.enter_context(nc.sbuf_tensor("y_sb", [128, 8 * 512], BF16)).ap()
    exp2 = [es.enter_context(nc.sbuf_tensor(f"exp2_{i}", [128, 1024], BF16)).ap() for i in range(2)]
    vbuf = [es.enter_context(nc.sbuf_tensor(f"vbuf{i}", [128, 32 * 130], BF16)).ap() for i in range(2)]
    qbuf = [es.enter_context(nc.sbuf_tensor(f"qbuf{i}", [128, 512], BF16)).ap() for i in range(2)]
    bc_sb = es.enter_context(nc.sbuf_tensor("bc_sb", [64, 512], BF16)).ap()
    srow = [es.enter_context(nc.sbuf_tensor(f"srowp{i}", [1, 1024], F32)).ap() for i in range(2)]
    xsm = [es.enter_context(nc.sbuf_tensor(f"xsm{i}", [128, 1024], F32)).ap() for i in range(2)]
    psSd = [es.enter_context(nc.psum_tensor(f"psSd{i}", [128, 1024], F32)).ap() for i in range(2)]
    psAV = [es.enter_context(nc.psum_tensor(f"psAVd{i}", [65, 512], F32)).ap() for i in range(2)]
    psOB = es.enter_context(nc.psum_tensor("psOB", [128, 512], F32)).ap()
    psB = psOB[0:64, :]
    b.inc("dw3", g.dma_start(out=tri, in_=dr["l0_tri"]), 16)
    dma3(b, g, wproj, dr["l0_wproj"], 1024, "dw3")
    b.inc("dw3", g.dma_start(out=bp_row, in_=dr["l0_bproj_row"]), 16)
    b.wait(t, "dw3")
    b.wait(v, "dw3")

    exp_rel = {0: ("act", 0), 1: ("act", 0)}
    expbuf_rel = {0: 0, 1: 0}
    vbuf_rel = {0: 0, 1: 0}
    qbuf_rel = {0: 0, 1: 0}
    dvl_cnt = {0: 0, 1: 0}
    dql_cnt = {0: 0, 1: 0}
    psO_rel4 = {0: 0, 1: 0, 2: 0, 3: 0}
    oad_rel2 = {0: 0, 1: 0}
    dx1_rel = {0: 0, 1: 0}
    P._p2_psO_rel = 0
    P._p2_pending = []
    P._p2_avwar = {0: 0, 1: 0}
    P._p2_state = {"srow_war": 0, "psb_war": 0, "bce_war": 0, "vv_rec": 0}

    def make_epi1(av_all, par):
        def epi():
            st = P._p2_state
            pav = psAV[par]
            b.wait(v, "av", av_all)
            b.wait(v, "pe", st["srow_war"])
            b.vstep(v.tensor_scalar(srow[par][0:1, 0:512], pav[64:65, :], 1e-30, None, ALU.max))
            b.vdep()
            b.vstep(v.reciprocal(srow[par][0:1, 512:1024], srow[par][0:1, 0:512]))
            st["vv_rec"] = b.n("vv")
        return epi

    def make_epi2(hh, hg, par):
        def epi():
            st = P._p2_state
            pav = psAV[par]
            b.wait(t, "vv", st["vv_rec"])
            b.wait(t, "bce", st["psb_war"])
            b.wait(t, "oadd", P._p2_psO_rel)
            mmb = t.matmul(psB, P.ones_row_f[0:1, 0:64], srow[par][0:1, 512:1024], start=True, stop=True)
            st["srow_war"] = peb = b.inc("pe", mmb)
            b.wait(s, "pe", peb)
            b.wait(s, "ynorm", st["bce_war"])
            st["psb_war"] = b.inc("bce", s.copy(bc_sb, psB))
            b.wait(v, "bce", st["psb_war"])
            yn = v.tensor_tensor(y_sb[64 * hh:64 * hh + 64, hg * 512:(hg + 1) * 512],
                                 pav[0:64, :], bc_sb, ALU.mult)
            st["bce_war"] = b.inc("ynorm", yn)
            P._p2_avwar[par] = st["bce_war"]
        return epi

    for sl in range(8):
        KT = KT_FULL[sl]
        plist = [(hg, hh, kt2) for hg in range(8) for hh in range(2)
                 for kt2 in range(0, KT, 2)]

        def issue_hg(isl, hg):
            gp = hg % 2
            ktl = KT_FULL[isl]
            b.wait(y, "av", vbuf_rel[gp])
            b.inc(f"dvl{gp}", y.dma_start(
                out=vbuf[gp][:, 0:ktl * 130].rearrange("p (k e) -> p k e", e=130),
                in_=P.v_dram[hg, :, 0:ktl, :]), 16)
            dvl_cnt[gp] = b.n(f"dvl{gp}")
            b.wait(y, "pe", qbuf_rel[gp])
            b.inc(f"dql{gp}", y.dma_start(out=qbuf[gp], in_=P.q_dram[hg, :, isl, :]), 16)
            dql_cnt[gp] = b.n(f"dql{gp}")

        if sl == 0:
            issue_hg(0, 0)
            issue_hg(0, 1)

        def emit_S(j):
            hg, hh, kt2 = plist[j]
            gp = hg % 2
            sp = j % 2
            b.wait(t, exp_rel[sp][0], exp_rel[sp][1])
            if hh == 0 and kt2 == 0:
                b.wait(t, f"dql{gp}", dql_cnt[gp])
            mm = None
            for u in (0, 1):
                kt = kt2 + u
                mm = t.matmul(psSd[sp][:, u * 512:(u + 1) * 512],
                              K_sb[64 * hh:64 * hh + 64, hg * 4096 + kt * 128: hg * 4096 + (kt + 1) * 128],
                              qbuf[gp][64 * hh:64 * hh + 64, :],
                              start=True, stop=True)
            qbuf_rel[gp] = b.inc("pe", mm)
            return b.n("pe")

        pe_S = {0: emit_S(0)}
        for j, (hg, hh, kt2) in enumerate(plist):
            gp = hg % 2
            sp = j % 2
            b.wait(s, "pe", pe_S[j])
            b.wait(s, "av", expbuf_rel[sp])
            e_in = s.activation(exp2[sp], psSd[sp], AF.Exp, scale=0.125)
            act_cnt = b.inc("act", e_in)
            exp_rel[sp] = ("act", act_cnt)
            wait_for_av = ("act", act_cnt)
            for u in (0, 1):
                jj = kt2 + u - (KT - 4)
                if jj >= 0:
                    b.wait(v, "act", act_cnt)
                    b.wait(v, "av", expbuf_rel[sp])
                    mk = v.tensor_tensor(exp2[sp][:, u * 512:(u + 1) * 512],
                                         exp2[sp][:, u * 512:(u + 1) * 512],
                                         tri[:, jj * 512:(jj + 1) * 512], ALU.mult)
                    dve_cnt = b.inc("dve", mk)
                    exp_rel[sp] = ("dve", dve_cnt)
                    wait_for_av = ("dve", dve_cnt)
            if j + 1 < len(plist):
                pe_S[j + 1] = emit_S(j + 1)
            while P._p2_pending and P._p2_pending[0][0] <= j:
                P._p2_pending.pop(0)[1]()
            b.wait(t, wait_for_av[0], wait_for_av[1])
            if kt2 == 0:
                b.wait(t, f"dvl{gp}", dvl_cnt[gp])
                b.wait(t, "ynorm", P._p2_avwar[hh])
            mm = None
            for u in (0, 1):
                kt = kt2 + u
                mm = t.matmul(psAV[hh], vbuf[gp][:, kt * 130 + hh * 65: kt * 130 + hh * 65 + 65],
                              exp2[sp][:, u * 512:(u + 1) * 512],
                              start=(kt == 0), stop=(kt == KT - 1))
            expbuf_rel[sp] = b.inc("av", mm)
            vbuf_rel[gp] = b.n("av")
            if kt2 == KT - 2:
                P._p2_pending.append((j + 1, make_epi1(b.n("av"), hh)))
                P._p2_pending.append((j + 3, make_epi2(hh, hg, hh)))
                if hh == 1:
                    if hg + 2 < 8:
                        issue_hg(sl, hg + 2)
                    elif sl + 1 < 8:
                        issue_hg(sl + 1, hg - 6)
        while P._p2_pending:
            P._p2_pending.pop(0)[1]()
        b.wait(t, "ynorm", b.n("ynorm"))
        for sp_ in (0, 1):
            b.wait(t, exp_rel[sp_][0], exp_rel[sp_][1])
        for tt in range(4):
            tp_ = tt % 2
            dsm = "dx" if tp_ == 0 else "dxa"
            ssm = "dx1" if tp_ == 0 else "dxb"
            b.wait(y, "oadd", oad_rel2[tp_])
            b.wait(y, ssm, dx1_rel[tp_])
            b.inc(dsm, y.dma_start(
                out=xsm[tp_], in_=dr["x_abs"][(sl * 4 + tt) * 128:(sl * 4 + tt + 1) * 128, :]), 16)
            dx_t = b.n(dsm)
            for dh in range(2):
                psO_ = psSd[tp_][:, dh * 512:(dh + 1) * 512]
                b.wait(t, "oadd", psO_rel4[tp_ * 2 + dh])
                for c in range(8):
                    t.matmul(psO_, y_sb[:, c * 512 + tt * 128: c * 512 + (tt + 1) * 128],
                             wproj[:, c * 1024 + dh * 512: c * 1024 + (dh + 1) * 512],
                             start=(c == 0), stop=False)
                mm = t.matmul(psO_, P.ones_row, bp_row[0:1, dh * 512:(dh + 1) * 512],
                              start=False, stop=True)
                pe_cnt = b.inc("pe", mm)
                b.wait(v, "pe", pe_cnt)
                b.wait(v, dsm, dx_t)
                ad = v.tensor_tensor(xsm[tp_][:, dh * 512:(dh + 1) * 512],
                                     xsm[tp_][:, dh * 512:(dh + 1) * 512], psO_, ALU.add)
                psO_rel4[tp_ * 2 + dh] = b.inc("oadd", ad)
            oad_rel2[tp_] = b.n("oadd")
            b.wait(y, "oadd", oad_rel2[tp_])
            b.inc(ssm, y.dma_start(
                out=P.x1_dram[(sl * 4 + tt) * 128:(sl * 4 + tt + 1) * 128, :], in_=xsm[tp_]), 16)
            dx1_rel[tp_] = b.n(ssm)
        exp_rel[0] = ("oadd", psO_rel4[1])
        exp_rel[1] = ("oadd", psO_rel4[3])
    es.close()
    b.barrier()



def emit_p3(P):
    nc, b, dr = P.nc, P.b, P.dr
    t, v, s, g, y = b.t, b.v, b.s, b.g, b.y
    es = ExitStack()
    xnb = [es.enter_context(nc.sbuf_tensor(f"xnb{i}", [128, 8 * 512], BF16)).ap() for i in range(2)]
    w13 = [es.enter_context(nc.sbuf_tensor(f"w13_{i}", [128, 8 * 2048], BF16)).ap() for i in range(2)]
    w2 = [es.enter_context(nc.sbuf_tensor(f"w2_sb{i}", [128, 8 * 1024], BF16)).ap() for i in range(2)]
    act_sb = es.enter_context(nc.sbuf_tensor("act_sb", [128, 8 * 512], BF16)).ap()
    b13 = es.enter_context(nc.sbuf_tensor("b13", [128, 128], F32)).ap()
    b13u = es.enter_context(nc.sbuf_tensor("b13u", [128, 64], F32)).ap()
    b2row = [es.enter_context(nc.sbuf_tensor(f"b2row{i}", [1, 1024], BF16)).ap() for i in range(2)]
    g_s = es.enter_context(nc.sbuf_tensor("g_s", [128, 512], F32)).ap()
    s_s = es.enter_context(nc.sbuf_tensor("s_s", [128, 512], F32)).ap()
    u_s = es.enter_context(nc.sbuf_tensor("u_s", [128, 512], F32)).ap()
    contrib = [es.enter_context(nc.sbuf_tensor(f"contrib{i}", [128, 1024], F32)).ap() for i in range(4)]
    psA = [es.enter_context(nc.psum_tensor(f"psA_{i}", [128, 512], F32)).ap() for i in range(2)]
    psB2 = [es.enter_context(nc.psum_tensor(f"psB2_{i}", [128, 512], F32)).ap() for i in range(2)]
    psM = [es.enter_context(nc.psum_tensor(f"psM{i}", [128, 512], F32)).ap() for i in range(4)]

    b.inc("dmo", g.dma_start(out=b13, in_=dr["l0_b13_cols"]), 16)
    b.wait(v, "dmo", 16)
    b.vstep(v.tensor_scalar(
        b13u.rearrange("p (e f) -> p e f", f=8),
        b13.rearrange("p (e f) -> p e f", f=16)[:, :, 8:16],
        1.0, None, ALU.add))
    b13u_vv = b.n("vv")

    w13_rel = [0, 0]
    dma3(b, g, w13[0], dr["l0_w13"][0], 2048, "dm13_0")
    dma3(b, g, w2[0], dr["l0_w2"][0], 1024, "dmo2_0")
    b.inc("dme2_0", g.dma_start(out=b2row[0], in_=dr["l0_b2_rows"][0:1, 0:1024]), 16)
    psA_rel = [0, 0]
    psB2_rel = [0, 0]
    w2_rel = [0, 0]
    b2_rel = [0, 0]
    psM_rel = [0, 0, 0, 0]
    act_war = 0
    gs_war = 0
    ss_war = 0
    acc_reg = {}
    xnb_rel = [0, 0]
    cb_rel = [0, 0, 0, 0]
    dxnl_cnt = {0: 0, 1: 0}

    def issue_xnb(xp, tc):
        b.wait(y, "pe", xnb_rel[xp])
        b.inc(f"dxnl{xp}", y.dma_start(
            out=xnb[xp].rearrange("p (c f) -> p c f", f=512),
            in_=P.xnt_dram[tc]), 16)
        dxnl_cnt[xp] = b.n(f"dxnl{xp}")

    for e in range(8):
        ep = e % 2
        if e + 1 < 8:
            b.wait(g, "pe", w13_rel[1 - ep])
            dma3(b, g, w13[1 - ep], dr["l0_w13"][e + 1], 2048, f"dm13_{1 - ep}")
            b.wait(g, "pe", w2_rel[1 - ep])
            dma3(b, g, w2[1 - ep], dr["l0_w2"][e + 1], 1024, f"dmo2_{1 - ep}")
            b.wait(g, "pe", b2_rel[1 - ep])
            b.inc(f"dme2_{1 - ep}", g.dma_start(
                out=b2row[1 - ep], in_=dr["l0_b2_rows"][0:1, (e + 1) * 1024:(e + 2) * 1024]), 16)
        b.wait(t, f"dm13_{ep}", b.n(f"dm13_{ep}"))
        b.wait(t, f"dmo2_{ep}", b.n(f"dmo2_{ep}"))
        b.wait(t, f"dme2_{ep}", b.n(f"dme2_{ep}"))
        for tc in range(8):
            xp = tc % 2
            if e == 0 and tc == 0:
                issue_xnb(0, 0)
            b.wait(t, f"dxnl{xp}", dxnl_cnt[xp])
            for ip in range(8):
                if ip == 3:
                    if tc + 1 < 8:
                        issue_xnb(1 - xp, tc + 1)
                    elif e + 1 < 8:
                        issue_xnb(1 - xp, 0)
                ipp = ip % 2
                b.wait(t, "mgl", psA_rel[ipp])
                for c in range(8):
                    mm = t.matmul(psA[ipp], w13[ep][:, c * 2048 + ip * 128: c * 2048 + (ip + 1) * 128],
                                  xnb[xp][:, c * 512:(c + 1) * 512], start=(c == 0), stop=(c == 7))
                peA = b.inc("pe", mm)
                b.wait(t, "mup", psB2_rel[ipp])
                for c in range(8):
                    mm = t.matmul(psB2[ipp], w13[ep][:, c * 2048 + 1024 + ip * 128: c * 2048 + 1024 + (ip + 1) * 128],
                                  xnb[xp][:, c * 512:(c + 1) * 512], start=(c == 0), stop=(c == 7))
                peB = b.inc("pe", mm)
                w13_rel[ep] = peB
                b.wait(v, "pe", peA)
                b.wait(v, "sil", gs_war)
                psA_rel[ipp] = b.inc("mgl", v.tensor_scalar(
                    g_s, psA[ipp], b13[:, e * 16 + ip:e * 16 + ip + 1], 7.0, ALU.add, ALU.min))
                b.wait(s, "mgl", psA_rel[ipp])
                b.wait(s, "mact", ss_war)
                gs_war = b.inc("sil", s.activation(s_s, g_s, AF.Silu, scale=1.702))
                b.vdep(b13u_vv)
                b.wait(v, "mact", ss_war)
                b.wait(v, "pe", peB)
                b.wait(v, "pe", act_war)
                psB2_rel[ipp] = b.inc("mup", v.tensor_scalar(
                    u_s, psB2[ipp], b13u[:, e * 8 + ip:e * 8 + ip + 1], 8.0, ALU.add, ALU.min))
                b.wait(v, "sil", gs_war)
                b.wait(v, "mup", psB2_rel[ipp])
                ss_war = b.inc("mact", v.scalar_tensor_tensor(
                    act_sb[:, ip * 512:(ip + 1) * 512], u_s, -6.0, s_s, ALU.max, ALU.mult))
            xnb_rel[xp] = b.n("pe")
            b.wait(t, "mact", b.n("mact"))
            for tt in range(4):
                T = tc * 4 + tt
                cp = tt
                for dh in range(2):
                    mp = (tt * 2 + dh) % 4
                    b.wait(t, "macc", psM_rel[mp])
                    for ic in range(8):
                        t.matmul(psM[mp], act_sb[:, ic * 512 + tt * 128: ic * 512 + (tt + 1) * 128],
                                 w2[ep][:, ic * 1024 + dh * 512: ic * 1024 + (dh + 1) * 512],
                                 start=(ic == 0), stop=False)
                    mm = t.matmul(psM[mp], P.row1702,
                                  b2row[ep][0:1, dh * 512:(dh + 1) * 512], start=False, stop=True)
                    peM = b.inc("pe", mm)
                    b.wait(s, "pe", peM)
                    if cb_rel[cp] != 0:
                        b.wait(s, cb_rel[cp][0], cb_rel[cp][1])
                    psM_rel[mp] = b.inc("macc", s.activation(
                        contrib[cp][:, dh * 512:(dh + 1) * 512], psM[mp], AF.Identity,
                        scale=P.wsc[:, T * 8 + e: T * 8 + e + 1]))
                b.wait(g, "macc", b.n("macc"))
                dsem = f"dacc{T % 16}"
                b.wait(g, dsem, b.n(dsem))
                b.inc(dsem, g.dma_start(
                    out=P.x1_dram[T * 128:(T + 1) * 128, :], in_=contrib[cp],
                    accum_op=ALU.add), 16)
                cb_rel[cp] = (dsem, b.n(dsem))
            act_war = b.n("pe")
        w2_rel[ep] = b2_rel[ep] = b.n("pe")
    es.close()
    b.barrier()



def emit_p4(P, xn1T):
    nc, b, dr = P.nc, P.b, P.dr
    t, v, s, g, y = b.t, b.v, b.s, b.g, b.y
    es = ExitStack()
    xA = [es.enter_context(nc.sbuf_tensor(f"xA{i}", [128, 4096], F32)).ap() for i in range(2)]
    xB = [es.enter_context(nc.sbuf_tensor(f"xB{i}", [128, 4096], F32)).ap() for i in range(2)]
    xnf = [es.enter_context(nc.sbuf_tensor(f"xnf4{i}", [128, 1024], F32)).ap() for i in range(2)]
    scr = es.enter_context(nc.sbuf_tensor("scr4", [128, 1024], F32)).ap()
    xnTf = es.enter_context(nc.sbuf_tensor("xnTf4", [128, 1024], F32)).ap()
    gwT = es.enter_context(nc.sbuf_tensor("gwT", [128, 8 * 16], F32)).ap()
    gb_sb = es.enter_context(nc.sbuf_tensor("gb_sb", [128, 16], F32)).ap()
    gbrow = es.enter_context(nc.sbuf_tensor("gbrow", [1, 16], F32)).ap()
    tsc = es.enter_context(nc.sbuf_tensor("tsc", [128, 320], F32)).ap()
    psT = [es.enter_context(nc.psum_tensor(f"psT4_{i}", [128, 128], F32)).ap() for i in range(2)]
    psG = es.enter_context(nc.psum_tensor("psG", [128, 64], F32)).ap()
    dma3(b, g, gwT, dr["l1_gate_wT"], 16, "dl1")
    b.inc("dl1", g.dma_start(out=gbrow, in_=dr["l1_gate_b_row"]), 16)
    b.wait(t, "dl1")
    b.wait(t, "cmisc")
    b.wait(t, "dmisc")
    mmgb = t.matmul(psG[:, 0:16], P.ones_row_f[0:1, :], gbrow[0:1, :], start=True, stop=True)
    pegb = b.inc("pe", mmgb)
    b.wait(v, "pe", pegb)
    b.vstep(v.tensor_copy(gb_sb, psG[:, 0:16]))
    gb_vv = b.n("vv")
    b.wait(v, "dmisc")

    psT_rel = [0, 0]
    psG_rel4 = [("vv", gb_vv), None, None, None]
    sig_war4 = [0, 0, 0, 0]
    pend_gb = []
    ln_done = 0
    ln_rel4 = {0: 0, 1: 0}
    sel_rel4 = {0: 0, 1: 0}
    xnf_rel4 = {0: 0, 1: 0}
    dxa_cnt = {0: 0, 1: 0}
    dxb_cnt = {0: 0, 1: 0}

    def issue_x14(tloc):
        pp = tloc % 2
        offA = 512 * tloc
        offB = (2048 + 512 * tloc) % 4096
        sma = "dxb" if pp == 0 else "dv0"
        smb = "dv1" if pp == 0 else "dq1"
        b.wait(y, "vv", sel_rel4[pp])
        b.wait(y, "ln_out", ln_rel4[pp])
        dma3(b, y, xA[pp], P.x1_dram[offA:offA + 512, :], 1024, sma)
        dma3(b, y, xB[pp], P.x1_dram[offB:offB + 512, :], 1024, smb)
        dxa_cnt[pp] = (sma, b.n(sma))
        dxb_cnt[pp] = (smb, b.n(smb))

    issue_x14(0)
    for tloc in range(8):
        pp = tloc % 2
        if tloc + 1 < 8:
            issue_x14(tloc + 1)
        b.wait(v, dxa_cnt[pp][0], dxa_cnt[pp][1])
        b.wait(v, dxb_cnt[pp][0], dxb_cnt[pp][1])
        b.wait(v, "ln_out", b.n("ln_out"))
        b.vdep()
        b.vstep(v.tensor_scalar(xA[pp], xA[pp], P.hsel[:, 0:1], None, ALU.mult))
        b.vdep()
        b.vstep(v.scalar_tensor_tensor(xA[pp], xB[pp], P.hsel[:, 1:2], xA[pp], ALU.mult, ALU.add))
        sel_rel4[pp] = b.n("vv")
        P.emit_ln_stats(xA[pp], scr, [("vv", sel_rel4[pp])])
        for tt in range(4):
            T = tloc * 4 + tt
            fp_ = tt % 2
            b.wait(s, "pe", xnf_rel4[fp_])
            ln_done = P.emit_ln_tile(xA[pp], tt, xnf[fp_])
            if tt == 3:
                ln_rel4[pp] = ln_done
            for c in range(8):
                tp = c % 2
                b.wait(t, "xe4", psT_rel[tp])
                b.wait(t, "ln_out", ln_done)
                pe_cnt = b.inc("pe", t.transpose(psT[tp], xnf[fp_][:, c * 128:(c + 1) * 128], P.ident))
                b.wait(v, "pe", pe_cnt)
                psT_rel[tp] = b.inc("xe4", v.tensor_copy(
                    xn1T[:, c * 4096 + T * 128: c * 4096 + (T + 1) * 128], psT[tp]))
                if tloc < 4:
                    psT_rel[tp] = b.inc("xe4", v.tensor_copy(xnTf[:, c * 128:(c + 1) * 128], psT[tp]))
            xnf_rel4[fp_] = b.n("pe")
            if tloc >= 4:
                continue
            b.wait(t, "xe4", b.n("xe4"))
            if pend_gb:
                pend_gb.pop(0)()
            slot = psG[:, tt * 16:(tt + 1) * 16]
            sc = tsc[:, tt * 80:(tt + 1) * 80]
            if psG_rel4[tt] is not None:
                b.wait(t, psG_rel4[tt][0], psG_rel4[tt][1])
            for c in range(8):
                mm = t.matmul(slot, xnTf[:, c * 128:(c + 1) * 128], gwT[:, c * 16:(c + 1) * 16],
                              start=(c == 0), stop=(c == 7))
            gmm = b.inc("rmm", mm)
            b.wait(s, "rmm", gmm)
            b.wait(s, "vv", sig_war4[tt])
            b.sstep(s.activation(sc[:, 0:16], slot, AF.Sigmoid))
            psG_rel4[tt] = ("ss", b.n("ss"))
            gex_ss = b.n("ss")

            def mk_gateB(tt=tt, T=T, sc=sc, gex_ss=gex_ss):
                def f():
                    b.wait(v, "ss", gex_ss)
                    b.vdep(gb_vv)
                    b.vstep(v.tensor_tensor(sc[:, 16:32], sc[:, 0:16], gb_sb, ALU.add))
                    b.vdep()
                    b.vstep(v.tensor_copy(sc[:, 32:48], sc[:, 16:32]))
                    for it in range(4):
                        b.vdep()
                        b.vstep(v.reduce_max(sc[:, 64 + it:65 + it], sc[:, 32:48], AXX))
                        if it < 3:
                            b.vdep()
                            b.vstep(v.tensor_scalar(sc[:, 48:64], sc[:, 32:48], sc[:, 64 + it:65 + it], None, ALU.is_equal))
                            b.vdep()
                            b.vstep(v.scalar_tensor_tensor(sc[:, 32:48], sc[:, 48:64], -1e30, sc[:, 32:48], ALU.mult, ALU.add))
                    b.vdep()
                    b.vstep(v.tensor_scalar(sc[:, 48:64], sc[:, 16:32], sc[:, 67:68], None, ALU.is_ge))
                    b.vdep()
                    b.vstep(v.tensor_tensor(sc[:, 32:48], sc[:, 0:16], sc[:, 48:64], ALU.mult))
                    b.vdep()
                    b.vstep(v.reduce_sum(sc[:, 68:69], sc[:, 32:48], AXX))
                    b.vdep()
                    b.vstep(v.reciprocal(sc[:, 69:70], sc[:, 68:69]))
                    b.vdep()
                    b.vstep(v.tensor_scalar(P.comb[:, T * 16:(T + 1) * 16], sc[:, 32:48],
                                            sc[:, 69:70], 2.5, ALU.mult, ALU.mult))
                    sig_war4[tt] = b.n("vv")
                return f
            pend_gb.append(mk_gateB())
    while pend_gb:
        pend_gb.pop(0)()
    es.close()
    b.barrier()


def emit_p4b(P, xn1T, pooled, rk_sb, rv_sb):
    nc, b, dr = P.nc, P.b, P.dr
    t, v, s, g, y = b.t, b.v, b.s, b.g, b.y
    es = ExitStack()
    rkvw = es.enter_context(nc.sbuf_tensor("rkvw", [128, 8 * 512], BF16)).ap()
    rkb = es.enter_context(nc.sbuf_tensor("rkb", [128, 2], F32)).ap()
    rvb = es.enter_context(nc.sbuf_tensor("rvb", [1, 256], BF16)).ap()
    psP1 = es.enter_context(nc.psum_tensor("psP1", [128, 512], F32)).ap()
    dma3(b, g, rkvw, dr["l1_rkv_w"], 512, "drkv")
    b.inc("drkv", g.dma_start(out=rkb, in_=dr["l1_rk_b_cols"]), 16)
    b.inc("drkv", g.dma_start(out=rvb, in_=dr["l1_rv_b_row"]), 16)
    b.wait(t, "drkv")
    b.inc("rvones", v.memset(rv_sb, 1.0))
    b.wait(t, "rvones")
    poolf = es.enter_context(nc.sbuf_tensor("poolf", [128, 64], F32)).ap()
    for c in range(8):
        src = xn1T[:, c * 4096:(c + 1) * 4096].rearrange("p (l k) -> p l k", k=64)
        b.vdep()
        b.vstep(v.reduce_sum(poolf, src, AXX))
        b.vdep()
        b.vstep(v.tensor_scalar(pooled[:, c * 64:(c + 1) * 64],
                                poolf, 1.0 / 64.0, None, ALU.mult))
    pooled_bf = rv_sb
    b.wait(t, "vv", b.n("vv"))
    for rt in range(2):
        b.wait(t, "rkev", b.n("rkev"))
        for c in range(8):
            mm = t.matmul(psP1[:, 0:64], rkvw[:, c * 512 + rt * 128: c * 512 + (rt + 1) * 128],
                          pooled[:, c * 64:(c + 1) * 64], start=(c == 0), stop=(c == 7))
        pe_cnt = b.inc("pe", mm)
        b.wait(s, "pe", pe_cnt)
        b.inc("rkev", s.activation(rk_sb[:, rt * 64:(rt + 1) * 64], psP1[:, 0:64],
                                   AF.Identity, bias=rkb[:, rt:rt + 1]))
    b.wait(t, "rkev", b.n("rkev"))
    for c in range(8):
        t.matmul(psP1[0:64, 0:256], pooled[:, c * 64:(c + 1) * 64],
                 rkvw[:, c * 512 + 256: c * 512 + 512], start=(c == 0), stop=False)
    mm = t.matmul(psP1[0:64, 0:256], P.ones_row[0:1, 0:64], rvb[0:1, :], start=False, stop=True)
    pe_cnt = b.inc("pe", mm)
    b.wait(v, "pe", pe_cnt)
    dstv = rv_sb.rearrange("p (h e) -> p h e", e=65)[:, :, 0:64]
    b.vstep(v.tensor_copy(dstv, psP1[0:64, 0:256].rearrange("p (h d) -> p h d", d=64)))
    b.barrier()
    es.close()



def emit_p5(P, xn1T, pooled, rk_sb, rv_sb):
    nc, b, dr = P.nc, P.b, P.dr
    t, v, s, g, y = b.t, b.v, b.s, b.g, b.y
    es = ExitStack()
    wkv1 = es.enter_context(nc.sbuf_tensor("wkv1", [128, 8 * 2048], BF16)).ap()
    wq1 = es.enter_context(nc.sbuf_tensor("wq1", [128, 8 * 1024], BF16)).ap()
    rqw = es.enter_context(nc.sbuf_tensor("rqw", [128, 8 * 256], BF16)).ap()
    routw = es.enter_context(nc.sbuf_tensor("routw", [128, 2 * 1024], BF16)).ap()
    bq1 = es.enter_context(nc.sbuf_tensor("bq1", [128, 24], F32)).ap()
    bv1 = es.enter_context(nc.sbuf_tensor("bv1", [1, 1024], BF16)).ap()
    rqb = es.enter_context(nc.sbuf_tensor("rqb", [128, 2], F32)).ap()
    routb = es.enter_context(nc.sbuf_tensor("routb", [128, 8], F32)).ap()
    glb = es.enter_context(nc.sbuf_tensor("glb", [128, 2], BF16)).ap()
    grb = es.enter_context(nc.sbuf_tensor("grb", [128, 2], BF16)).ap()
    mloc4 = es.enter_context(nc.sbuf_tensor("mloc4", [128, 4096], BF16)).ap()
    mlat = es.enter_context(nc.sbuf_tensor("mlat", [64, 2048], BF16)).ap()
    K1 = es.enter_context(nc.sbuf_tensor("K1", [128, 8 * 640], BF16)).ap()
    V1 = es.enter_context(nc.sbuf_tensor("V1", [128, 5 * 1040], BF16)).ap()
    Q1 = es.enter_context(nc.sbuf_tensor("Q1", [128, 8 * 512], BF16)).ap()
    rq1 = es.enter_context(nc.sbuf_tensor("rq1", [128, 2 * 512], BF16)).ap()
    lat1 = es.enter_context(nc.sbuf_tensor("lat1", [128, 2 * 512], BF16)).ap()
    rexp = es.enter_context(nc.sbuf_tensor("rexp", [64, 512], BF16)).ap()
    exp4 = [es.enter_context(nc.sbuf_tensor(f"exp4_{i}", [128, 1024], BF16)).ap() for i in range(2)]
    loc1 = [es.enter_context(nc.sbuf_tensor(f"loc1_{i}", [128, 512], BF16)).ap() for i in range(2)]
    rem1 = es.enter_context(nc.sbuf_tensor("rem1", [128, 512], BF16)).ap()
    d1f = es.enter_context(nc.sbuf_tensor("d1f", [128, 512], F32)).ap()
    bc1 = es.enter_context(nc.sbuf_tensor("bc1", [64, 512], BF16)).ap()
    g1 = es.enter_context(nc.sbuf_tensor("g1", [65, 512], F32)).ap()
    srow1 = es.enter_context(nc.sbuf_tensor("srow1", [1, 1024], F32)).ap()
    mxo = [es.enter_context(nc.sbuf_tensor(f"mxo{i}", [128, 512], BF16)).ap() for i in range(2)]
    psSc = [es.enter_context(nc.psum_tensor(f"psSc{i}", [128, 1024], F32)).ap() for i in range(2)]
    psAV4 = [es.enter_context(nc.psum_tensor(f"psAV4_{i}", [65, 512], F32)).ap() for i in range(2)]
    psB1 = es.enter_context(nc.psum_tensor("psB1", [65, 512], F32)).ap()
    psX = es.enter_context(nc.psum_tensor("psX", [128, 512], F32)).ap()
    psPa = psX[:, 0:128]
    psPb = psSc[0][:, 0:512]
    psL = psSc[1][0:64, 0:512]
    psLAV = psSc[1][0:65, 512:1024]

    dma3(b, g, wkv1, dr["l1_wqkv"][:, 1024:3072], 2048, "da1")
    dma3(b, g, wq1, dr["l1_wqkv"][:, 0:1024], 1024, "da1")
    dma3(b, g, rqw, dr["l1_rq_w"], 256, "da1")
    dma3(b, g, routw, dr["l1_rout_w"], 1024, "da1")
    b.inc("da1", g.dma_start(out=bq1, in_=dr["l1_bqkv_cols"]), 16)
    b.inc("da1", g.dma_start(out=bv1, in_=dr["l1_bv_row"]), 16)
    b.inc("da1", g.dma_start(out=rqb, in_=dr["l1_rq_b_cols"]), 16)
    b.inc("da1", g.dma_start(out=routb, in_=dr["l1_rout_b_cols"]), 16)
    b.inc("da1", g.dma_start(out=glb, in_=dr["l1_gl_blk"]), 16)
    b.inc("da1", g.dma_start(out=grb, in_=dr["l1_gr_blk"]), 16)
    b.inc("da1", g.dma_start(out=mloc4, in_=dr["l1_mloc4"]), 16)
    b.inc("da1", g.dma_start(out=mlat, in_=dr["l1_mlat"]), 16)
    b.inc("v1ones", v.memset(V1, 1.0))
    b.wait(t, "da1")
    b.wait(t, "v1ones")
    b.wait(v, "da1")
    b.wait(s, "da1")

    def halo_cols(c, sl):
        if sl == 0:
            return xn1T[:, c * 4096 + 3968: c * 4096 + 4096]
        return xn1T[:, c * 4096 + sl * 512 - 128: c * 4096 + sl * 512]

    st = {
        "kev_rel": 0, "qev_rel": [0, 0],
        "psL_rel": ("ss", 0), "srow_war": 0, "psb_war": 0, "bce_war": ("lyn", 0),
        "vv_rec": 0, "psRt_rel": 0, "rem_rel": 0, "g1_war": 0, "psGB_rel": 0,
        "mx_rel": [0, 0], "locrel_vv": [0, 0], "locrel_pe": [0, 0],
    }
    exp_rel = {0: [("ss", 0)], 1: [("ss", 0)]}
    expb_rel = {0: 0, 1: 0}
    avrel = {0: ("locn", 0), 1: ("locn", 0)}
    pend = []

    def mk_epi1(lv_all, par):
        def f():
            b.wait(v, "lv", lv_all)
            b.wait(v, "pe", st["srow_war"])
            b.vstep(v.tensor_scalar(srow1[0:1, 0:512], psAV4[par][64:65, :], 1e-30, None, ALU.max))
            b.vdep()
            b.vstep(v.reciprocal(srow1[0:1, 512:1024], srow1[0:1, 0:512]))
            st["vv_rec"] = b.n("vv")
        return f

    def mk_epi2(hp, hh, par):
        def f():
            b.wait(t, "vv", st["vv_rec"])
            b.wait(t, "bce1", st["psb_war"])
            b.wait(t, "vv", st["psGB_rel"])
            mmb = t.matmul(psB1[0:64, :], P.ones_row_f[0:1, 0:64], srow1[0:1, 512:1024],
                           start=True, stop=True)
            st["srow_war"] = peb = b.inc("pe", mmb)
            b.wait(s, "pe", peb)
            b.wait(s, st["bce_war"][0], st["bce_war"][1])
            st["psb_war"] = b.inc("bce1", s.copy(bc1, psB1[0:64, :]))
            b.wait(v, "bce1", st["psb_war"])
            b.wait(v, "vv", st["locrel_vv"][hp % 2])
            b.wait(v, "pe", st["locrel_pe"][hp % 2])
            yn = v.tensor_tensor(loc1[hp % 2][64 * hh:64 * hh + 64, :],
                                 psAV4[par][0:64, :], bc1, ALU.mult)
            cnt = b.inc("locn", yn)
            st["bce_war"] = ("locn", cnt)
            avrel[par] = ("locn", cnt)
        return f

    def mk_stage3(hp, sl):
        def f():
            mp_ = hp % 2
            lcb = loc1[hp % 2]
            b.wait(t, "remev", st["psRt_rel"])
            mm = None
            for lp in range(2):
                mm = t.matmul(psX, routw[:, lp * 1024 + hp * 128: lp * 1024 + (hp + 1) * 128],
                              lat1[:, lp * 512:(lp + 1) * 512], start=(lp == 0), stop=(lp == 1))
            pe_cnt = b.inc("pe", mm)
            b.wait(s, "pe", pe_cnt)
            b.wait(s, "vv", st["rem_rel"])
            remev = b.inc("remev", s.activation(rem1, psX, AF.Identity, bias=routb[:, hp:hp + 1]))
            st["psRt_rel"] = remev
            locn_all = b.n("locn")
            b.wait(t, "locn", locn_all)
            b.wait(t, "remev", remev)
            b.wait(t, "bce1", st["psb_war"])
            t.matmul(psB1[0:1, :], glb[:, 0:1], lcb, start=True, stop=False)
            t.matmul(psB1[0:1, :], grb[:, 0:1], rem1, start=False, stop=True)
            t.matmul(psB1[64:65, :], glb[:, 1:2], lcb, start=True, stop=False)
            mm = t.matmul(psB1[64:65, :], grb[:, 1:2], rem1, start=False, stop=True)
            peg = b.inc("pe", mm)
            b.wait(s, "pe", peg)
            b.wait(s, "pe", st["g1_war"])
            s.activation(g1[0:1, :], psB1[0:1, :], AF.Sigmoid)
            sg = s.activation(g1[64:65, :], psB1[64:65, :], AF.Sigmoid)
            sg_cnt = b.inc("ss", sg)
            b.wait(t, "ss", sg_cnt)
            b.wait(t, avrel[1][0], avrel[1][1])
            t.matmul(psAV4[1][0:64, :], P.ones_row_f[0:1, 0:64], g1[0:1, :], start=True, stop=True)
            mm2 = t.matmul(psB1[0:64, :], P.ones_row_f[64:65, 0:64], g1[64:65, :], start=True, stop=True)
            peb3 = b.inc("pe", mm2)
            b.wait(v, "pe", peb3)
            b.wait(v, "remev", remev)
            b.wait(v, "locn", locn_all)
            b.vdep()
            b.vstep(v.tensor_sub(d1f, lcb, rem1))
            b.vdep()
            b.vstep(v.tensor_tensor(d1f[0:64, :], d1f[0:64, :], psAV4[1][0:64, :], ALU.mult))
            b.vstep(v.tensor_tensor(d1f[64:128, :], d1f[64:128, :], psB1[0:64, :], ALU.mult))
            b.wait(v, f"dmx{mp_}", st["mx_rel"][mp_])
            b.vdep()
            b.vstep(v.tensor_tensor(mxo[mp_], d1f, rem1, ALU.add))
            b.wait(y, "vv", b.n("vv"))
            b.inc(f"dmx{mp_}", y.dma_start(out=P.mx_dram[hp, :, sl * 512:(sl + 1) * 512], in_=mxo[mp_]), 16)
            st["mx_rel"][mp_] = b.n(f"dmx{mp_}")
            vvn = b.n("vv")
            st["rem_rel"] = vvn
            st["psGB_rel"] = vvn
            st["g1_war"] = peb3
            st["psb_war"] = max(st["psb_war"], 0)
            avrel[1] = ("vv", vvn)
            st["locrel_vv"][hp % 2] = vvn
            st["locrel_pe"][hp % 2] = peg
        return f

    for sl in range(4):
        for hp in range(8):
            b.wait(t, "kev", st["kev_rel"])
            if hp == 0:
                b.wait(t, "remev", st["psRt_rel"])
                for w_ in exp_rel[0]:
                    b.wait(t, w_[0], w_[1])
            mm = None
            for c in range(8):
                mm = t.matmul(psPa, wkv1[:, c * 2048 + hp * 128: c * 2048 + (hp + 1) * 128],
                              halo_cols(c, sl), start=(c == 0), stop=(c == 7))
            pea = b.inc("pe", mm)
            for c in range(8):
                mm = t.matmul(psPb, wkv1[:, c * 2048 + hp * 128: c * 2048 + (hp + 1) * 128],
                              xn1T[:, c * 4096 + sl * 512: c * 4096 + (sl + 1) * 512],
                              start=(c == 0), stop=(c == 7))
            peb = b.inc("pe", mm)
            b.wait(s, "pe", peb)
            s.activation(K1[:, hp * 640: hp * 640 + 128], psPa, AF.Identity, bias=bq1[:, 8 + hp:9 + hp])
            st["kev_rel"] = b.inc("kev", s.activation(
                K1[:, hp * 640 + 128: hp * 640 + 640], psPb, AF.Identity, bias=bq1[:, 8 + hp:9 + hp]))
        b.wait(t, "kev", b.n("kev"))
        for kt in range(5):
            for vh in range(2):
                b.wait(t, "vev", b.n("vev"))
                for c in range(8):
                    src = halo_cols(c, sl) if kt == 0 else \
                        xn1T[:, c * 4096 + sl * 512 + (kt - 1) * 128: c * 4096 + sl * 512 + kt * 128]
                    t.matmul(psPb, src,
                             wkv1[:, c * 2048 + 1024 + vh * 512: c * 2048 + 1024 + (vh + 1) * 512],
                             start=(c == 0), stop=False)
                mm = t.matmul(psPb, P.ones_row, bv1[0:1, vh * 512:(vh + 1) * 512], start=False, stop=True)
                pe_cnt = b.inc("pe", mm)
                b.wait(v, "pe", pe_cnt)
                src = psPb.rearrange("p (h d) -> p h d", d=64)
                for gsub in range(4):
                    hg = vh * 4 + gsub
                    dst = V1[:, kt * 1040 + hg * 130: kt * 1040 + (hg + 1) * 130]
                    dst = dst.rearrange("p (h e) -> p h e", e=65)[:, :, 0:64]
                    b.inc("vev", v.tensor_copy(dst, src[:, gsub * 2:gsub * 2 + 2, :]))
        b.wait(t, "vev", b.n("vev"))
        for hp in range(8):
            qp = hp % 2
            b.wait(t, "qev", st["qev_rel"][qp])
            ps = psPb if qp else psX
            for c in range(8):
                mm = t.matmul(ps, wq1[:, c * 1024 + hp * 128: c * 1024 + (hp + 1) * 128],
                              xn1T[:, c * 4096 + sl * 512: c * 4096 + (sl + 1) * 512],
                              start=(c == 0), stop=(c == 7))
            pe_cnt = b.inc("pe", mm)
            b.wait(s, "pe", pe_cnt)
            st["qev_rel"][qp] = b.inc("qev", s.activation(
                Q1[:, hp * 512:(hp + 1) * 512], ps, AF.Identity, bias=bq1[:, hp:hp + 1]))
        b.wait(t, "qev", b.n("qev"))
        for rt in range(2):
            qp = rt % 2
            b.wait(t, "rqev", max(0, b.n("rqev") - 1))
            ps = psPb if qp else psX
            for c in range(8):
                mm = t.matmul(ps, rqw[:, c * 256 + rt * 128: c * 256 + (rt + 1) * 128],
                              xn1T[:, c * 4096 + sl * 512: c * 4096 + (sl + 1) * 512],
                              start=(c == 0), stop=(c == 7))
            pe_cnt = b.inc("pe", mm)
            b.wait(s, "pe", pe_cnt)
            b.inc("rqev", s.activation(rq1[:, rt * 512:(rt + 1) * 512], ps,
                                       AF.Identity, bias=rqb[:, rt:rt + 1]))
        b.wait(t, "rqev", b.n("rqev"))
        for lh in range(4):
            lp, lr = lh // 2, 64 * (lh % 2)
            b.wait(t, st["psL_rel"][0], st["psL_rel"][1])
            if lh == 0:
                for w_ in exp_rel[1]:
                    b.wait(t, w_[0], w_[1])
            mm = t.matmul(psL, rk_sb[lr:lr + 64, lp * 64:(lp + 1) * 64],
                          rq1[lr:lr + 64, lp * 512:(lp + 1) * 512], start=True, stop=True)
            pe_cnt = b.inc("pe", mm)
            b.wait(s, "pe", pe_cnt)
            b.wait(s, "lmk", b.n("lmk"))
            b.sstep(s.activation(rexp, psL, AF.Exp, scale=0.125))
            st["psL_rel"] = ("ss", b.n("ss"))
            b.wait(v, "ss", b.n("ss"))
            b.wait(v, "lav", b.n("lav"))
            lmk = b.inc("lmk", v.tensor_tensor(rexp, rexp, mlat[:, sl * 512:(sl + 1) * 512], ALU.mult))
            b.wait(t, "lmk", lmk)
            b.wait(t, "lyn", b.n("lyn"))
            mm = t.matmul(psLAV, rv_sb[:, lh * 65: lh * 65 + 65], rexp, start=True, stop=True)
            b.inc("lav", mm)
            b.wait(v, "lav", b.n("lav"))
            b.wait(v, "pe", st["srow_war"])
            b.vstep(v.tensor_scalar(srow1[0:1, 0:512], psLAV[64:65, :], 1e-30, None, ALU.max))
            b.vdep()
            b.vstep(v.reciprocal(srow1[0:1, 512:1024], srow1[0:1, 0:512]))
            b.wait(t, "vv", b.n("vv"))
            b.wait(t, "bce1", st["psb_war"])
            b.wait(t, "vv", st["psGB_rel"])
            mmb = t.matmul(psB1[0:64, :], P.ones_row_f[0:1, 0:64], srow1[0:1, 512:1024],
                           start=True, stop=True)
            st["srow_war"] = peb = b.inc("pe", mmb)
            b.wait(s, "pe", peb)
            b.wait(s, st["bce_war"][0], st["bce_war"][1])
            st["psb_war"] = b.inc("bce1", s.copy(bc1, psB1[0:64, :]))
            b.wait(v, "bce1", b.n("bce1"))
            yn = v.tensor_tensor(lat1[lr:lr + 64, lp * 512:(lp + 1) * 512], psLAV[0:64, :], bc1, ALU.mult)
            st["bce_war"] = ("lyn", b.inc("lyn", yn))
        b.wait(t, "lyn", b.n("lyn"))
        exp_rel[0] = [("rqev", b.n("rqev"))]
        exp_rel[1] = [("lyn", b.n("lyn"))]
        for hp in range(8):
            for hh in range(2):
                ci = hp * 2 + hh
                sp = hh
                for w_ in exp_rel[sp]:
                    b.wait(t, w_[0], w_[1])
                psc = psSc[sp]
                mm = None
                for qt in range(4):
                    for u in (0, 1):
                        mm = t.matmul(
                            psc[:, qt * 256 + u * 128: qt * 256 + (u + 1) * 128],
                            K1[64 * hh:64 * hh + 64, hp * 640 + (qt + u) * 128: hp * 640 + (qt + u + 1) * 128],
                            Q1[64 * hh:64 * hh + 64, hp * 512 + qt * 128: hp * 512 + (qt + 1) * 128],
                            start=True, stop=True)
                pe_cnt = b.inc("pe", mm)
                b.wait(s, "pe", pe_cnt)
                b.wait(s, "lv", expb_rel[sp])
                e_in = s.activation(exp4[sp], psc, AF.Exp, scale=0.125)
                ss_cnt = b.inc("ss", e_in)
                exp_rel[sp] = [("ss", ss_cnt)]
                b.wait(v, "ss", ss_cnt)
                b.wait(v, "lv", expb_rel[sp])
                mk = b.inc("lmkk", v.tensor_tensor(exp4[sp], exp4[sp],
                                                   mloc4[:, sl * 1024:(sl + 1) * 1024], ALU.mult))
                exp_rel[sp] = [("lmkk", mk)]
                while pend and pend[0][0] <= ci:
                    pend.pop(0)[1]()
                b.wait(t, "lmkk", mk)
                b.wait(t, avrel[sp][0], avrel[sp][1])
                mm = None
                for qt in range(4):
                    t.matmul(psAV4[sp][:, qt * 128:(qt + 1) * 128],
                             V1[:, qt * 1040 + hp * 130 + hh * 65: qt * 1040 + hp * 130 + hh * 65 + 65],
                             exp4[sp][:, qt * 256: qt * 256 + 128], start=True, stop=False)
                    mm = t.matmul(psAV4[sp][:, qt * 128:(qt + 1) * 128],
                                  V1[:, (qt + 1) * 1040 + hp * 130 + hh * 65: (qt + 1) * 1040 + hp * 130 + hh * 65 + 65],
                                  exp4[sp][:, qt * 256 + 128: qt * 256 + 256], start=False, stop=True)
                expb_rel[sp] = b.inc("lv", mm)
                pend.append((ci + 1, mk_epi1(b.n("lv"), sp)))
                pend.append((ci + 2, mk_epi2(hp, hh, sp)))
            pend.append((hp * 2 + 3, mk_stage3(hp, sl)))
        while pend:
            pend.pop(0)[1]()
    b.barrier()
    es.close()



def emit_p5b(P):
    nc, b, dr = P.nc, P.b, P.dr
    t, v, s, g, y = b.t, b.v, b.s, b.g, b.y
    es_op = ExitStack()
    wout = es_op.enter_context(nc.sbuf_tensor("wout", [128, 8 * 1024], BF16)).ap()
    bo_row = es_op.enter_context(nc.sbuf_tensor("bo_row", [1, 1024], BF16)).ap()
    xsA = [es_op.enter_context(nc.sbuf_tensor(f"xsA{i}", [128, 1024], F32)).ap() for i in range(2)]
    xsB = [es_op.enter_context(nc.sbuf_tensor(f"xsB{i}", [128, 1024], F32)).ap() for i in range(2)]
    mixbuf = [es_op.enter_context(nc.sbuf_tensor(f"mixbuf{i}", [128, 8 * 128], BF16)).ap() for i in range(2)]
    psO5 = [es_op.enter_context(nc.psum_tensor(f"psO5_{i}", [128, 512], F32)).ap() for i in range(2)]
    dma3(b, g, wout, dr["l1_wout"], 1024, "dop")
    b.inc("dop", g.dma_start(out=bo_row, in_=dr["l1_bout_row"]), 16)
    b.wait(t, "dop")
    psO_rel = [0, 0]
    mixbuf_rel = [0, 0]
    oad_rel = [0, 0]
    st2_rel = {0: ("dx2", 0), 1: ("dout", 0)}
    dmxl_cnt = {0: 0, 1: 0}

    def issue_mx(T):
        mbp = T % 2
        sl, tt = T // 4, T % 4
        b.wait(y, "pe", mixbuf_rel[mbp])
        b.inc(f"dmxl{mbp}", y.dma_start(
            out=mixbuf[mbp].rearrange("p (c e) -> p c e", e=128),
            in_=P.mx_dram[:, :, sl * 512 + tt * 128: sl * 512 + (tt + 1) * 128].rearrange("c p e -> p c e")), 16)
        dmxl_cnt[mbp] = b.n(f"dmxl{mbp}")

    issue_mx(0)
    for T in range(16):
        sl, tt = T // 4, T % 4
        offA = 128 * T
        offB = (2048 + 128 * T) % 4096
        mbp = T % 2
        if T + 1 < 16:
            issue_mx(T + 1)
        b.wait(y, "oadd1", oad_rel[mbp])
        b.wait(y, st2_rel[mbp][0], st2_rel[mbp][1])
        xsem = "dxr" if mbp == 0 else "dxn"
        b.inc(xsem, y.dma_start(out=xsA[mbp], in_=P.x1_dram[offA:offA + 128, :]), 16)
        b.inc(xsem, y.dma_start(out=xsB[mbp], in_=P.x1_dram[offB:offB + 128, :]), 16)
        dx_t = b.n(xsem)
        b.wait(t, f"dmxl{mbp}", dmxl_cnt[mbp])
        b.wait(v, xsem, dx_t)
        b.vdep()
        b.vstep(v.tensor_scalar(xsA[mbp], xsA[mbp], P.hsel[:, 0:1], None, ALU.mult))
        b.vdep()
        b.vstep(v.scalar_tensor_tensor(xsA[mbp], xsB[mbp], P.hsel[:, 1:2], xsA[mbp], ALU.mult, ALU.add))
        sel_vv = b.n("vv")
        for dh in range(2):
            op = dh
            b.wait(t, "oadd1", psO_rel[op])
            ps = psO5[op]
            for c in range(8):
                t.matmul(ps, mixbuf[mbp][:, c * 128:(c + 1) * 128],
                         wout[:, c * 1024 + dh * 512: c * 1024 + (dh + 1) * 512],
                         start=(c == 0), stop=False)
            mm = t.matmul(ps, P.ones_row, bo_row[0:1, dh * 512:(dh + 1) * 512],
                          start=False, stop=True)
            pe_cnt = b.inc("pe", mm)
            b.wait(v, "pe", pe_cnt)
            b.vdep(sel_vv)
            ad = v.tensor_tensor(xsA[mbp][:, dh * 512:(dh + 1) * 512],
                                 xsA[mbp][:, dh * 512:(dh + 1) * 512], ps, ALU.add)
            psO_rel[op] = b.inc("oadd1", ad)
        mixbuf_rel[mbp] = b.n("pe")
        oad_rel[mbp] = b.n("oadd1")
        b.wait(y, "oadd1", b.n("oadd1"))
        ssem = "dx2" if mbp == 0 else "dout"
        b.inc(ssem, y.dma_start(out=P.x2p_dram[T * 128:(T + 1) * 128, :], in_=xsA[mbp]), 16)
        st2_rel[mbp] = (ssem, b.n(ssem))
    b.barrier()
    es_op.close()



def emit_p6(P, xn1T, out):
    nc, b, dr = P.nc, P.b, P.dr
    t, v, s, g, y = b.t, b.v, b.s, b.g, b.y
    es = ExitStack()
    x_res = es.enter_context(nc.sbuf_tensor("x_res", [128, 16 * 1024], F32)).ap()
    w13 = [es.enter_context(nc.sbuf_tensor(f"w13a_{i}", [128, 8 * 1024], BF16)).ap() for i in range(2)]
    w2 = [es.enter_context(nc.sbuf_tensor(f"w2a_{i}", [128, 4 * 1024], BF16)).ap() for i in range(2)]
    act_sb = es.enter_context(nc.sbuf_tensor("act1_sb", [128, 4 * 512], BF16)).ap()
    s_s = es.enter_context(nc.sbuf_tensor("s1_s", [128, 512], F32)).ap()
    psA = es.enter_context(nc.psum_tensor("psA1", [128, 512], F32)).ap()
    psB2 = es.enter_context(nc.psum_tensor("psB21", [128, 512], F32)).ap()
    psM = [es.enter_context(nc.psum_tensor(f"psM1{i}", [128, 512], F32)).ap() for i in range(4)]

    for T in range(16):
        b.inc("dx2l", y.dma_start(out=x_res[:, T * 1024:(T + 1) * 1024],
                                  in_=P.x2p_dram[T * 128:(T + 1) * 128, :]), 16)
    b.wait(v, "dx2l", b.n("dx2l"))

    NEXP = 17

    def wsrc(e):
        if e < 16:
            return dr["l1_w13"][e], dr["l1_w2"][e]
        return dr["l1_w13s"], dr["l1_w2s"]
    w_rel = [0, 0]
    dma3(b, g, w13[0], wsrc(0)[0], 1024, "dm1_0")
    dma3(b, g, w2[0], wsrc(0)[1], 1024, "dm2_0")
    psA_rel = 0
    psB2_rel = 0
    psM_rel = [0, 0, 0, 0]
    act_war = 0
    ss_war = 0
    macc_reg = {}
    for e in range(NEXP):
        ep = e % 2
        if e + 1 < NEXP:
            b.wait(g, "pe", w_rel[1 - ep])
            dma3(b, g, w13[1 - ep], wsrc(e + 1)[0], 1024, f"dm1_{1 - ep}")
            dma3(b, g, w2[1 - ep], wsrc(e + 1)[1], 1024, f"dm2_{1 - ep}")
        b.wait(t, f"dm1_{ep}", b.n(f"dm1_{ep}"))
        b.wait(t, f"dm2_{ep}", b.n(f"dm2_{ep}"))
        for tc in range(4):
            for fp in range(4):
                b.wait(t, "sil1", psA_rel)
                for c in range(8):
                    mm = t.matmul(psA, w13[ep][:, c * 1024 + fp * 128: c * 1024 + (fp + 1) * 128],
                                  xn1T[:, c * 4096 + tc * 512: c * 4096 + (tc + 1) * 512],
                                  start=(c == 0), stop=(c == 7))
                peA = b.inc("pe", mm)
                b.wait(t, "mact1", psB2_rel)
                for c in range(8):
                    mm = t.matmul(psB2, w13[ep][:, c * 1024 + 512 + fp * 128: c * 1024 + 512 + (fp + 1) * 128],
                                  xn1T[:, c * 4096 + tc * 512: c * 4096 + (tc + 1) * 512],
                                  start=(c == 0), stop=(c == 7))
                peB = b.inc("pe", mm)
                w_rel[ep] = peB
                b.wait(s, "pe", peA)
                b.wait(s, "mact1", ss_war)
                psA_rel = b.inc("sil1", s.activation(s_s, psA, AF.Silu))
                b.wait(v, "pe", peB)
                b.wait(v, "sil1", psA_rel)
                b.wait(v, "pe", act_war)
                psB2_rel = ss_war = b.inc("mact1", v.tensor_tensor(
                    act_sb[:, fp * 512:(fp + 1) * 512], psB2, s_s, ALU.mult))
            b.wait(t, "mact1", b.n("mact1"))
            for tt in range(4):
                T = tc * 4 + tt
                for dh in range(2):
                    mp = (tt * 2 + dh) % 4
                    b.wait(t, "macc1", psM_rel[mp])
                    for ic in range(4):
                        mm = t.matmul(psM[mp], act_sb[:, ic * 512 + tt * 128: ic * 512 + (tt + 1) * 128],
                                      w2[ep][:, ic * 1024 + dh * 512: ic * 1024 + (dh + 1) * 512],
                                      start=(ic == 0), stop=(ic == 3))
                    peM = b.inc("pe", mm)
                    b.wait(v, "pe", peM)
                    b.wait(v, "macc1", macc_reg.get((T, dh), 0))
                    scl = P.comb[:, T * 16 + e: T * 16 + e + 1] if e < 16 else 1.0
                    psM_rel[mp] = macc_reg[(T, dh)] = b.inc("macc1", v.scalar_tensor_tensor(
                        x_res[:, T * 1024 + dh * 512: T * 1024 + (dh + 1) * 512],
                        psM[mp], scl,
                        x_res[:, T * 1024 + dh * 512: T * 1024 + (dh + 1) * 512],
                        ALU.mult, ALU.add))
            act_war = b.n("pe")
            w_rel[ep] = b.n("pe")
    b.barrier()
    for T in range(16):
        b.inc("dout", y.dma_start(out=out[T * 128:(T + 1) * 128, :],
                                  in_=x_res[:, T * 1024:(T + 1) * 1024]), 16)
    b.wait(y, "dout", b.n("dout"))
    es.close()



def build_program(stage="full"):
    nc = bass.Bass("TRN2", target_bir_lowering=False)
    b = Bld(nc)
    dr = {}
    for name, shape in ALL_INPUTS:
        dr[name] = nc.dram_tensor(name, list(shape), F32, kind="ExternalInput").ap()
    P = Prog(nc, b, dr)
    P.out = nc.dram_tensor("out", [2048, 1024], F32, kind="ExternalOutput").ap()
    if stage in ("l0x", "l0m"):
        P.dbg = nc.dram_tensor("dbg", [4096, 1024], F32, kind="ExternalOutput").ap()
    P.x1_dram = nc.dram_tensor("x1_scratch", [4096, 1024], F32, kind="Internal").ap()
    P.v_dram = nc.dram_tensor("v_scratch", [8, 128, 32, 130], BF16, kind="Internal").ap()
    P.q_dram = nc.dram_tensor("q_scratch", [8, 128, 8, 512], BF16, kind="Internal").ap()
    P.xnt_dram = nc.dram_tensor("xnt_scratch", [8, 128, 8, 512], BF16, kind="Internal").ap()
    P.x2p_dram = nc.dram_tensor("x2p_scratch", [2048, 1024], F32, kind="Internal").ap()
    P.mx_dram = nc.dram_tensor("mx_scratch", [8, 128, 2048], BF16, kind="Internal").ap()

    es_k = ExitStack()
    K_sb = es_k.enter_context(nc.sbuf_tensor("K_sb", [128, 8 * 4096], BF16)).ap()
    emit_p1(P, K_sb)
    emit_p2(P, K_sb)
    es_k.close()

    def dbg_dump():
        for T in range(32):
            b.inc("dout", b.y.dma_start(out=P.dbg[T * 128:(T + 1) * 128, :],
                                        in_=P.x1_dram[T * 128:(T + 1) * 128, :]), 16)
        for T in range(16):
            b.inc("dout", b.y.dma_start(out=P.out[T * 128:(T + 1) * 128, :],
                                        in_=P.x1_dram[T * 128:(T + 1) * 128, :]), 16)
        b.wait(b.y, "dout", b.n("dout"))

    if stage == "l0x":
        dbg_dump()
        return nc
    emit_p3(P)
    if stage == "l0m":
        dbg_dump()
        return nc

    es1 = ExitStack()
    xn1T = es1.enter_context(nc.sbuf_tensor("xn1T", [128, 8 * 4096], BF16)).ap()
    pooled = es1.enter_context(nc.sbuf_tensor("pooled", [128, 8 * 64], BF16)).ap()
    rk_sb = es1.enter_context(nc.sbuf_tensor("rk_sb", [128, 2 * 64], BF16)).ap()
    rv_sb = es1.enter_context(nc.sbuf_tensor("rv_sb", [64, 4 * 65], BF16)).ap()
    emit_p4(P, xn1T)
    emit_p4b(P, xn1T, pooled, rk_sb, rv_sb)
    emit_p5(P, xn1T, pooled, rk_sb, rv_sb)
    emit_p5b(P)
    emit_p6(P, xn1T, P.out)
    es1.close()
    return nc


def kernel(**inputs):
    inputs.get("L1_ffn_input_ids")
    nc = build_program("full")
    im = [host_inputs(inputs, core) for core in range(8)]
    res = run_bass_kernel_spmd(nc, im, core_ids=list(range(8)))
    return reassemble([res.results[c]["out"] for c in range(8)])
